# version 04
# baseline (speedup 1.0000x reference)
import contextlib
import numpy as np
import concourse.bass as bass
import concourse.mybir as mybir
from concourse.bass_utils import run_bass_kernel_spmd

F32 = mybir.dt.float32
BF16 = mybir.dt.bfloat16
AF = mybir.ActivationFunctionType
ALU = mybir.AluOpType
AX = mybir.AxisListType.X

D_MODEL = 1024
D_STATE = 128
HEADDIM = 64
D_INNER = 2048
NHEADS = 32
CONV_DIM = 2304
D_IN_PROJ = 4384
EPS = 1e-5
Q = 128
SL = 512
NEXP = 8
M_FF = 2048

OC_X0 = 16
OC_B = 32
OC_C = 33
OC_DT = 34


def build_nc(NT_MAIN=2048, NT_PREF=2048):
    assert NT_MAIN % SL == 0 and NT_PREF % SL == 0 and NT_PREF >= SL
    n_sl_pref = NT_PREF // SL
    n_sl_main = NT_MAIN // SL
    nch_main = NT_MAIN // Q
    nch_pref = NT_PREF // Q
    NTOT = NT_PREF + NT_MAIN
    n_tc = NT_MAIN // 128
    n_tt = NT_MAIN // 512

    nc = bass.Bass("TRN2", target_bir_lowering=False)

    d_xm = nc.dram_tensor("x_main", [NT_MAIN, D_MODEL], F32, kind="ExternalInput").ap()
    d_xp = nc.dram_tensor("x_pref", [NT_PREF, D_MODEL], F32, kind="ExternalInput").ap()
    d_flag = nc.dram_tensor("pref_flag", [128, 1], F32, kind="ExternalInput").ap()
    d_winT = nc.dram_tensor("w_inT", [D_MODEL, D_IN_PROJ], F32, kind="ExternalInput").ap()
    d_woutT = nc.dram_tensor("w_outT", [D_INNER, D_MODEL], F32, kind="ExternalInput").ap()
    d_gwT = nc.dram_tensor("gwT", [D_MODEL, NEXP], F32, kind="ExternalInput").ap()
    d_convw = nc.dram_tensor("convw", [128, 18 * 4], F32, kind="ExternalInput").ap()
    d_convb = nc.dram_tensor("convb", [128, 18], F32, kind="ExternalInput").ap()
    d_normw = nc.dram_tensor("normw", [128, 16], F32, kind="ExternalInput").ap()
    d_dpar = nc.dram_tensor("dpar", [128, NHEADS], F32, kind="ExternalInput").ap()
    d_dtb = nc.dram_tensor("dtb", [NHEADS, 1], F32, kind="ExternalInput").ap()
    d_alog = nc.dram_tensor("alog", [NHEADS, 1], F32, kind="ExternalInput").ap()
    d_beta = nc.dram_tensor("beta_t", [128, NEXP], F32, kind="ExternalInput").ap()
    d_wg = nc.dram_tensor("w_gate", [NEXP + 1, D_MODEL, M_FF], F32, kind="ExternalInput").ap()
    d_wu = nc.dram_tensor("w_up", [NEXP + 1, D_MODEL, M_FF], F32, kind="ExternalInput").ap()
    d_wd = nc.dram_tensor("w_down", [NEXP + 1, M_FF, D_MODEL], F32, kind="ExternalInput").ap()
    d_out = nc.dram_tensor("out", [NT_MAIN, D_MODEL], F32, kind="ExternalOutput").ap()
    d_y4 = nc.dram_tensor("y4scr", [nch_main * 128, D_INNER], BF16, kind="Internal").ap()

    C = {}
    H = {}

    class Sem:
        def __init__(self, handle, name):
            self.h, self.name = handle, name
            C[name] = 0

        def inc(self, instr, n=1):
            instr.then_inc(self.h, n)
            C[self.name] += n

    tags = {}

    ES = contextlib.ExitStack()
    ident_f = ES.enter_context(nc.sbuf_tensor("ident_f", [128, 128], F32))
    ident_b = ES.enter_context(nc.sbuf_tensor("ident_b", [128, 128], BF16))
    ones_row = ES.enter_context(nc.sbuf_tensor("ones_row", [1, 128], F32))
    eps_col = ES.enter_context(nc.sbuf_tensor("eps_col", [128, 1], F32))
    zero_row = ES.enter_context(nc.sbuf_tensor("zero_row", [1, 512], BF16))
    ones_row_b = ES.enter_context(nc.sbuf_tensor("ones_row_b", [1, 128], BF16))
    ones_col_b = ES.enter_context(nc.sbuf_tensor("ones_col_b", [128, 1], BF16))
    ones32 = ES.enter_context(nc.sbuf_tensor("ones32", [32, SL], F32))
    mask01 = ES.enter_context(nc.sbuf_tensor("mask01", [128, 128], F32))
    maskneg8 = ES.enter_context(nc.sbuf_tensor("maskneg8", [128, 8 * 128], BF16))
    convw_sb = ES.enter_context(nc.sbuf_tensor("convw_sb", [128, 18 * 4], F32))
    convb_sb = ES.enter_context(nc.sbuf_tensor("convb_sb", [128, 18], F32))
    normw_sb = ES.enter_context(nc.sbuf_tensor("normw_sb", [128, 16], F32))
    dpar_sb = ES.enter_context(nc.sbuf_tensor("dpar_sb", [128, NHEADS], F32))
    dtb_sb = ES.enter_context(nc.sbuf_tensor("dtb_sb", [NHEADS, 1], F32))
    alog_sb = ES.enter_context(nc.sbuf_tensor("alog_sb", [NHEADS, 1], F32))
    acol_sb = ES.enter_context(nc.sbuf_tensor("acol_sb", [NHEADS, 1], F32))
    beta_sb = ES.enter_context(nc.sbuf_tensor("beta_sb", [128, NEXP], F32))
    gwT_sb = ES.enter_context(nc.sbuf_tensor("gwT_sb", [128, 8 * NEXP], BF16))
    flag_sb = ES.enter_context(nc.sbuf_tensor("flag_sb", [128, 1], F32))
    xn_fm = ES.enter_context(nc.sbuf_tensor("xn_fm", [128, 8 * NT_MAIN], BF16))
    c_tm = ES.enter_context(nc.sbuf_tensor("c_tm", [128, 16 * NEXP], F32))
    ps_big = ES.enter_context(nc.psum_tensor("ps_big", [128, 2048], F32))
    ps_e = ES.enter_context(nc.psum_tensor("ps_e", [128, 1024], F32))
    ps_aux = ES.enter_context(nc.psum_tensor("ps_aux", [128, 512], F32))
    ps_nrm = ES.enter_context(nc.psum_tensor("ps_nrm", [128, 512], F32))
    _dS = ES.enter_context(nc.semaphore("dS"))
    _dG = ES.enter_context(nc.semaphore("dG"))
    _dC = ES.enter_context(nc.semaphore("dC"))
    _dX0 = ES.enter_context(nc.semaphore("dX0"))
    _dX1 = ES.enter_context(nc.semaphore("dX1"))
    _dY = ES.enter_context(nc.semaphore("dY"))
    _dF0 = ES.enter_context(nc.semaphore("dF0"))
    _dF1 = ES.enter_context(nc.semaphore("dF1"))
    _dO0 = ES.enter_context(nc.semaphore("dO0"))
    _dO1 = ES.enter_context(nc.semaphore("dO1"))
    _dWc = ES.enter_context(nc.semaphore("dWc"))
    _dI0 = ES.enter_context(nc.semaphore("dI0"))
    _dI1 = ES.enter_context(nc.semaphore("dI1"))
    _dM0 = ES.enter_context(nc.semaphore("dM0"))
    _dM1 = ES.enter_context(nc.semaphore("dM1"))
    _dD = ES.enter_context(nc.semaphore("dD"))
    _dWo = ES.enter_context(nc.semaphore("dWo"))
    _sT = ES.enter_context(nc.semaphore("sT"))
    _sV = ES.enter_context(nc.semaphore("sV"))
    _sS = ES.enter_context(nc.semaphore("sS"))
    _sP = ES.enter_context(nc.semaphore("sP"))
    block = ES.enter_context(nc.Block())
    with ES:
     if True:
        dS, dG = Sem(_dS, "dS"), Sem(_dG, "dG")
        dC, dY, dWc, dD, dWo = Sem(_dC, "dC"), Sem(_dY, "dY"), Sem(_dWc, "dWc"), Sem(_dD, "dD"), Sem(_dWo, "dWo")
        dX = [Sem(_dX0, "dX0"), Sem(_dX1, "dX1")]
        dF = [Sem(_dF0, "dF0"), Sem(_dF1, "dF1")]
        dO = [Sem(_dO0, "dO0"), Sem(_dO1, "dO1")]
        dI = [Sem(_dI0, "dI0"), Sem(_dI1, "dI1")]
        dM = [Sem(_dM0, "dM0"), Sem(_dM1, "dM1")]
        sT, sV, sS, sP = Sem(_sT, "sT"), Sem(_sV, "sV"), Sem(_sS, "sS"), Sem(_sP, "sP")
        SEMS = {s.name: s for s in [dS, dG, dC, dY, dWc, dD, dWo] + dX + dF + dO + dI + dM + [sT, sV, sS, sP]}

        def tag(name, sem):
            tags[name] = (sem.name, C[sem.name])

        def wait(eng, name):
            s, v = tags[name]
            if v > 0:
                eng.wait_ge(SEMS[s].h, v)

        @block.sync
        def _(sync):
            for dst, src in [(convw_sb, d_convw), (convb_sb, d_convb), (normw_sb, d_normw),
                             (dpar_sb, d_dpar), (dtb_sb, d_dtb), (alog_sb, d_alog),
                             (beta_sb, d_beta), (flag_sb, d_flag)]:
                inst = sync.dma_start(out=dst[:], in_=src[:])
                dC.inc(inst, 16)
        tag("p0_dS", dC)

        @block.gpsimd
        def _(gpsimd):
            inst = gpsimd.dma_start(
                out=gwT_sb[:].rearrange("p (dc e) -> p dc e", dc=8),
                in_=d_gwT.rearrange("(dc p) e -> p dc e", p=128))
            dWc.inc(inst, 16)
            gpsimd.memset(ident_f[:], 0.0)
            gpsimd.memset(ident_b[:], 0.0)
            gpsimd.memset(mask01[:], 0.0)
            gpsimd.memset(maskneg8[:], 0.0)
            gpsimd.drain()
            gpsimd.affine_select(out=ident_f[:], in_=ident_f[:], compare_op=ALU.not_equal,
                                 fill=1.0, base=0, pattern=[[-1, 128]], channel_multiplier=1)
            gpsimd.affine_select(out=ident_b[:], in_=ident_b[:], compare_op=ALU.not_equal,
                                 fill=1.0, base=0, pattern=[[-1, 128]], channel_multiplier=1)
            gpsimd.affine_select(out=mask01[:], in_=mask01[:], compare_op=ALU.is_gt,
                                 fill=1.0, base=0, pattern=[[-1, 128]], channel_multiplier=1)
            gpsimd.drain()
            for g8 in range(8):
                gpsimd.affine_select(out=maskneg8[:, g8 * 128:(g8 + 1) * 128],
                                     in_=maskneg8[:, g8 * 128:(g8 + 1) * 128],
                                     compare_op=ALU.is_ge, fill=-1e30, base=0,
                                     pattern=[[1, 128]], channel_multiplier=-1)
            gpsimd.drain()
            inst = gpsimd.engine_nop()
            sP.inc(inst)
        tag("p0_consts", sP)
        tag("p0_dG", dWc)

        @block.vector
        def _(vector):
            vector.memset(ones_row[:], 1.0)
            vector.memset(eps_col[:], EPS)
            vector.memset(zero_row[:], 0.0)
            vector.memset(ones_row_b[:], 1.0)
            vector.memset(ones_col_b[:], 1.0)
            vector.memset(ones32[:], 1.0)
            inst = vector.drain()
            sV.inc(inst)
        tag("p0_v", sV)

        @block.scalar
        def _(scalar):
            wait(scalar, "p0_dS")
            inst = scalar.activation(acol_sb[:], alog_sb[:], AF.Exp)
            sS.inc(inst)
            scalar.drain()
        tag("p0_s", sS)

        @block.vector
        def _(vector):
            wait(vector, "p0_s")
            inst = vector.tensor_scalar_mul(acol_sb[:], acol_sb[:], -1.0)
            sV.inc(inst)
            vector.drain()
        tag("p0_acol", sV)

        ES1 = contextlib.ExitStack()
        sel32 = ES1.enter_context(nc.sbuf_tensor("sel32", [32, 32 * 128], F32))
        S_sb = ES1.enter_context(nc.sbuf_tensor("S_sb", [128, D_INNER], BF16))
        S_t1 = ES1.enter_context(nc.sbuf_tensor("S_t1", [128, D_INNER], BF16))
        Pa = ES1.enter_context(nc.sbuf_tensor("Pa_sl", [32, SL + 1], F32))
        nPa = ES1.enter_context(nc.sbuf_tensor("nPa_sl", [32, SL + 1], F32))
        x_tile = ES1.enter_context(nc.sbuf_tensor("x_tile", [128, 2 * D_MODEL], F32))
        xn_tm = ES1.enter_context(nc.sbuf_tensor("xn_tm", [128, 2 * D_MODEL], F32))
        lnst = ES1.enter_context(nc.sbuf_tensor("lnst", [128, 16], F32))
        zs = ES1.enter_context(nc.sbuf_tensor("zs", [128, 16 * SL], BF16))
        xbc = ES1.enter_context(nc.sbuf_tensor("xbc", [128, 18 * (SL + 3)], BF16))
        xsilu = ES1.enter_context(nc.sbuf_tensor("xsilu", [128, 16 * SL], BF16))
        B_sl = ES1.enter_context(nc.sbuf_tensor("B_sl", [128, SL], BF16))
        C_sl = ES1.enter_context(nc.sbuf_tensor("C_sl", [128, SL], BF16))
        cva = ES1.enter_context(nc.sbuf_tensor("cva", [128, 2 * SL], BF16))
        cvb = ES1.enter_context(nc.sbuf_tensor("cvb", [128, 2 * SL], BF16))
        dts = ES1.enter_context(nc.sbuf_tensor("dts", [32, 4 * SL], F32))
        win_buf = ES1.enter_context(nc.sbuf_tensor("win_buf", [128, 2 * 1024], BF16))
        Xd = ES1.enter_context(nc.sbuf_tensor("Xd", [128, D_INNER], BF16))
        Xw = ES1.enter_context(nc.sbuf_tensor("Xw", [128, D_INNER], BF16))
        XDt = ES1.enter_context(nc.sbuf_tensor("XDt", [128, D_INNER], BF16))
        small_tm = ES1.enter_context(nc.sbuf_tensor("small_tm", [128, 128], F32))
        rr_hm = ES1.enter_context(nc.sbuf_tensor("rr_hm", [32, 128], F32))
        npp_rows = ES1.enter_context(nc.sbuf_tensor("npp_rows", [32, 128], F32))
        dec_sm = ES1.enter_context(nc.sbuf_tensor("dec_sm", [32, 8], F32))
        decrow = ES1.enter_context(nc.sbuf_tensor("decrow", [4, 32], F32))
        B_tm = ES1.enter_context(nc.sbuf_tensor("B_tm", [128, 128], BF16))
        Gt_m = ES1.enter_context(nc.sbuf_tensor("Gt_m", [128, 128], BF16))
        Eex = ES1.enter_context(nc.sbuf_tensor("Eex", [128, 2 * 1024], BF16))
        Mt = ES1.enter_context(nc.sbuf_tensor("Mt", [128, 2 * 1024], BF16))
        Mts = ES1.enter_context(nc.sbuf_tensor("Mts", [128, 2 * 1024], BF16))
        y2 = ES1.enter_context(nc.sbuf_tensor("y2", [128, D_INNER], BF16))
        y3 = ES1.enter_context(nc.sbuf_tensor("y3", [128, D_INNER], BF16))
        y4asm = ES1.enter_context(nc.sbuf_tensor("y4asm", [128, D_INNER], BF16))
        nrm_sm = ES1.enter_context(nc.sbuf_tensor("nrm_sm", [128, 8], F32))
        rrow = ES1.enter_context(nc.sbuf_tensor("rrow", [1, 128], F32))
        with ES1:
         if True:
            @block.gpsimd
            def _(gpsimd):
                wait(gpsimd, "p0_consts")
                gpsimd.memset(sel32[:], 0.0)
                gpsimd.drain()
                gpsimd.affine_select(out=sel32[:].rearrange("k (h m) -> k h m", h=32),
                                     in_=sel32[:].rearrange("k (h m) -> k h m", h=32),
                                     compare_op=ALU.not_equal, fill=1.0, base=0,
                                     pattern=[[-1, 32], [0, 128]], channel_multiplier=1)
                gpsimd.drain()
                inst = gpsimd.engine_nop()
                sP.inc(inst)
            tag("sel_done", sP)

            @block.vector
            def _(vector):
                for cc in range(18):
                    vector.memset(xbc[:, cc * (SL + 3):cc * (SL + 3) + 3], 0.0)
                vector.memset(S_sb[:], 0.0)
                vector.memset(Pa[:, 0:1], 0.0)
                vector.memset(nPa[:, 0:1], 0.0)
                inst = vector.drain()
                sV.inc(inst)
            tag("tails_s0", sV)

            def xbc_t(cc):
                return xbc[:, cc * (SL + 3):(cc + 1) * (SL + 3)]

            ipg = 0
            n_slices = n_sl_pref + n_sl_main
            NTT = SL // 128

            for sl_i in range(n_slices):
                is_pref = sl_i < n_sl_pref
                goff = sl_i * SL
                moff = goff - NT_PREF
                x_src = d_xp if is_pref else d_xm
                x_off = goff if is_pref else moff
                P_ = f"s{sl_i}_"
                prev_last = goff // Q - 1

                def xn_cols(dc, c0, cn, is_pref=is_pref, moff=moff):
                    if is_pref:
                        return zs[:, dc * SL + c0: dc * SL + c0 + cn]
                    return xn_fm[:, dc * NT_MAIN + moff + c0: dc * NT_MAIN + moff + c0 + cn]

                for tt in range(NTT):
                    par = tt % 2
                    xt = x_tile[:, par * D_MODEL:(par + 1) * D_MODEL]
                    xnt = xn_tm[:, par * D_MODEL:(par + 1) * D_MODEL]
                    ls = par * 8
                    gtt = sl_i * NTT + tt

                    @block.sync
                    def _(sync, tt=tt, xt=xt, x_src=x_src, x_off=x_off, P_=P_, gtt=gtt):
                        if gtt >= 2:
                            wait(sync, f"g{gtt - 2}_lnx")
                        inst = sync.dma_start(out=xt[:],
                                              in_=x_src[x_off + tt * 128: x_off + (tt + 1) * 128, :])
                        dX[gtt % 2].inc(inst, 16)
                    tag(f"g{gtt}_xld", dX[gtt % 2])

                    @block.vector
                    def _(vector, xt=xt, ls=ls, gtt=gtt):
                        wait(vector, f"g{gtt}_xld")
                        vector.reduce_sum(lnst[:, ls:ls + 1], xt[:], axis=AX)
                        vector.drain()
                        inst = vector.tensor_scalar_mul(lnst[:, ls + 1:ls + 2], lnst[:, ls:ls + 1], 1.0 / D_MODEL)
                        sV.inc(inst)
                        vector.drain()
                    tag(f"g{gtt}_mean", sV)

                    @block.scalar
                    def _(scalar, xt=xt, ls=ls, gtt=gtt):
                        wait(scalar, f"g{gtt}_xld")
                        if gtt >= 2:
                            wait(scalar, f"g{gtt - 2}_var")
                        inst = scalar.activation(xn_tm[:, (ls // 8) * D_MODEL:(ls // 8 + 1) * D_MODEL],
                                                 xt[:], AF.Square, accum_out=lnst[:, ls + 2:ls + 3])
                        sS.inc(inst)
                        scalar.drain()
                    tag(f"g{gtt}_sumsq", sS)

                    @block.vector
                    def _(vector, ls=ls, gtt=gtt):
                        wait(vector, f"g{gtt}_sumsq")
                        wait(vector, f"g{gtt}_mean")
                        vector.tensor_scalar(lnst[:, ls + 3:ls + 4], lnst[:, ls + 1:ls + 2],
                                             lnst[:, ls + 1:ls + 2], None, ALU.mult)
                        vector.drain()
                        inst = vector.scalar_tensor_tensor(lnst[:, ls + 4:ls + 5], lnst[:, ls + 2:ls + 3],
                                                           1.0 / D_MODEL, lnst[:, ls + 3:ls + 4],
                                                           ALU.mult, ALU.subtract)
                        sV.inc(inst)
                        vector.drain()
                    tag(f"g{gtt}_var", sV)

                    @block.scalar
                    def _(scalar, ls=ls, gtt=gtt):
                        wait(scalar, f"g{gtt}_var")
                        scalar.activation(lnst[:, ls + 5:ls + 6], lnst[:, ls + 4:ls + 5], AF.Ln, bias=eps_col[:])
                        scalar.drain()
                        inst = scalar.activation(lnst[:, ls + 6:ls + 7], lnst[:, ls + 5:ls + 6], AF.Exp, scale=-0.5)
                        sS.inc(inst)
                        scalar.drain()
                    tag(f"g{gtt}_sd", sS)

                    @block.vector
                    def _(vector, xt=xt, xnt=xnt, ls=ls, gtt=gtt):
                        wait(vector, f"g{gtt}_sd")
                        if gtt >= 2:
                            wait(vector, f"g{gtt - 2}_xntr")
                        inst = vector.tensor_scalar(xnt[:], xt[:], lnst[:, ls + 1:ls + 2],
                                                    lnst[:, ls + 6:ls + 7], ALU.subtract, ALU.mult)
                        sV.inc(inst)
                        vector.drain()
                    tag(f"g{gtt}_lnx", sV)

                    @block.tensor
                    def _(tensor, xnt=xnt, gtt=gtt):
                        wait(tensor, f"g{gtt}_lnx")
                        wait(tensor, "p0_consts")
                        if gtt >= 1:
                            wait(tensor, f"g{gtt - 1}_xnev")
                        for dc in range(8):
                            inst = tensor.transpose(ps_e[:, dc * 128:(dc + 1) * 128],
                                                    xnt[:, dc * 128:(dc + 1) * 128], ident_f[:])
                        sT.inc(inst)
                    tag(f"g{gtt}_xntr", sT)

                    @block.scalar
                    def _(scalar, tt=tt, gtt=gtt, is_pref=is_pref, moff=moff):
                        wait(scalar, f"g{gtt}_xntr")
                        if is_pref:
                            dst = zs[:].rearrange("p (dc t) -> p dc t", dc=16)[:, 0:8, tt * 128:(tt + 1) * 128]
                        else:
                            dst = xn_fm[:].rearrange("p (dc t) -> p dc t", dc=8)[:, :, moff + tt * 128: moff + (tt + 1) * 128]
                        inst = scalar.copy(dst, ps_e[:].rearrange("p (dc i) -> p dc i", dc=8))
                        sS.inc(inst)
                        scalar.drain()
                    tag(f"g{gtt}_xnev", sS)
                tag(f"{P_}xn_done", sS)

                if sl_i > 0:
                    @block.vector
                    def _(vector):
                        wait(vector, f"s{sl_i - 1}_convdone")
                        for cc in range(18):
                            vector.tensor_copy(xbc_t(cc)[:, 0:3], xbc_t(cc)[:, SL:SL + 3])
                        inst = vector.drain()
                        sV.inc(inst)
                    tag(f"{P_}tails", sV)
                else:
                    tags[f"{P_}tails"] = tags["tails_s0"]

                oc_list = list(range(OC_X0, OC_DT + 1)) if is_pref else list(range(35))
                for j_loc, oc in enumerate(oc_list):
                    par = ipg % 2
                    wb = win_buf[:, par * 1024:(par + 1) * 1024]
                    psd = ps_aux if par == 0 else ps_nrm
                    jg = ipg

                    @block.gpsimd
                    def _(gpsimd, oc=oc, wb=wb, jg=jg):
                        if jg >= 2:
                            wait(gpsimd, f"ip{jg - 2}_mm")
                        ncols = 128 if oc != OC_DT else 32
                        inst = gpsimd.dma_start(
                            out=wb[:].rearrange("p (ic o) -> p ic o", ic=8)[:, :, 0:ncols],
                            in_=d_winT.rearrange("(ic p) o -> p ic o", p=128)[:, :, oc * 128: oc * 128 + ncols])
                        dI[jg % 2].inc(inst, 16)
                    tag(f"ip{jg}_w", dI[jg % 2])

                    @block.tensor
                    def _(tensor, oc=oc, wb=wb, psd=psd, jg=jg, j_loc=j_loc, P_=P_,
                          prev_last=prev_last, sl_i=sl_i):
                        wait(tensor, f"ip{jg}_w")
                        wait(tensor, f"{P_}xn_done")
                        if jg >= 2:
                            wait(tensor, f"ip{jg - 2}_ev")
                        if j_loc < 2 and prev_last >= 0:
                            wait(tensor, f"c{prev_last}_sup")
                            if sl_i > n_sl_pref:
                                wait(tensor, f"c{prev_last}_psnrm_free")
                        ncols = 128 if oc != OC_DT else 32
                        for ic in range(8):
                            inst = tensor.matmul(psd[0:ncols, :],
                                                 wb[:].rearrange("p (ic o) -> p ic o", ic=8)[:, ic, 0:ncols],
                                                 xn_cols(ic, 0, SL), start=(ic == 0), stop=(ic == 7))
                        sT.inc(inst)
                    tag(f"ip{jg}_mm", sT)

                    if oc < OC_X0:
                        @block.scalar
                        def _(scalar, oc=oc, psd=psd, jg=jg):
                            wait(scalar, f"ip{jg}_mm")
                            inst = scalar.activation(zs[:, oc * SL:(oc + 1) * SL], psd[:], AF.Silu)
                            sS.inc(inst)
                            scalar.drain()
                        tag(f"ip{jg}_ev", sS)
                    elif oc == OC_DT:
                        @block.vector
                        def _(vector, psd=psd, jg=jg):
                            wait(vector, f"ip{jg}_mm")
                            inst = vector.tensor_copy(dts[:, 0:SL], psd[0:32, :])
                            sV.inc(inst)
                            vector.drain()
                        tag(f"ip{jg}_ev", sV)
                    else:
                        cc = oc - OC_X0

                        @block.scalar
                        def _(scalar, cc=cc, psd=psd, jg=jg, P_=P_):
                            wait(scalar, f"ip{jg}_mm")
                            wait(scalar, f"{P_}tails")
                            inst = scalar.copy(xbc_t(cc)[:, 3:3 + SL], psd[:])
                            sS.inc(inst)
                            scalar.drain()
                        tag(f"ip{jg}_ev", sS)
                    H[f"{P_}ip_{oc}"] = f"ip{jg}_ev"
                    ipg += 1

                cc_list = list(range(17)) if is_pref else list(range(18))
                for cc in cc_list:
                    vpar = cc % 2
                    cvx = cva if vpar == 0 else cvb

                    def conv_body(eng, sem, stt_ok, cc=cc, P_=P_, sl_i=sl_i, cvx=cvx, vpar=vpar):
                        wait(eng, H[f"{P_}ip_{OC_X0 + cc}"])
                        if cc > 1:
                            wait(eng, f"{P_}cvs{cc - 2}")
                        elif sl_i > 0:
                            lastcc = (16 - vpar) if sl_i <= n_sl_pref else (16 + vpar)
                            wait(eng, f"s{sl_i - 1}_cvs{lastcc}")
                        srcp = xbc_t(cc)
                        w = lambda k: convw_sb[:, cc * 4 + k: cc * 4 + k + 1]
                        if stt_ok:
                            eng.tensor_scalar(cvx[:, 0:SL], srcp[:, 0:SL], w(0), None, ALU.mult)
                            eng.drain()
                            eng.scalar_tensor_tensor(cvx[:, SL:2 * SL], srcp[:, 1:1 + SL], w(1), cvx[:, 0:SL], ALU.mult, ALU.add)
                            eng.drain()
                            eng.scalar_tensor_tensor(cvx[:, 0:SL], srcp[:, 2:2 + SL], w(2), cvx[:, SL:2 * SL], ALU.mult, ALU.add)
                            eng.drain()
                            inst = eng.scalar_tensor_tensor(cvx[:, SL:2 * SL], srcp[:, 3:3 + SL], w(3), cvx[:, 0:SL], ALU.mult, ALU.add)
                            sem.inc(inst)
                            eng.drain()
                        else:
                            eng.tensor_scalar(cvx[:, SL:2 * SL], srcp[:, 0:SL], w(0), None, ALU.mult)
                            eng.drain()
                            for k in (1, 2, 3):
                                eng.tensor_scalar(cvx[:, 0:SL], srcp[:, k:k + SL], w(k), None, ALU.mult)
                                eng.drain()
                                inst = eng.tensor_tensor(cvx[:, SL:2 * SL], cvx[:, SL:2 * SL], cvx[:, 0:SL], ALU.add)
                                eng.drain()
                            sem.inc(inst)
                            eng.drain()

                    if vpar == 0:
                        @block.vector
                        def _(vector, conv_body=conv_body):
                            conv_body(vector, sV, True)
                        tag(f"{P_}cv{cc}", sV)
                    else:
                        @block.gpsimd
                        def _(gpsimd, conv_body=conv_body):
                            conv_body(gpsimd, sP, False)
                        tag(f"{P_}cv{cc}", sP)

                    @block.scalar
                    def _(scalar, cc=cc, P_=P_, prev_last=prev_last, cvx=cvx):
                        wait(scalar, f"{P_}cv{cc}")
                        if prev_last >= 0:
                            wait(scalar, f"c{prev_last}_sup")
                        if cc < 16:
                            dst = xsilu[:, cc * SL:(cc + 1) * SL]
                        elif cc == 16:
                            dst = B_sl[:]
                        else:
                            dst = C_sl[:]
                        inst = scalar.activation(dst, cvx[:, SL:2 * SL], AF.Silu, bias=convb_sb[:, cc:cc + 1])
                        sS.inc(inst)
                        scalar.drain()
                    tag(f"{P_}cvs{cc}", sS)
                tag(f"{P_}convdone", sS)

                @block.vector
                def _(vector, P_=P_):
                    wait(vector, H[f"{P_}ip_{OC_DT}"])
                    wait(vector, "p0_acol")
                    inst = vector.tensor_scalar_add(dts[:, SL:2 * SL], dts[:, 0:SL], dtb_sb[:])
                    sV.inc(inst)
                    vector.drain()
                tag(f"{P_}dt0", sV)

                @block.scalar
                def _(scalar, P_=P_):
                    wait(scalar, f"{P_}dt0")
                    scalar.activation(dts[:, 2 * SL:3 * SL], dts[:, SL:2 * SL], AF.Abs)
                    scalar.drain()
                    inst = scalar.activation(dts[:, 2 * SL:3 * SL], dts[:, 2 * SL:3 * SL], AF.Exp, scale=-1.0)
                    sS.inc(inst)
                    scalar.drain()
                tag(f"{P_}dt1", sS)

                @block.vector
                def _(vector, P_=P_):
                    wait(vector, f"{P_}dt1")
                    inst = vector.tensor_scalar_add(dts[:, 2 * SL:3 * SL], dts[:, 2 * SL:3 * SL], 1.0)
                    sV.inc(inst)
                    vector.drain()
                tag(f"{P_}dt2", sV)

                @block.scalar
                def _(scalar, P_=P_):
                    wait(scalar, f"{P_}dt2")
                    inst = scalar.activation(dts[:, 2 * SL:3 * SL], dts[:, 2 * SL:3 * SL], AF.Ln)
                    sS.inc(inst)
                    scalar.drain()
                tag(f"{P_}dt3", sS)

                @block.vector
                def _(vector, P_=P_, sl_i=sl_i, prev_last=prev_last):
                    wait(vector, f"{P_}dt3")
                    if prev_last >= 0:
                        wait(vector, f"c{prev_last}_sup")
                    if sl_i > 0:
                        vector.tensor_copy(Pa[:, 0:1], Pa[:, SL:SL + 1])
                        vector.drain()
                    vector.tensor_scalar_max(dts[:, 3 * SL:4 * SL], dts[:, SL:2 * SL], 0.0)
                    vector.drain()
                    vector.tensor_add(dts[:, 0:SL], dts[:, 3 * SL:4 * SL], dts[:, 2 * SL:3 * SL])
                    vector.drain()
                    vector.tensor_scalar_mul(dts[:, 3 * SL:4 * SL], dts[:, 0:SL], acol_sb[:])
                    vector.drain()
                    vector.tensor_tensor_scan(Pa[:, 1:1 + SL], ones32[:],
                                              dts[:, 3 * SL:4 * SL], Pa[:, 0:1],
                                              ALU.mult, ALU.add)
                    vector.drain()
                    inst = vector.tensor_scalar_mul(nPa[:, 0:1 + SL], Pa[:, 0:1 + SL], -1.0)
                    sV.inc(inst)
                    vector.drain()
                tag(f"{P_}pa", sV)

                @block.vector
                def _(vector, P_=P_):
                    wait(vector, f"{P_}pa")
                    pe = bass.AP(tensor=Pa[:].tensor, offset=128, ap=[[SL + 1, 32], [128, SL // 128]])
                    pb = bass.AP(tensor=Pa[:].tensor, offset=0, ap=[[SL + 1, 32], [128, SL // 128]])
                    inst = vector.tensor_tensor(dec_sm[:, 0:SL // 128], pe, pb, ALU.subtract)
                    sV.inc(inst)
                    vector.drain()
                tag(f"{P_}dec0", sV)

                @block.scalar
                def _(scalar, P_=P_):
                    wait(scalar, f"{P_}dec0")
                    inst = scalar.activation(dec_sm[:, 4:4 + SL // 128], dec_sm[:, 0:SL // 128], AF.Exp)
                    sS.inc(inst)
                    scalar.drain()
                tag(f"{P_}dec1", sS)

                @block.tensor
                def _(tensor, P_=P_):
                    wait(tensor, f"{P_}dec1")
                    if goff > 0:
                        wait(tensor, f"c{goff // Q - 1}_sup")
                    inst = tensor.transpose(ps_aux[0:SL // 128, 448:480], dec_sm[:, 4:4 + SL // 128],
                                            ident_f[0:32, 0:32])
                    sT.inc(inst)
                tag(f"{P_}dectr", sT)

                @block.vector
                def _(vector, P_=P_):
                    wait(vector, f"{P_}dectr")
                    inst = vector.tensor_copy(decrow[0:SL // 128, :], ps_aux[0:SL // 128, 448:480])
                    sV.inc(inst)
                    vector.drain()
                tag(f"{P_}decrows", sV)

                for cl in range(SL // Q):
                    gch = goff // Q + cl
                    col = cl * Q
                    CP = f"c{gch}_"
                    pcol = cl * Q

                    @block.tensor
                    def _(tensor, col=col, CP=CP, P_=P_, gch=gch):
                        wait(tensor, f"{P_}convdone")
                        wait(tensor, f"{P_}pa")
                        if gch > 0:
                            wait(tensor, f"c{gch - 1}_psbig_free")
                        psbig_bf = ps_big[:].bitcast(BF16)
                        aux_bf = ps_aux[:].bitcast(BF16)
                        for cc in range(16):
                            tensor.transpose(psbig_bf[:, cc * 128:(cc + 1) * 128],
                                             xsilu[:, cc * SL + col: cc * SL + col + 128], ident_b[:])
                        tensor.transpose(ps_aux[:, 0:32], dts[:, col:col + 128], ident_f[0:32, 0:32])
                        inst = tensor.transpose(aux_bf[:, 128:256], B_sl[:, col:col + 128], ident_b[:])
                        sT.inc(inst)
                    tag(f"{CP}xtr", sT)

                    @block.scalar
                    def _(scalar, CP=CP, P_=P_, pcol=pcol, gch=gch):
                        wait(scalar, f"{P_}pa")
                        if gch > 0:
                            wait(scalar, f"c{gch - 1}_rrtr")
                        inst = scalar.activation(rr_hm[:], Pa[:, pcol + 1:pcol + 129], AF.Exp,
                                                 scale=-1.0, bias=Pa[:, pcol + 128:pcol + 129])
                        sS.inc(inst)
                        scalar.drain()
                    tag(f"{CP}rr", sS)

                    @block.vector
                    def _(vector, CP=CP):
                        wait(vector, f"{CP}xtr")
                        inst = vector.tensor_copy(small_tm[:, 0:32], ps_aux[:, 0:32])
                        sV.inc(inst)
                        vector.drain()
                    tag(f"{CP}dttm", sV)

                    @block.tensor
                    def _(tensor, CP=CP):
                        wait(tensor, f"{CP}rr")
                        wait(tensor, f"{CP}dttm")
                        inst = tensor.transpose(ps_aux[:, 32:64], rr_hm[:], ident_f[0:32, 0:32])
                        sT.inc(inst)
                    tag(f"{CP}rrtr", sT)

                    @block.vector
                    def _(vector, CP=CP):
                        wait(vector, f"{CP}rrtr")
                        vector.tensor_copy(small_tm[:, 32:64], ps_aux[:, 32:64])
                        vector.drain()
                        inst = vector.tensor_copy(B_tm[:], ps_aux[:].bitcast(BF16)[:, 128:256])
                        sV.inc(inst)
                        vector.drain()
                    tag(f"{CP}rrtm", sV)

                    @block.vector
                    def _(vector, CP=CP):
                        wait(vector, f"{CP}rrtm")
                        dt_rep = small_tm[:, 0:32].unsqueeze(2).broadcast_to([128, 32, 64])
                        rr_rep = small_tm[:, 32:64].unsqueeze(2).broadcast_to([128, 32, 64])
                        xv = lambda t: t[:].rearrange("p (h e) -> p h e", h=32)
                        psbig_bf = ps_big[:].bitcast(BF16)[:, 0:2048]
                        vector.tensor_tensor(xv(Xd), psbig_bf.rearrange("p (h e) -> p h e", h=32),
                                             dt_rep, ALU.mult)
                        vector.drain()
                        vector.tensor_tensor(xv(Xw), xv(Xd), rr_rep, ALU.mult)
                        vector.drain()
                        vector.reciprocal(small_tm[:, 96:128], small_tm[:, 0:32])
                        vector.drain()
                        vector.tensor_tensor(small_tm[:, 64:96], dpar_sb[:], small_tm[:, 96:128], ALU.mult)
                        vector.drain()
                        ddt_rep = small_tm[:, 64:96].unsqueeze(2).broadcast_to([128, 32, 64])
                        inst = vector.tensor_tensor(xv(XDt), xv(Xd), ddt_rep, ALU.mult)
                        sV.inc(inst)
                        vector.drain()
                    tag(f"{CP}xd", sV)

                    if not is_pref:
                        @block.tensor
                        def _(tensor, col=col, CP=CP, P_=P_):
                            wait(tensor, f"{CP}rrtm")
                            inst = tensor.matmul(ps_aux[:, 256:384], B_sl[:, col:col + 128],
                                                 C_sl[:, col:col + 128], start=True, stop=True)
                            sT.inc(inst)
                        tag(f"{CP}gt", sT)

                        @block.vector
                        def _(vector, CP=CP):
                            wait(vector, f"{CP}gt")
                            inst = vector.tensor_tensor(Gt_m[:], ps_aux[:, 256:384], mask01[:], ALU.mult)
                            sV.inc(inst)
                            vector.drain()
                        tag(f"{CP}gtm", sV)

                        @block.vector
                        def _(vector, CP=CP, pcol=pcol):
                            inst = vector.tensor_scalar_mul(npp_rows[:], ones32[:, 0:128], nPa[:, pcol:pcol + 1])
                            sV.inc(inst)
                            vector.drain()
                        tag(f"{CP}npp", sV)

                        for grp in range(4):
                            gpar = grp % 2

                            @block.tensor
                            def _(tensor, grp=grp, CP=CP, pcol=pcol, gch=gch):
                                wait(tensor, f"{CP}gtm" if grp == 0 else f"{CP}mt{grp - 1}")
                                if grp == 0 and gch == nch_pref:
                                    wait(tensor, "sel_done")
                                tensor.matmul(ps_e[:, 0:512], ident_b[:], maskneg8[:, 0:512],
                                              start=True, stop=False, skip_group_check=True)
                                tensor.matmul(ps_e[:, 512:1024], ident_b[:], maskneg8[:, 512:1024],
                                              start=True, stop=False, skip_group_check=True)
                                for hh in range(8):
                                    h = grp * 8 + hh
                                    sl8 = ps_e[:, hh * 128:(hh + 1) * 128]
                                    selh = sel32[:, h * 128:(h + 1) * 128]
                                    tensor.matmul(sl8, selh, Pa[0:32, pcol + 1:pcol + 129],
                                                  start=False, stop=False, skip_group_check=True)
                                    inst = tensor.matmul(sl8, nPa[0:32, pcol + 1:pcol + 129], selh,
                                                         start=False, stop=(hh == 7), skip_group_check=True)
                                sT.inc(inst)
                            tag(f"{CP}emm{grp}", sT)

                            @block.scalar
                            def _(scalar, grp=grp, gpar=gpar, CP=CP):
                                wait(scalar, f"{CP}emm{grp}")
                                if grp >= 2:
                                    wait(scalar, f"{CP}mt{grp - 2}")
                                inst = scalar.activation(Eex[:, gpar * 1024:(gpar + 1) * 1024], ps_e[:], AF.Exp)
                                sS.inc(inst)
                                scalar.drain()
                            tag(f"{CP}eex{grp}", sS)

                            @block.vector
                            def _(vector, grp=grp, gpar=gpar, CP=CP):
                                wait(vector, f"{CP}eex{grp}")
                                gt_rep = Gt_m[:].unsqueeze(1).broadcast_to([128, 8, 128])
                                inst = vector.tensor_tensor(
                                    Mt[:, gpar * 1024:(gpar + 1) * 1024].rearrange("p (h i) -> p h i", h=8),
                                    Eex[:, gpar * 1024:(gpar + 1) * 1024].rearrange("p (h i) -> p h i", h=8),
                                    gt_rep, ALU.mult)
                                sV.inc(inst)
                                vector.drain()
                            tag(f"{CP}mt{grp}", sV)

                            @block.tensor
                            def _(tensor, grp=grp, gpar=gpar, CP=CP):
                                wait(tensor, f"{CP}mt{grp}")
                                if grp == 0:
                                    wait(tensor, f"{CP}xd")
                                    for b4 in range(4):
                                        tensor.matmul(ps_big[:, b4 * 512:(b4 + 1) * 512], ones_row_b[:],
                                                      zero_row[:], start=True, stop=False,
                                                      skip_group_check=True)
                                for hh in range(8):
                                    h = grp * 8 + hh
                                    inst = tensor.matmul(ps_big[:, h * 64:(h + 1) * 64],
                                                         Mt[:, gpar * 1024 + hh * 128: gpar * 1024 + (hh + 1) * 128],
                                                         Xd[:, h * 64:(h + 1) * 64],
                                                         start=False, stop=False, skip_group_check=True)
                                sT.inc(inst)
                            tag(f"{CP}yi{grp}", sT)

                        for grp in range(4):
                            gpar = grp % 2

                            @block.tensor
                            def _(tensor, grp=grp, CP=CP, pcol=pcol):
                                wait(tensor, f"{CP}yi3" if grp == 0 else f"{CP}mts{grp - 1}")
                                if grp == 0:
                                    wait(tensor, f"{CP}npp")
                                for hh in range(8):
                                    h = grp * 8 + hh
                                    sl8 = ps_e[:, hh * 128:(hh + 1) * 128]
                                    selh = sel32[:, h * 128:(h + 1) * 128]
                                    tensor.matmul(sl8, selh, Pa[0:32, pcol + 1:pcol + 129],
                                                  start=True, stop=False, skip_group_check=True)
                                    inst = tensor.matmul(sl8, npp_rows[:], selh,
                                                         start=False, stop=True, skip_group_check=True)
                                sT.inc(inst)
                            tag(f"{CP}esmm{grp}", sT)

                            @block.scalar
                            def _(scalar, grp=grp, gpar=gpar, CP=CP):
                                wait(scalar, f"{CP}esmm{grp}")
                                wait(scalar, f"{CP}mts{grp - 2}" if grp >= 2 else f"{CP}mt{2 + grp}")
                                inst = scalar.activation(Eex[:, gpar * 1024:(gpar + 1) * 1024], ps_e[:], AF.Exp)
                                sS.inc(inst)
                                scalar.drain()
                            tag(f"{CP}esex{grp}", sS)

                            @block.vector
                            def _(vector, grp=grp, gpar=gpar, CP=CP, col=col):
                                wait(vector, f"{CP}esex{grp}")
                                c_rep = C_sl[:, col:col + 128].unsqueeze(1).broadcast_to([128, 8, 128])
                                inst = vector.tensor_tensor(
                                    Mts[:, gpar * 1024:(gpar + 1) * 1024].rearrange("p (h i) -> p h i", h=8),
                                    Eex[:, gpar * 1024:(gpar + 1) * 1024].rearrange("p (h i) -> p h i", h=8),
                                    c_rep, ALU.mult)
                                sV.inc(inst)
                                vector.drain()
                            tag(f"{CP}mts{grp}", sV)

                            @block.tensor
                            def _(tensor, grp=grp, gpar=gpar, CP=CP):
                                wait(tensor, f"{CP}mts{grp}")
                                for hh in range(8):
                                    h = grp * 8 + hh
                                    inst = tensor.matmul(ps_big[:, h * 64:(h + 1) * 64],
                                                         Mts[:, gpar * 1024 + hh * 128: gpar * 1024 + (hh + 1) * 128],
                                                         S_sb[:, h * 64:(h + 1) * 64],
                                                         start=False, stop=(hh == 7), skip_group_check=True)
                                sT.inc(inst)
                            tag(f"{CP}ys{grp}", sT)

                        @block.vector
                        def _(vector, CP=CP):
                            wait(vector, f"{CP}ys3")
                            inst = vector.tensor_tensor(y2[:], ps_big[:], XDt[:], ALU.add)
                            sV.inc(inst)
                            vector.drain()
                        tag(f"{CP}y2", sV)

                        @block.tensor
                        def _(tensor, CP=CP):
                            wait(tensor, f"{CP}y2")
                            psbig_bf = ps_big[:].bitcast(BF16)
                            for ec in range(16):
                                inst = tensor.transpose(psbig_bf[:, ec * 128:(ec + 1) * 128],
                                                        y2[:, ec * 128:(ec + 1) * 128], ident_b[:])
                            sT.inc(inst)
                        tag(f"{CP}y2tr", sT)

                        @block.vector
                        def _(vector, CP=CP, col=col, P_=P_):
                            wait(vector, f"{CP}y2tr")
                            psbig_bf = ps_big[:].bitcast(BF16)[:, 0:2048]
                            inst = vector.tensor_tensor(
                                y3[:].rearrange("p (ec i) -> p ec i", ec=16),
                                psbig_bf.rearrange("p (ec i) -> p ec i", ec=16),
                                zs[:].rearrange("p (ec t) -> p ec t", ec=16)[:, :, col:col + 128],
                                ALU.mult)
                            sV.inc(inst)
                            vector.drain()
                        tag(f"{CP}y3", sV)

                        @block.scalar
                        def _(scalar, CP=CP):
                            wait(scalar, f"{CP}y3")
                            inst = scalar.activation(y2[:], y3[:], AF.Square)
                            sS.inc(inst)
                            scalar.drain()
                        tag(f"{CP}sqy", sS)

                        @block.tensor
                        def _(tensor, CP=CP, gch=gch):
                            wait(tensor, f"{CP}sqy")
                            if gch > 0 and gch - 1 >= nch_pref:
                                wait(tensor, f"c{gch - 1}_psnrm_free")
                            for ec in range(16):
                                inst = tensor.matmul(ps_nrm[:, 0:1], y2[:, ec * 128:(ec + 1) * 128],
                                                     ones_col_b[:], start=(ec == 0), stop=(ec == 15))
                            sT.inc(inst)
                        tag(f"{CP}ms", sT)

                        @block.vector
                        def _(vector, CP=CP):
                            wait(vector, f"{CP}ms")
                            inst = vector.tensor_scalar(nrm_sm[:, 0:1], ps_nrm[:, 0:1], 1.0 / D_INNER,
                                                        EPS, ALU.mult, ALU.add)
                            sV.inc(inst)
                            vector.drain()
                        tag(f"{CP}mse", sV)

                        @block.scalar
                        def _(scalar, CP=CP):
                            wait(scalar, f"{CP}mse")
                            scalar.activation(nrm_sm[:, 1:2], nrm_sm[:, 0:1], AF.Ln)
                            scalar.drain()
                            inst = scalar.activation(nrm_sm[:, 2:3], nrm_sm[:, 1:2], AF.Exp, scale=-0.5)
                            sS.inc(inst)
                            scalar.drain()
                        tag(f"{CP}rstd", sS)

                        @block.tensor
                        def _(tensor, CP=CP):
                            wait(tensor, f"{CP}rstd")
                            inst = tensor.transpose(ps_nrm[0:1, 128:256], nrm_sm[:, 2:3], ident_f[:])
                            sT.inc(inst)
                        tag(f"{CP}rstdtr", sT)

                        @block.vector
                        def _(vector, CP=CP):
                            wait(vector, f"{CP}rstdtr")
                            inst = vector.tensor_copy(rrow[:], ps_nrm[0:1, 128:256])
                            sV.inc(inst)
                            vector.drain()
                        tag(f"{CP}rrow", sV)

                        @block.tensor
                        def _(tensor, CP=CP):
                            wait(tensor, f"{CP}rrow")
                            inst = tensor.matmul(ps_nrm[:, 256:384], ones_row[:], rrow[:],
                                                 start=True, stop=True)
                            sT.inc(inst)
                        tag(f"{CP}rb", sT)

                        @block.vector
                        def _(vector, CP=CP, gch=gch):
                            wait(vector, f"{CP}rb")
                            if gch - 1 >= nch_pref:
                                wait(vector, f"c{gch - 1}_y4out")
                            nv = lambda t: t.rearrange("p (ec i) -> p ec i", ec=16)
                            vector.tensor_tensor(nv(y4asm[:]), nv(y3[:]),
                                                 normw_sb[:].unsqueeze(2).broadcast_to([128, 16, 128]),
                                                 ALU.mult)
                            vector.drain()
                            inst = vector.tensor_tensor(nv(y4asm[:]), nv(y4asm[:]),
                                                        ps_nrm[:, 256:384].unsqueeze(1).broadcast_to([128, 16, 128]),
                                                        ALU.mult)
                            sV.inc(inst)
                            vector.drain()
                        tag(f"{CP}y4", sV)
                        tag(f"{CP}psnrm_free", sV)

                        @block.sync
                        def _(sync, CP=CP, gch=gch):
                            wait(sync, f"{CP}y4")
                            mch = gch - nch_pref
                            inst = sync.dma_start(out=d_y4.rearrange("(c p) f -> c p f", p=128)[mch],
                                                  in_=y4asm[:])
                            dY.inc(inst, 16)
                        tag(f"{CP}y4out", dY)

                        tag_y_done = f"{CP}y3"
                    else:
                        tag_y_done = f"{CP}xd"

                    @block.tensor
                    def _(tensor, CP=CP, tag_y_done=tag_y_done):
                        wait(tensor, tag_y_done)
                        wait(tensor, f"{CP}xd")
                        for j4 in range(4):
                            inst = tensor.matmul(ps_big[:, j4 * 512:(j4 + 1) * 512], B_tm[:],
                                                 Xw[:, j4 * 512:(j4 + 1) * 512], start=True, stop=True)
                        sT.inc(inst)
                    tag(f"{CP}sch", sT)

                    @block.tensor
                    def _(tensor, CP=CP, cl=cl, P_=P_):
                        wait(tensor, f"{P_}decrows")
                        wait(tensor, f"{CP}sch")
                        inst = tensor.matmul(ps_aux[:, 480:512], sel32[0:SL // 128, cl * 128:(cl + 1) * 128],
                                             decrow[0:SL // 128, :], start=True, stop=True)
                        sT.inc(inst)
                    tag(f"{CP}decb", sT)

                    @block.vector
                    def _(vector, CP=CP, gch=gch):
                        wait(vector, f"{CP}decb")
                        dec_rep = ps_aux[:, 480:512].unsqueeze(2).broadcast_to([128, 32, 64])
                        vector.tensor_tensor(S_t1[:].rearrange("p (h e) -> p h e", h=32),
                                             S_sb[:].rearrange("p (h e) -> p h e", h=32), dec_rep, ALU.mult)
                        vector.drain()
                        inst = vector.tensor_tensor(S_sb[:], S_t1[:], ps_big[:], ALU.add)
                        sV.inc(inst)
                        vector.drain()
                        if gch == nch_pref - 1:
                            inst = vector.tensor_scalar_mul(S_sb[:], S_sb[:], flag_sb[:])
                            sV.inc(inst)
                            vector.drain()
                    tag(f"{CP}sup", sV)
                    tag(f"{CP}psbig_free", sV)

            last_sup = f"c{NTOT // Q - 1}_sup"

        ESm = contextlib.ExitStack()
        ffn_acc = ESm.enter_context(nc.sbuf_tensor("ffn_acc", [128, 16 * D_MODEL], BF16))
        ES2 = contextlib.ExitStack()
        hbuf = ES2.enter_context(nc.sbuf_tensor("hbuf", [128, 16 * NT_MAIN], BF16))
        wd_sb = ES2.enter_context(nc.sbuf_tensor("wd_sb", [128, 16 * D_MODEL], BF16))
        wgu_buf = ES2.enter_context(nc.sbuf_tensor("wgu_buf", [128, 2 * 2048], BF16))
        sg_sb = ES2.enter_context(nc.sbuf_tensor("sg_sb", [128, 2 * 512], BF16))
        lgs = ES2.enter_context(nc.sbuf_tensor("lgs", [128, 40], F32))
        with ES2:
         if True:
            for tc in range(n_tc):
                @block.tensor
                def _(tensor, tc=tc):
                    wait(tensor, last_sup)
                    wait(tensor, "p0_dG")
                    if tc > 0:
                        wait(tensor, f"rt{tc - 1}_lg")
                    for dc in range(8):
                        inst = tensor.matmul(ps_aux[:, 0:8],
                                             xn_fm[:, dc * NT_MAIN + tc * 128: dc * NT_MAIN + (tc + 1) * 128],
                                             gwT_sb[:, dc * 8:(dc + 1) * 8],
                                             start=(dc == 0), stop=(dc == 7))
                    sT.inc(inst)
                tag(f"rt{tc}_mm", sT)

                @block.vector
                def _(vector, tc=tc):
                    wait(vector, f"rt{tc}_mm")
                    if tc > 0:
                        wait(vector, f"rt{tc - 1}_sg")
                    inst = vector.tensor_copy(lgs[:, 0:8], ps_aux[:, 0:8])
                    sV.inc(inst)
                    vector.drain()
                tag(f"rt{tc}_lg", sV)

                @block.scalar
                def _(scalar, tc=tc):
                    wait(scalar, f"rt{tc}_lg")
                    inst = scalar.activation(lgs[:, 8:16], lgs[:, 0:8], AF.Sigmoid)
                    sS.inc(inst)
                    scalar.drain()
                tag(f"rt{tc}_sg", sS)

                @block.vector
                def _(vector, tc=tc):
                    wait(vector, f"rt{tc}_sg")
                    cslot = c_tm[:, tc * 8:(tc + 1) * 8]
                    vector.tensor_tensor(cslot, lgs[:, 0:8], beta_sb[:], ALU.add)
                    vector.drain()
                    vector.reduce_max(lgs[:, 32:33], cslot, axis=AX)
                    vector.drain()
                    vector.tensor_scalar(lgs[:, 16:24], cslot, lgs[:, 32:33], None, ALU.is_ge)
                    vector.drain()
                    vector.scalar_tensor_tensor(lgs[:, 24:32], lgs[:, 16:24], -1e30, cslot, ALU.mult, ALU.add)
                    vector.drain()
                    vector.reduce_max(lgs[:, 33:34], lgs[:, 24:32], axis=AX)
                    vector.drain()
                    vector.tensor_scalar(lgs[:, 24:32], lgs[:, 24:32], lgs[:, 33:34], None, ALU.is_ge)
                    vector.drain()
                    vector.tensor_add(lgs[:, 16:24], lgs[:, 16:24], lgs[:, 24:32])
                    vector.drain()
                    inst = vector.tensor_tensor(cslot, lgs[:, 8:16], lgs[:, 16:24], ALU.mult)
                    sV.inc(inst)
                    vector.drain()
                tag(f"rt{tc}_c", sV)
            tag("router_done", sV)

            for e in range(NEXP + 1):
                is_shared = e == NEXP
                EP = f"e{e}_"
                for mc in range(16):
                    par = mc % 2

                    @block.gpsimd
                    def _(gpsimd, e=e, mc=mc, par=par, EP=EP):
                        if mc >= 2:
                            wait(gpsimd, f"{EP}gu_{mc - 2}_{n_tt - 1}")
                        elif e > 0:
                            wait(gpsimd, f"e{e - 1}_gu_{mc + 14}_{n_tt - 1}")
                        else:
                            wait(gpsimd, last_sup)
                        inst = gpsimd.dma_start(
                            out=wgu_buf[:, par * 2048: par * 2048 + 1024].rearrange("p (hc m) -> p hc m", hc=8),
                            in_=d_wg.rearrange("e (hc p) m -> e p hc m", p=128)[e][:, :, mc * 128:(mc + 1) * 128])
                        dM[mc % 2].inc(inst, 16)
                        inst = gpsimd.dma_start(
                            out=wgu_buf[:, par * 2048 + 1024: par * 2048 + 2048].rearrange("p (hc m) -> p hc m", hc=8),
                            in_=d_wu.rearrange("e (hc p) m -> e p hc m", p=128)[e][:, :, mc * 128:(mc + 1) * 128])
                        dM[mc % 2].inc(inst, 16)
                        if e > 0:
                            wait(gpsimd, f"e{e - 1}_dn{2 * n_tc - 1}")
                        inst = gpsimd.dma_start(
                            out=wd_sb[:, mc * 1024:(mc + 1) * 1024],
                            in_=d_wd.rearrange("e (mc p) hh -> e mc p hh", p=128)[e, mc])
                        dD.inc(inst, 16)
                    tag(f"{EP}w{mc}", dM[mc % 2])
                    tag(f"{EP}wd{mc}", dD)

                    for tt in range(n_tt):
                        j = mc * n_tt + tt
                        qpar = j % 2
                        gps = ps_big[:, qpar * 1024: qpar * 1024 + 512]
                        ups = ps_big[:, qpar * 1024 + 512: qpar * 1024 + 1024]

                        @block.tensor
                        def _(tensor, e=e, mc=mc, tt=tt, j=j, gps=gps, ups=ups, par=par, EP=EP):
                            wait(tensor, f"{EP}w{mc}")
                            if j >= 2:
                                wait(tensor, f"{EP}h_{(j - 2) // n_tt}_{(j - 2) % n_tt}")
                            elif e > 0:
                                wait(tensor, f"e{e - 1}_h_15_{n_tt - 1}")
                            for hc in range(8):
                                tensor.matmul(gps, wgu_buf[:, par * 2048 + hc * 128: par * 2048 + (hc + 1) * 128],
                                              xn_fm[:, hc * NT_MAIN + tt * 512: hc * NT_MAIN + tt * 512 + 512],
                                              start=(hc == 0), stop=(hc == 7))
                            for hc in range(8):
                                inst = tensor.matmul(ups,
                                                     wgu_buf[:, par * 2048 + 1024 + hc * 128: par * 2048 + 1024 + (hc + 1) * 128],
                                                     xn_fm[:, hc * NT_MAIN + tt * 512: hc * NT_MAIN + tt * 512 + 512],
                                                     start=(hc == 0), stop=(hc == 7))
                            sT.inc(inst)
                        tag(f"{EP}gu_{mc}_{tt}", sT)

                        @block.scalar
                        def _(scalar, j=j, mc=mc, tt=tt, qpar=qpar, gps=gps, e=e, EP=EP):
                            wait(scalar, f"{EP}gu_{mc}_{tt}")
                            if j >= 2:
                                wait(scalar, f"{EP}h_{(j - 2) // n_tt}_{(j - 2) % n_tt}")
                            elif e > 0:
                                wait(scalar, f"e{e - 1}_h_15_{n_tt - 1}")
                            inst = scalar.activation(sg_sb[:, qpar * 512:(qpar + 1) * 512], gps, AF.Silu)
                            sS.inc(inst)
                            scalar.drain()
                        tag(f"{EP}sg_{mc}_{tt}", sS)

                        @block.vector
                        def _(vector, j=j, mc=mc, tt=tt, qpar=qpar, ups=ups, e=e, EP=EP):
                            wait(vector, f"{EP}sg_{mc}_{tt}")
                            inst = vector.tensor_tensor(
                                hbuf[:, mc * NT_MAIN + tt * 512: mc * NT_MAIN + tt * 512 + 512],
                                sg_sb[:, qpar * 512:(qpar + 1) * 512], ups, ALU.mult)
                            sV.inc(inst)
                            vector.drain()
                        tag(f"{EP}h_{mc}_{tt}", sV)

                for tc in range(n_tc):
                    for nt in range(2):
                        j = tc * 2 + nt
                        dpar_ = j % 2
                        dps = ps_e[:, dpar_ * 512:(dpar_ + 1) * 512]

                        @block.tensor
                        def _(tensor, e=e, tc=tc, nt=nt, j=j, dps=dps, EP=EP):
                            if j == 0:
                                for mc in range(16):
                                    wait(tensor, f"{EP}h_{mc}_{n_tt - 1}")
                                wait(tensor, f"{EP}wd15")
                                wait(tensor, "router_done")
                            if j >= 2:
                                wait(tensor, f"{EP}dnv{j - 2}")
                            elif e > 0:
                                wait(tensor, f"e{e - 1}_dnv{2 * n_tc - 2 + j}")
                            for mc in range(16):
                                inst = tensor.matmul(dps,
                                                     hbuf[:, mc * NT_MAIN + tc * 128: mc * NT_MAIN + (tc + 1) * 128],
                                                     wd_sb[:, mc * 1024 + nt * 512: mc * 1024 + nt * 512 + 512],
                                                     start=(mc == 0), stop=(mc == 15))
                            sT.inc(inst)
                        tag(f"{EP}dn{j}", sT)

                        @block.vector
                        def _(vector, e=e, tc=tc, nt=nt, j=j, dps=dps, EP=EP, is_shared=is_shared):
                            wait(vector, f"{EP}dn{j}")
                            acc = ffn_acc[:, tc * D_MODEL + nt * 512: tc * D_MODEL + nt * 512 + 512]
                            if e == 0:
                                inst = vector.tensor_scalar_mul(acc, dps, c_tm[:, tc * 8 + e: tc * 8 + e + 1])
                            elif is_shared:
                                inst = vector.tensor_tensor(acc, dps, acc, ALU.add)
                            else:
                                inst = vector.scalar_tensor_tensor(acc, dps, c_tm[:, tc * 8 + e: tc * 8 + e + 1],
                                                                   acc, ALU.mult, ALU.add)
                            sV.inc(inst)
                            vector.drain()
                        tag(f"{EP}dnv{j}", sV)
            tag("moe_done", sV)

        ES3 = contextlib.ExitStack()
        woutT_sb = ES3.enter_context(nc.sbuf_tensor("woutT_sb", [128, 16 * D_MODEL], BF16))
        y4t = ES3.enter_context(nc.sbuf_tensor("y4t", [128, 2 * D_INNER], BF16))
        xres = ES3.enter_context(nc.sbuf_tensor("xres", [128, 2 * D_MODEL], F32))
        gsil = ES3.enter_context(nc.sbuf_tensor("gsil", [128, 2 * D_MODEL], BF16))
        sg_f = ES3.enter_context(nc.sbuf_tensor("sg_f", [128, D_MODEL], BF16))
        otile = ES3.enter_context(nc.sbuf_tensor("otile", [128, 2 * D_MODEL], F32))
        with ES3:
         if True:
            @block.gpsimd
            def _(gpsimd):
                wait(gpsimd, "moe_done")
                for ec in range(16):
                    inst = gpsimd.dma_start(out=woutT_sb[:, ec * 1024:(ec + 1) * 1024],
                                            in_=d_woutT.rearrange("(ec p) d -> ec p d", p=128)[ec])
                    dWo.inc(inst, 16)
            tag("wout_ld", dWo)

            for tc in range(n_tc):
                par = tc % 2

                @block.sync
                def _(sync, tc=tc, par=par):
                    wait(sync, "moe_done")
                    wait(sync, f"c{nch_pref + nch_main - 1}_y4out")
                    if tc >= 2:
                        wait(sync, f"fin{tc - 2}_out")
                    inst = sync.dma_start(out=y4t[:, par * 2048:(par + 1) * 2048],
                                          in_=d_y4.rearrange("(c p) f -> c p f", p=128)[tc])
                    dF[par].inc(inst, 16)
                    inst = sync.dma_start(out=xres[:, par * 1024:(par + 1) * 1024],
                                          in_=d_xm[tc * 128:(tc + 1) * 128, :])
                    dF[par].inc(inst, 16)
                tag(f"fin{tc}_ld", dF[par])

                @block.tensor
                def _(tensor, tc=tc, par=par):
                    wait(tensor, f"fin{tc}_ld")
                    wait(tensor, "wout_ld")
                    if tc >= 2:
                        wait(tensor, f"fin{tc - 2}_sev")
                    for half in range(2):
                        dpsF = ps_aux if half == 0 else ps_nrm
                        for ec in range(16):
                            inst = tensor.matmul(dpsF[:],
                                                 y4t[:, par * 2048 + ec * 128: par * 2048 + (ec + 1) * 128],
                                                 woutT_sb[:, ec * 1024 + half * 512: ec * 1024 + half * 512 + 512],
                                                 start=(ec == 0), stop=(ec == 15))
                    sT.inc(inst)
                tag(f"fin{tc}_s", sT)

                @block.scalar
                def _(scalar, tc=tc, par=par):
                    wait(scalar, "moe_done")
                    if tc >= 2:
                        wait(scalar, f"fin{tc - 2}_sev")
                    inst = scalar.activation(gsil[:, par * 1024:(par + 1) * 1024],
                                             ffn_acc[:, tc * D_MODEL:(tc + 1) * D_MODEL], AF.Silu)
                    sS.inc(inst)
                    scalar.drain()
                tag(f"fin{tc}_g", sS)

                @block.vector
                def _(vector, tc=tc, par=par):
                    wait(vector, f"fin{tc}_s")
                    wait(vector, f"fin{tc}_g")
                    if tc >= 2:
                        wait(vector, f"fin{tc - 2}_out")
                    vector.tensor_tensor(sg_f[:, 0:512], ps_aux[:], gsil[:, par * 1024: par * 1024 + 512], ALU.mult)
                    vector.drain()
                    vector.tensor_tensor(sg_f[:, 512:1024], ps_nrm[:], gsil[:, par * 1024 + 512:(par + 1) * 1024], ALU.mult)
                    vector.drain()
                    inst = vector.tensor_tensor(otile[:, par * 1024:(par + 1) * 1024],
                                                xres[:, par * 1024:(par + 1) * 1024], sg_f[:], ALU.add)
                    sV.inc(inst)
                    vector.drain()
                tag(f"fin{tc}_sev", sV)

                @block.sync
                def _(sync, tc=tc, par=par):
                    wait(sync, f"fin{tc}_sev")
                    inst = sync.dma_start(out=d_out[tc * 128:(tc + 1) * 128, :],
                                          in_=otile[:, par * 1024:(par + 1) * 1024])
                    dO[par].inc(inst, 16)
                tag(f"fin{tc}_out", dO[par])

            @block.sync
            def _(sync):
                sync.wait_ge(_dO0, C["dO0"])
                sync.wait_ge(_dO1, C["dO1"])
        ESm.close()

    return nc


def _prep_inputs(kw):
    x = np.ascontiguousarray(np.asarray(kw["L0_attn_x"], np.float32))
    W_in = np.asarray(kw["L0_attn_in_proj_w"], np.float32)
    conv_w = np.asarray(kw["L0_attn_conv_w"], np.float32)
    conv_b = np.asarray(kw["L0_attn_conv_b"], np.float32)
    norm_w = np.asarray(kw["L0_attn_norm_w"], np.float32)
    W_out = np.asarray(kw["L0_attn_out_proj_w"], np.float32)
    dt_bias = np.asarray(kw["L0_attn_dt_bias"], np.float32)
    A_log = np.asarray(kw["L0_attn_A_log"], np.float32)
    D_param = np.asarray(kw["L0_attn_D_param"], np.float32)
    gate_w = np.asarray(kw["L0_ffn_gate_w"], np.float32)
    beta = np.asarray(kw["L0_ffn_beta"], np.float32)
    com = dict(
        w_inT=np.ascontiguousarray(W_in.T),
        w_outT=np.ascontiguousarray(W_out.T),
        gwT=np.ascontiguousarray(gate_w.T),
        convw=np.ascontiguousarray(conv_w[:, 0, :].reshape(18, 128, 4).transpose(1, 0, 2).reshape(128, 72)),
        convb=np.ascontiguousarray(conv_b.reshape(18, 128).T),
        normw=np.ascontiguousarray(norm_w.reshape(16, 128).T),
        dpar=np.ascontiguousarray(np.tile(D_param[None, :], (128, 1))),
        dtb=np.ascontiguousarray(dt_bias[:, None]),
        alog=np.ascontiguousarray(A_log[:, None]),
        beta_t=np.ascontiguousarray(np.tile(beta[None, :], (128, 1))),
        w_gate=np.ascontiguousarray(np.concatenate(
            [np.asarray(kw["L0_ffn_gate_proj_w"], np.float32),
             np.asarray(kw["L0_ffn_shared_gate_w"], np.float32)], axis=0)),
        w_up=np.ascontiguousarray(np.concatenate(
            [np.asarray(kw["L0_ffn_up_proj_w"], np.float32),
             np.asarray(kw["L0_ffn_shared_up_w"], np.float32)], axis=0)),
        w_down=np.ascontiguousarray(np.concatenate(
            [np.asarray(kw["L0_ffn_down_proj_w"], np.float32),
             np.asarray(kw["L0_ffn_shared_down_w"], np.float32)], axis=0)),
    )
    return x, com


def kernel(**kw):
    x, com = _prep_inputs(kw)
    B, L, D = x.shape
    NT = L // 2
    nc = build_nc(NT_MAIN=NT, NT_PREF=NT)
    in_maps = []
    for core in range(8):
        b, h = core // 2, core % 2
        m = dict(com)
        m["x_main"] = np.ascontiguousarray(x[b, h * NT:(h + 1) * NT])
        m["x_pref"] = np.ascontiguousarray(x[b, 0:NT]) if h == 1 else np.zeros((NT, D), np.float32)
        m["pref_flag"] = np.full((128, 1), float(h), np.float32)
        in_maps.append(m)
    res = run_bass_kernel_spmd(nc, in_maps, core_ids=list(range(8)))
    out = np.empty((B, L, D), np.float32)
    for core in range(8):
        b, h = core // 2, core % 2
        out[b, h * NT:(h + 1) * NT] = res.results[core]["out"]
    return out


# version 05
# speedup vs baseline: 1.2920x; 1.2920x over previous
import contextlib
import numpy as np
import concourse.bass as bass
import concourse.mybir as mybir
from concourse.bass_utils import run_bass_kernel_spmd

F32 = mybir.dt.float32
BF16 = mybir.dt.bfloat16
AF = mybir.ActivationFunctionType
ALU = mybir.AluOpType
AX = mybir.AxisListType.X

D_MODEL = 1024
D_STATE = 128
HEADDIM = 64
D_INNER = 2048
NHEADS = 32
CONV_DIM = 2304
D_IN_PROJ = 4384
EPS = 1e-5
Q = 128
SL = 512
NEXP = 8
M_FF = 2048

OC_X0 = 16
OC_B = 32
OC_C = 33
OC_DT = 34


def build_nc(NT_MAIN=2048, NT_PREF=2048):
    assert NT_MAIN % SL == 0 and NT_PREF % SL == 0 and NT_PREF >= SL
    n_sl_pref = NT_PREF // SL
    n_sl_main = NT_MAIN // SL
    nch_main = NT_MAIN // Q
    nch_pref = NT_PREF // Q
    NTOT = NT_PREF + NT_MAIN
    n_tc = NT_MAIN // 128
    n_tt = NT_MAIN // 512

    nc = bass.Bass("TRN2", target_bir_lowering=False)

    d_xm = nc.dram_tensor("x_main", [NT_MAIN, D_MODEL], F32, kind="ExternalInput").ap()
    d_xp = nc.dram_tensor("x_pref", [NT_PREF, D_MODEL], F32, kind="ExternalInput").ap()
    d_flag = nc.dram_tensor("pref_flag", [128, 1], F32, kind="ExternalInput").ap()
    d_winT = nc.dram_tensor("w_inT", [D_MODEL, D_IN_PROJ], F32, kind="ExternalInput").ap()
    d_woutT = nc.dram_tensor("w_outT", [D_INNER, D_MODEL], F32, kind="ExternalInput").ap()
    d_gwT = nc.dram_tensor("gwT", [D_MODEL, NEXP], F32, kind="ExternalInput").ap()
    d_convw = nc.dram_tensor("convw", [128, 18 * 4], F32, kind="ExternalInput").ap()
    d_convb = nc.dram_tensor("convb", [128, 18], F32, kind="ExternalInput").ap()
    d_normw = nc.dram_tensor("normw", [128, 16], F32, kind="ExternalInput").ap()
    d_dpar = nc.dram_tensor("dpar", [128, NHEADS], F32, kind="ExternalInput").ap()
    d_dtb = nc.dram_tensor("dtb", [NHEADS, 1], F32, kind="ExternalInput").ap()
    d_alog = nc.dram_tensor("alog", [NHEADS, 1], F32, kind="ExternalInput").ap()
    d_beta = nc.dram_tensor("beta_t", [128, NEXP], F32, kind="ExternalInput").ap()
    d_wg = nc.dram_tensor("w_gate", [NEXP + 1, D_MODEL, M_FF], F32, kind="ExternalInput").ap()
    d_wu = nc.dram_tensor("w_up", [NEXP + 1, D_MODEL, M_FF], F32, kind="ExternalInput").ap()
    d_wd = nc.dram_tensor("w_down", [NEXP + 1, M_FF, D_MODEL], F32, kind="ExternalInput").ap()
    d_out = nc.dram_tensor("out", [NT_MAIN, D_MODEL], F32, kind="ExternalOutput").ap()
    d_y4 = nc.dram_tensor("y4scr", [nch_main * 128, D_INNER], BF16, kind="Internal").ap()

    C = {}
    H = {}

    class Sem:
        def __init__(self, handle, name):
            self.h, self.name = handle, name
            C[name] = 0

        def inc(self, instr, n=1):
            instr.then_inc(self.h, n)
            C[self.name] += n

    tags = {}

    ES = contextlib.ExitStack()
    ident_f = ES.enter_context(nc.sbuf_tensor("ident_f", [128, 128], F32))
    ident_b = ES.enter_context(nc.sbuf_tensor("ident_b", [128, 128], BF16))
    ones_row = ES.enter_context(nc.sbuf_tensor("ones_row", [1, 128], F32))
    eps_col = ES.enter_context(nc.sbuf_tensor("eps_col", [128, 1], F32))
    zero_row = ES.enter_context(nc.sbuf_tensor("zero_row", [1, 512], BF16))
    ones_row_b = ES.enter_context(nc.sbuf_tensor("ones_row_b", [1, 128], BF16))
    ones_col_b = ES.enter_context(nc.sbuf_tensor("ones_col_b", [128, 1], BF16))
    ones32 = ES.enter_context(nc.sbuf_tensor("ones32", [32, SL], F32))
    mask01 = ES.enter_context(nc.sbuf_tensor("mask01", [128, 128], F32))
    maskneg8 = ES.enter_context(nc.sbuf_tensor("maskneg8", [128, 8 * 128], BF16))
    convw_sb = ES.enter_context(nc.sbuf_tensor("convw_sb", [128, 18 * 4], F32))
    convb_sb = ES.enter_context(nc.sbuf_tensor("convb_sb", [128, 18], F32))
    normw_sb = ES.enter_context(nc.sbuf_tensor("normw_sb", [128, 16], F32))
    dpar_sb = ES.enter_context(nc.sbuf_tensor("dpar_sb", [128, NHEADS], F32))
    dtb_sb = ES.enter_context(nc.sbuf_tensor("dtb_sb", [NHEADS, 1], F32))
    alog_sb = ES.enter_context(nc.sbuf_tensor("alog_sb", [NHEADS, 1], F32))
    acol_sb = ES.enter_context(nc.sbuf_tensor("acol_sb", [NHEADS, 1], F32))
    beta_sb = ES.enter_context(nc.sbuf_tensor("beta_sb", [128, NEXP], F32))
    gwT_sb = ES.enter_context(nc.sbuf_tensor("gwT_sb", [128, 8 * NEXP], BF16))
    flag_sb = ES.enter_context(nc.sbuf_tensor("flag_sb", [128, 1], F32))
    xn_fm = ES.enter_context(nc.sbuf_tensor("xn_fm", [128, 8 * NT_MAIN], BF16))
    c_tm = ES.enter_context(nc.sbuf_tensor("c_tm", [128, 16 * NEXP], F32))
    ps_big = ES.enter_context(nc.psum_tensor("ps_big", [128, 2048], F32))
    ps_e = ES.enter_context(nc.psum_tensor("ps_e", [128, 1024], F32))
    ps_aux = ES.enter_context(nc.psum_tensor("ps_aux", [128, 512], F32))
    ps_nrm = ES.enter_context(nc.psum_tensor("ps_nrm", [128, 512], F32))
    _dS = ES.enter_context(nc.semaphore("dS"))
    _dG = ES.enter_context(nc.semaphore("dG"))
    _dC = ES.enter_context(nc.semaphore("dC"))
    _dX0 = ES.enter_context(nc.semaphore("dX0"))
    _dX1 = ES.enter_context(nc.semaphore("dX1"))
    _dY = ES.enter_context(nc.semaphore("dY"))
    _dF0 = ES.enter_context(nc.semaphore("dF0"))
    _dF1 = ES.enter_context(nc.semaphore("dF1"))
    _dO0 = ES.enter_context(nc.semaphore("dO0"))
    _dO1 = ES.enter_context(nc.semaphore("dO1"))
    _dWc = ES.enter_context(nc.semaphore("dWc"))
    _dI0 = ES.enter_context(nc.semaphore("dI0"))
    _dI1 = ES.enter_context(nc.semaphore("dI1"))
    _dM0 = ES.enter_context(nc.semaphore("dM0"))
    _dM1 = ES.enter_context(nc.semaphore("dM1"))
    _dD = ES.enter_context(nc.semaphore("dD"))
    _dWo = ES.enter_context(nc.semaphore("dWo"))
    _sT = ES.enter_context(nc.semaphore("sT"))
    _sV = ES.enter_context(nc.semaphore("sV"))
    _sS = ES.enter_context(nc.semaphore("sS"))
    _sP = ES.enter_context(nc.semaphore("sP"))
    block = ES.enter_context(nc.Block())
    with ES:
     if True:
        dS, dG = Sem(_dS, "dS"), Sem(_dG, "dG")
        dC, dY, dWc, dD, dWo = Sem(_dC, "dC"), Sem(_dY, "dY"), Sem(_dWc, "dWc"), Sem(_dD, "dD"), Sem(_dWo, "dWo")
        dX = [Sem(_dX0, "dX0"), Sem(_dX1, "dX1")]
        dF = [Sem(_dF0, "dF0"), Sem(_dF1, "dF1")]
        dO = [Sem(_dO0, "dO0"), Sem(_dO1, "dO1")]
        dI = [Sem(_dI0, "dI0"), Sem(_dI1, "dI1")]
        dM = [Sem(_dM0, "dM0"), Sem(_dM1, "dM1")]
        sT, sV, sS, sP = Sem(_sT, "sT"), Sem(_sV, "sV"), Sem(_sS, "sS"), Sem(_sP, "sP")
        SEMS = {s.name: s for s in [dS, dG, dC, dY, dWc, dD, dWo] + dX + dF + dO + dI + dM + [sT, sV, sS, sP]}

        def tag(name, sem):
            tags[name] = (sem.name, C[sem.name])

        def wait(eng, name):
            s, v = tags[name]
            if v > 0:
                eng.wait_ge(SEMS[s].h, v)

        @block.sync
        def _(sync):
            for dst, src in [(convw_sb, d_convw), (convb_sb, d_convb), (normw_sb, d_normw),
                             (dpar_sb, d_dpar), (dtb_sb, d_dtb), (alog_sb, d_alog),
                             (beta_sb, d_beta), (flag_sb, d_flag)]:
                inst = sync.dma_start(out=dst[:], in_=src[:])
                dC.inc(inst, 16)
        tag("p0_dS", dC)

        @block.gpsimd
        def _(gpsimd):
            inst = gpsimd.dma_start(
                out=gwT_sb[:].rearrange("p (dc e) -> p dc e", dc=8),
                in_=d_gwT.rearrange("(dc p) e -> p dc e", p=128))
            dWc.inc(inst, 16)
            gpsimd.memset(ident_f[:], 0.0)
            gpsimd.memset(ident_b[:], 0.0)
            gpsimd.memset(mask01[:], 0.0)
            gpsimd.memset(maskneg8[:], 0.0)
            gpsimd.drain()
            gpsimd.affine_select(out=ident_f[:], in_=ident_f[:], compare_op=ALU.not_equal,
                                 fill=1.0, base=0, pattern=[[-1, 128]], channel_multiplier=1)
            gpsimd.affine_select(out=ident_b[:], in_=ident_b[:], compare_op=ALU.not_equal,
                                 fill=1.0, base=0, pattern=[[-1, 128]], channel_multiplier=1)
            gpsimd.affine_select(out=mask01[:], in_=mask01[:], compare_op=ALU.is_gt,
                                 fill=1.0, base=0, pattern=[[-1, 128]], channel_multiplier=1)
            gpsimd.drain()
            for g8 in range(8):
                gpsimd.affine_select(out=maskneg8[:, g8 * 128:(g8 + 1) * 128],
                                     in_=maskneg8[:, g8 * 128:(g8 + 1) * 128],
                                     compare_op=ALU.is_ge, fill=-1e30, base=0,
                                     pattern=[[1, 128]], channel_multiplier=-1)
            gpsimd.drain()
            inst = gpsimd.engine_nop()
            sP.inc(inst)
        tag("p0_consts", sP)
        tag("p0_dG", dWc)

        @block.vector
        def _(vector):
            vector.memset(ones_row[:], 1.0)
            vector.memset(eps_col[:], EPS)
            vector.memset(zero_row[:], 0.0)
            vector.memset(ones_row_b[:], 1.0)
            vector.memset(ones_col_b[:], 1.0)
            vector.memset(ones32[:], 1.0)
            inst = vector.drain()
            sV.inc(inst)
        tag("p0_v", sV)

        @block.scalar
        def _(scalar):
            wait(scalar, "p0_dS")
            inst = scalar.activation(acol_sb[:], alog_sb[:], AF.Exp)
            sS.inc(inst)
            scalar.drain()
        tag("p0_s", sS)

        @block.vector
        def _(vector):
            wait(vector, "p0_s")
            inst = vector.tensor_scalar_mul(acol_sb[:], acol_sb[:], -1.0)
            sV.inc(inst)
            vector.drain()
        tag("p0_acol", sV)

        ES1 = contextlib.ExitStack()
        sel32 = ES1.enter_context(nc.sbuf_tensor("sel32", [32, 32 * 128], F32))
        S_sb = ES1.enter_context(nc.sbuf_tensor("S_sb", [128, D_INNER], BF16))
        S_t1 = ES1.enter_context(nc.sbuf_tensor("S_t1", [128, D_INNER], BF16))
        Pa = ES1.enter_context(nc.sbuf_tensor("Pa_sl", [32, SL + 1], F32))
        nPa = ES1.enter_context(nc.sbuf_tensor("nPa_sl", [32, SL + 1], F32))
        x_tile = ES1.enter_context(nc.sbuf_tensor("x_tile", [128, 2 * D_MODEL], F32))
        xn_tm = ES1.enter_context(nc.sbuf_tensor("xn_tm", [128, 2 * D_MODEL], F32))
        lnst = ES1.enter_context(nc.sbuf_tensor("lnst", [128, 16], F32))
        zs = ES1.enter_context(nc.sbuf_tensor("zs", [128, 16 * SL], BF16))
        xbc = ES1.enter_context(nc.sbuf_tensor("xbc", [128, 18 * (SL + 3)], BF16))
        xsilu = ES1.enter_context(nc.sbuf_tensor("xsilu", [128, 16 * SL], BF16))
        B_sl = ES1.enter_context(nc.sbuf_tensor("B_sl", [128, SL], BF16))
        C_sl = ES1.enter_context(nc.sbuf_tensor("C_sl", [128, SL], BF16))
        cva = ES1.enter_context(nc.sbuf_tensor("cva", [128, 2 * SL], BF16))
        cvb = ES1.enter_context(nc.sbuf_tensor("cvb", [128, 2 * SL], BF16))
        dts = ES1.enter_context(nc.sbuf_tensor("dts", [32, 4 * SL], F32))
        win_buf = ES1.enter_context(nc.sbuf_tensor("win_buf", [128, 2 * 1024], BF16))
        Xd = ES1.enter_context(nc.sbuf_tensor("Xd", [128, D_INNER], BF16))
        Xw = ES1.enter_context(nc.sbuf_tensor("Xw", [128, D_INNER], BF16))
        XDt = ES1.enter_context(nc.sbuf_tensor("XDt", [128, D_INNER], BF16))
        small_tm = ES1.enter_context(nc.sbuf_tensor("small_tm", [128, 128], F32))
        rr_hm = ES1.enter_context(nc.sbuf_tensor("rr_hm", [32, 128], F32))
        npp_rows = ES1.enter_context(nc.sbuf_tensor("npp_rows", [32, 128], F32))
        dec_sm = ES1.enter_context(nc.sbuf_tensor("dec_sm", [32, 8], F32))
        decrow = ES1.enter_context(nc.sbuf_tensor("decrow", [4, 32], F32))
        B_tm = ES1.enter_context(nc.sbuf_tensor("B_tm", [128, 128], BF16))
        Gt_m = ES1.enter_context(nc.sbuf_tensor("Gt_m", [128, 128], BF16))
        Eex = ES1.enter_context(nc.sbuf_tensor("Eex", [128, 2 * 1024], BF16))
        Mt = ES1.enter_context(nc.sbuf_tensor("Mt", [128, 2 * 1024], BF16))
        Mts = ES1.enter_context(nc.sbuf_tensor("Mts", [128, 2 * 1024], BF16))
        y2 = ES1.enter_context(nc.sbuf_tensor("y2", [128, D_INNER], BF16))
        y3 = ES1.enter_context(nc.sbuf_tensor("y3", [128, D_INNER], BF16))
        y4asm = ES1.enter_context(nc.sbuf_tensor("y4asm", [128, D_INNER], BF16))
        nrm_sm = ES1.enter_context(nc.sbuf_tensor("nrm_sm", [128, 8], F32))
        rrow = ES1.enter_context(nc.sbuf_tensor("rrow", [1, 128], F32))
        with ES1:
         if True:
            @block.gpsimd
            def _(gpsimd):
                wait(gpsimd, "p0_consts")
                gpsimd.memset(sel32[:], 0.0)
                gpsimd.drain()
                gpsimd.affine_select(out=sel32[:].rearrange("k (h m) -> k h m", h=32),
                                     in_=sel32[:].rearrange("k (h m) -> k h m", h=32),
                                     compare_op=ALU.not_equal, fill=1.0, base=0,
                                     pattern=[[-1, 32], [0, 128]], channel_multiplier=1)
                gpsimd.drain()
                inst = gpsimd.engine_nop()
                sP.inc(inst)
            tag("sel_done", sP)

            @block.vector
            def _(vector):
                for cc in range(18):
                    vector.memset(xbc[:, cc * (SL + 3):cc * (SL + 3) + 3], 0.0)
                vector.memset(S_sb[:], 0.0)
                vector.memset(Pa[:, 0:1], 0.0)
                vector.memset(nPa[:, 0:1], 0.0)
                inst = vector.drain()
                sV.inc(inst)
            tag("tails_s0", sV)

            def xbc_t(cc):
                return xbc[:, cc * (SL + 3):(cc + 1) * (SL + 3)]

            ipg = 0
            n_slices = n_sl_pref + n_sl_main
            NTT = SL // 128

            for sl_i in range(n_slices):
                is_pref = sl_i < n_sl_pref
                goff = sl_i * SL
                moff = goff - NT_PREF
                x_src = d_xp if is_pref else d_xm
                x_off = goff if is_pref else moff
                P_ = f"s{sl_i}_"
                prev_last = goff // Q - 1

                def xn_cols(dc, c0, cn, is_pref=is_pref, moff=moff):
                    if is_pref:
                        return zs[:, dc * SL + c0: dc * SL + c0 + cn]
                    return xn_fm[:, dc * NT_MAIN + moff + c0: dc * NT_MAIN + moff + c0 + cn]

                for tt in range(NTT):
                    par = tt % 2
                    xt = x_tile[:, par * D_MODEL:(par + 1) * D_MODEL]
                    xnt = xn_tm[:, par * D_MODEL:(par + 1) * D_MODEL]
                    ls = par * 8
                    gtt = sl_i * NTT + tt

                    @block.sync
                    def _(sync, tt=tt, xt=xt, x_src=x_src, x_off=x_off, P_=P_, gtt=gtt):
                        if gtt >= 2:
                            wait(sync, f"g{gtt - 2}_lnx")
                        inst = sync.dma_start(out=xt[:],
                                              in_=x_src[x_off + tt * 128: x_off + (tt + 1) * 128, :])
                        dX[gtt % 2].inc(inst, 16)
                    tag(f"g{gtt}_xld", dX[gtt % 2])

                    @block.vector
                    def _(vector, xt=xt, ls=ls, gtt=gtt):
                        wait(vector, f"g{gtt}_xld")
                        vector.reduce_sum(lnst[:, ls:ls + 1], xt[:], axis=AX)
                        vector.drain()
                        inst = vector.tensor_scalar_mul(lnst[:, ls + 1:ls + 2], lnst[:, ls:ls + 1], 1.0 / D_MODEL)
                        sV.inc(inst)
                        vector.drain()
                    tag(f"g{gtt}_mean", sV)

                    @block.scalar
                    def _(scalar, xt=xt, ls=ls, gtt=gtt):
                        wait(scalar, f"g{gtt}_xld")
                        if gtt >= 2:
                            wait(scalar, f"g{gtt - 2}_var")
                        inst = scalar.activation(xn_tm[:, (ls // 8) * D_MODEL:(ls // 8 + 1) * D_MODEL],
                                                 xt[:], AF.Square, accum_out=lnst[:, ls + 2:ls + 3])
                        sS.inc(inst)
                        scalar.drain()
                    tag(f"g{gtt}_sumsq", sS)

                    @block.vector
                    def _(vector, ls=ls, gtt=gtt):
                        wait(vector, f"g{gtt}_sumsq")
                        wait(vector, f"g{gtt}_mean")
                        vector.tensor_scalar(lnst[:, ls + 3:ls + 4], lnst[:, ls + 1:ls + 2],
                                             lnst[:, ls + 1:ls + 2], None, ALU.mult)
                        vector.drain()
                        inst = vector.scalar_tensor_tensor(lnst[:, ls + 4:ls + 5], lnst[:, ls + 2:ls + 3],
                                                           1.0 / D_MODEL, lnst[:, ls + 3:ls + 4],
                                                           ALU.mult, ALU.subtract)
                        sV.inc(inst)
                        vector.drain()
                    tag(f"g{gtt}_var", sV)

                    @block.scalar
                    def _(scalar, ls=ls, gtt=gtt):
                        wait(scalar, f"g{gtt}_var")
                        scalar.activation(lnst[:, ls + 5:ls + 6], lnst[:, ls + 4:ls + 5], AF.Ln, bias=eps_col[:])
                        scalar.drain()
                        inst = scalar.activation(lnst[:, ls + 6:ls + 7], lnst[:, ls + 5:ls + 6], AF.Exp, scale=-0.5)
                        sS.inc(inst)
                        scalar.drain()
                    tag(f"g{gtt}_sd", sS)

                    @block.vector
                    def _(vector, xt=xt, xnt=xnt, ls=ls, gtt=gtt):
                        wait(vector, f"g{gtt}_sd")
                        if gtt >= 2:
                            wait(vector, f"g{gtt - 2}_xntr")
                        inst = vector.tensor_scalar(xnt[:], xt[:], lnst[:, ls + 1:ls + 2],
                                                    lnst[:, ls + 6:ls + 7], ALU.subtract, ALU.mult)
                        sV.inc(inst)
                        vector.drain()
                    tag(f"g{gtt}_lnx", sV)

                    @block.tensor
                    def _(tensor, xnt=xnt, gtt=gtt):
                        wait(tensor, f"g{gtt}_lnx")
                        wait(tensor, "p0_consts")
                        if gtt >= 1:
                            wait(tensor, f"g{gtt - 1}_xnev")
                        for dc in range(8):
                            inst = tensor.transpose(ps_e[:, dc * 128:(dc + 1) * 128],
                                                    xnt[:, dc * 128:(dc + 1) * 128], ident_f[:])
                        sT.inc(inst)
                    tag(f"g{gtt}_xntr", sT)

                    @block.scalar
                    def _(scalar, tt=tt, gtt=gtt, is_pref=is_pref, moff=moff):
                        wait(scalar, f"g{gtt}_xntr")
                        if is_pref:
                            dst = zs[:].rearrange("p (dc t) -> p dc t", dc=16)[:, 0:8, tt * 128:(tt + 1) * 128]
                        else:
                            dst = xn_fm[:].rearrange("p (dc t) -> p dc t", dc=8)[:, :, moff + tt * 128: moff + (tt + 1) * 128]
                        inst = scalar.copy(dst, ps_e[:].rearrange("p (dc i) -> p dc i", dc=8))
                        sS.inc(inst)
                        scalar.drain()
                    tag(f"g{gtt}_xnev", sS)
                tag(f"{P_}xn_done", sS)

                if sl_i > 0:
                    @block.vector
                    def _(vector):
                        wait(vector, f"s{sl_i - 1}_convdone")
                        for cc in range(18):
                            vector.tensor_copy(xbc_t(cc)[:, 0:3], xbc_t(cc)[:, SL:SL + 3])
                        inst = vector.drain()
                        sV.inc(inst)
                    tag(f"{P_}tails", sV)
                else:
                    tags[f"{P_}tails"] = tags["tails_s0"]

                oc_list = list(range(OC_X0, OC_DT + 1)) if is_pref else list(range(35))
                for j_loc, oc in enumerate(oc_list):
                    par = ipg % 2
                    wb = win_buf[:, par * 1024:(par + 1) * 1024]
                    psd = ps_aux if par == 0 else ps_nrm
                    jg = ipg

                    @block.gpsimd
                    def _(gpsimd, oc=oc, wb=wb, jg=jg):
                        if jg >= 2:
                            wait(gpsimd, f"ip{jg - 2}_mm")
                        ncols = 128 if oc != OC_DT else 32
                        inst = gpsimd.dma_start(
                            out=wb[:].rearrange("p (ic o) -> p ic o", ic=8)[:, :, 0:ncols],
                            in_=d_winT.rearrange("(ic p) o -> p ic o", p=128)[:, :, oc * 128: oc * 128 + ncols])
                        dI[jg % 2].inc(inst, 16)
                    tag(f"ip{jg}_w", dI[jg % 2])

                    @block.tensor
                    def _(tensor, oc=oc, wb=wb, psd=psd, jg=jg, j_loc=j_loc, P_=P_,
                          prev_last=prev_last, sl_i=sl_i):
                        wait(tensor, f"ip{jg}_w")
                        wait(tensor, f"{P_}xn_done")
                        if jg >= 2:
                            wait(tensor, f"ip{jg - 2}_ev")
                        if j_loc < 2 and prev_last >= 0:
                            wait(tensor, f"c{prev_last}_sup")
                            if sl_i > n_sl_pref:
                                wait(tensor, f"c{prev_last}_psnrm_free")
                        ncols = 128 if oc != OC_DT else 32
                        for ic in range(8):
                            inst = tensor.matmul(psd[0:ncols, :],
                                                 wb[:].rearrange("p (ic o) -> p ic o", ic=8)[:, ic, 0:ncols],
                                                 xn_cols(ic, 0, SL), start=(ic == 0), stop=(ic == 7))
                        sT.inc(inst)
                    tag(f"ip{jg}_mm", sT)

                    if oc < OC_X0:
                        @block.scalar
                        def _(scalar, oc=oc, psd=psd, jg=jg):
                            wait(scalar, f"ip{jg}_mm")
                            inst = scalar.activation(zs[:, oc * SL:(oc + 1) * SL], psd[:], AF.Silu)
                            sS.inc(inst)
                            scalar.drain()
                        tag(f"ip{jg}_ev", sS)
                    elif oc == OC_DT:
                        @block.vector
                        def _(vector, psd=psd, jg=jg):
                            wait(vector, f"ip{jg}_mm")
                            inst = vector.tensor_copy(dts[:, 0:SL], psd[0:32, :])
                            sV.inc(inst)
                            vector.drain()
                        tag(f"ip{jg}_ev", sV)
                    else:
                        cc = oc - OC_X0

                        @block.scalar
                        def _(scalar, cc=cc, psd=psd, jg=jg, P_=P_):
                            wait(scalar, f"ip{jg}_mm")
                            wait(scalar, f"{P_}tails")
                            inst = scalar.copy(xbc_t(cc)[:, 3:3 + SL], psd[:])
                            sS.inc(inst)
                            scalar.drain()
                        tag(f"ip{jg}_ev", sS)
                    H[f"{P_}ip_{oc}"] = f"ip{jg}_ev"
                    ipg += 1

                cc_list = list(range(17)) if is_pref else list(range(18))
                for cc in cc_list:
                    vpar = cc % 2
                    cvx = cva if vpar == 0 else cvb

                    def conv_body(eng, sem, stt_ok, cc=cc, P_=P_, sl_i=sl_i, cvx=cvx, vpar=vpar):
                        wait(eng, H[f"{P_}ip_{OC_X0 + cc}"])
                        if cc > 1:
                            wait(eng, f"{P_}cvs{cc - 2}")
                        elif sl_i > 0:
                            lastcc = (16 - vpar) if sl_i <= n_sl_pref else (16 + vpar)
                            wait(eng, f"s{sl_i - 1}_cvs{lastcc}")
                        srcp = xbc_t(cc)
                        w = lambda k: convw_sb[:, cc * 4 + k: cc * 4 + k + 1]
                        if stt_ok:
                            eng.tensor_scalar(cvx[:, 0:SL], srcp[:, 0:SL], w(0), None, ALU.mult)
                            eng.drain()
                            eng.scalar_tensor_tensor(cvx[:, SL:2 * SL], srcp[:, 1:1 + SL], w(1), cvx[:, 0:SL], ALU.mult, ALU.add)
                            eng.drain()
                            eng.scalar_tensor_tensor(cvx[:, 0:SL], srcp[:, 2:2 + SL], w(2), cvx[:, SL:2 * SL], ALU.mult, ALU.add)
                            eng.drain()
                            inst = eng.scalar_tensor_tensor(cvx[:, SL:2 * SL], srcp[:, 3:3 + SL], w(3), cvx[:, 0:SL], ALU.mult, ALU.add)
                            sem.inc(inst)
                            eng.drain()
                        else:
                            eng.tensor_scalar(cvx[:, SL:2 * SL], srcp[:, 0:SL], w(0), None, ALU.mult)
                            eng.drain()
                            for k in (1, 2, 3):
                                eng.tensor_scalar(cvx[:, 0:SL], srcp[:, k:k + SL], w(k), None, ALU.mult)
                                eng.drain()
                                inst = eng.tensor_tensor(cvx[:, SL:2 * SL], cvx[:, SL:2 * SL], cvx[:, 0:SL], ALU.add)
                                eng.drain()
                            sem.inc(inst)
                            eng.drain()

                    @block.vector
                    def _(vector, conv_body=conv_body):
                        conv_body(vector, sV, True)
                    tag(f"{P_}cv{cc}", sV)

                    @block.scalar
                    def _(scalar, cc=cc, P_=P_, prev_last=prev_last, cvx=cvx):
                        wait(scalar, f"{P_}cv{cc}")
                        if prev_last >= 0:
                            wait(scalar, f"c{prev_last}_sup")
                        if cc < 16:
                            dst = xsilu[:, cc * SL:(cc + 1) * SL]
                        elif cc == 16:
                            dst = B_sl[:]
                        else:
                            dst = C_sl[:]
                        inst = scalar.activation(dst, cvx[:, SL:2 * SL], AF.Silu, bias=convb_sb[:, cc:cc + 1])
                        sS.inc(inst)
                        scalar.drain()
                    tag(f"{P_}cvs{cc}", sS)
                tag(f"{P_}convdone", sS)

                @block.vector
                def _(vector, P_=P_):
                    wait(vector, H[f"{P_}ip_{OC_DT}"])
                    wait(vector, "p0_acol")
                    inst = vector.tensor_scalar_add(dts[:, SL:2 * SL], dts[:, 0:SL], dtb_sb[:])
                    sV.inc(inst)
                    vector.drain()
                tag(f"{P_}dt0", sV)

                @block.scalar
                def _(scalar, P_=P_):
                    wait(scalar, f"{P_}dt0")
                    scalar.activation(dts[:, 2 * SL:3 * SL], dts[:, SL:2 * SL], AF.Abs)
                    scalar.drain()
                    inst = scalar.activation(dts[:, 2 * SL:3 * SL], dts[:, 2 * SL:3 * SL], AF.Exp, scale=-1.0)
                    sS.inc(inst)
                    scalar.drain()
                tag(f"{P_}dt1", sS)

                @block.vector
                def _(vector, P_=P_):
                    wait(vector, f"{P_}dt1")
                    inst = vector.tensor_scalar_add(dts[:, 2 * SL:3 * SL], dts[:, 2 * SL:3 * SL], 1.0)
                    sV.inc(inst)
                    vector.drain()
                tag(f"{P_}dt2", sV)

                @block.scalar
                def _(scalar, P_=P_):
                    wait(scalar, f"{P_}dt2")
                    inst = scalar.activation(dts[:, 2 * SL:3 * SL], dts[:, 2 * SL:3 * SL], AF.Ln)
                    sS.inc(inst)
                    scalar.drain()
                tag(f"{P_}dt3", sS)

                @block.vector
                def _(vector, P_=P_, sl_i=sl_i, prev_last=prev_last):
                    wait(vector, f"{P_}dt3")
                    if prev_last >= 0:
                        wait(vector, f"c{prev_last}_sup")
                    if sl_i > 0:
                        vector.tensor_copy(Pa[:, 0:1], Pa[:, SL:SL + 1])
                        vector.drain()
                    vector.tensor_scalar_max(dts[:, 3 * SL:4 * SL], dts[:, SL:2 * SL], 0.0)
                    vector.drain()
                    vector.tensor_add(dts[:, 0:SL], dts[:, 3 * SL:4 * SL], dts[:, 2 * SL:3 * SL])
                    vector.drain()
                    vector.tensor_scalar_mul(dts[:, 3 * SL:4 * SL], dts[:, 0:SL], acol_sb[:])
                    vector.drain()
                    vector.tensor_tensor_scan(Pa[:, 1:1 + SL], ones32[:],
                                              dts[:, 3 * SL:4 * SL], Pa[:, 0:1],
                                              ALU.mult, ALU.add)
                    vector.drain()
                    inst = vector.tensor_scalar_mul(nPa[:, 0:1 + SL], Pa[:, 0:1 + SL], -1.0)
                    sV.inc(inst)
                    vector.drain()
                tag(f"{P_}pa", sV)

                @block.vector
                def _(vector, P_=P_):
                    wait(vector, f"{P_}pa")
                    pe = bass.AP(tensor=Pa[:].tensor, offset=128, ap=[[SL + 1, 32], [128, SL // 128]])
                    pb = bass.AP(tensor=Pa[:].tensor, offset=0, ap=[[SL + 1, 32], [128, SL // 128]])
                    inst = vector.tensor_tensor(dec_sm[:, 0:SL // 128], pe, pb, ALU.subtract)
                    sV.inc(inst)
                    vector.drain()
                tag(f"{P_}dec0", sV)

                @block.scalar
                def _(scalar, P_=P_):
                    wait(scalar, f"{P_}dec0")
                    inst = scalar.activation(dec_sm[:, 4:4 + SL // 128], dec_sm[:, 0:SL // 128], AF.Exp)
                    sS.inc(inst)
                    scalar.drain()
                tag(f"{P_}dec1", sS)

                @block.tensor
                def _(tensor, P_=P_):
                    wait(tensor, f"{P_}dec1")
                    if goff > 0:
                        wait(tensor, f"c{goff // Q - 1}_sup")
                    inst = tensor.transpose(ps_aux[0:SL // 128, 448:480], dec_sm[:, 4:4 + SL // 128],
                                            ident_f[0:32, 0:32])
                    sT.inc(inst)
                tag(f"{P_}dectr", sT)

                @block.vector
                def _(vector, P_=P_):
                    wait(vector, f"{P_}dectr")
                    inst = vector.tensor_copy(decrow[0:SL // 128, :], ps_aux[0:SL // 128, 448:480])
                    sV.inc(inst)
                    vector.drain()
                tag(f"{P_}decrows", sV)

                for cl in range(SL // Q):
                    gch = goff // Q + cl
                    col = cl * Q
                    CP = f"c{gch}_"
                    pcol = cl * Q

                    @block.tensor
                    def _(tensor, col=col, CP=CP, P_=P_, gch=gch):
                        wait(tensor, f"{P_}convdone")
                        wait(tensor, f"{P_}pa")
                        if gch > 0:
                            wait(tensor, f"c{gch - 1}_psbig_free")
                        psbig_bf = ps_big[:].bitcast(BF16)
                        aux_bf = ps_aux[:].bitcast(BF16)
                        for cc in range(16):
                            tensor.transpose(psbig_bf[:, cc * 128:(cc + 1) * 128],
                                             xsilu[:, cc * SL + col: cc * SL + col + 128], ident_b[:])
                        tensor.transpose(ps_aux[:, 0:32], dts[:, col:col + 128], ident_f[0:32, 0:32])
                        inst = tensor.transpose(aux_bf[:, 128:256], B_sl[:, col:col + 128], ident_b[:])
                        sT.inc(inst)
                    tag(f"{CP}xtr", sT)

                    @block.scalar
                    def _(scalar, CP=CP, P_=P_, pcol=pcol, gch=gch):
                        wait(scalar, f"{P_}pa")
                        if gch > 0:
                            wait(scalar, f"c{gch - 1}_rrtr")
                        inst = scalar.activation(rr_hm[:], Pa[:, pcol + 1:pcol + 129], AF.Exp,
                                                 scale=-1.0, bias=Pa[:, pcol + 128:pcol + 129])
                        sS.inc(inst)
                        scalar.drain()
                    tag(f"{CP}rr", sS)

                    @block.vector
                    def _(vector, CP=CP):
                        wait(vector, f"{CP}xtr")
                        inst = vector.tensor_copy(small_tm[:, 0:32], ps_aux[:, 0:32])
                        sV.inc(inst)
                        vector.drain()
                    tag(f"{CP}dttm", sV)

                    @block.tensor
                    def _(tensor, CP=CP):
                        wait(tensor, f"{CP}rr")
                        wait(tensor, f"{CP}dttm")
                        inst = tensor.transpose(ps_aux[:, 32:64], rr_hm[:], ident_f[0:32, 0:32])
                        sT.inc(inst)
                    tag(f"{CP}rrtr", sT)

                    @block.vector
                    def _(vector, CP=CP):
                        wait(vector, f"{CP}rrtr")
                        vector.tensor_copy(small_tm[:, 32:64], ps_aux[:, 32:64])
                        vector.drain()
                        inst = vector.tensor_copy(B_tm[:], ps_aux[:].bitcast(BF16)[:, 128:256])
                        sV.inc(inst)
                        vector.drain()
                    tag(f"{CP}rrtm", sV)

                    @block.vector
                    def _(vector, CP=CP):
                        wait(vector, f"{CP}rrtm")
                        dt_rep = small_tm[:, 0:32].unsqueeze(2).broadcast_to([128, 32, 64])
                        rr_rep = small_tm[:, 32:64].unsqueeze(2).broadcast_to([128, 32, 64])
                        xv = lambda t: t[:].rearrange("p (h e) -> p h e", h=32)
                        psbig_bf = ps_big[:].bitcast(BF16)[:, 0:2048]
                        vector.tensor_tensor(xv(Xd), psbig_bf.rearrange("p (h e) -> p h e", h=32),
                                             dt_rep, ALU.mult)
                        vector.drain()
                        vector.tensor_tensor(xv(Xw), xv(Xd), rr_rep, ALU.mult)
                        vector.drain()
                        vector.reciprocal(small_tm[:, 96:128], small_tm[:, 0:32])
                        vector.drain()
                        vector.tensor_tensor(small_tm[:, 64:96], dpar_sb[:], small_tm[:, 96:128], ALU.mult)
                        vector.drain()
                        ddt_rep = small_tm[:, 64:96].unsqueeze(2).broadcast_to([128, 32, 64])
                        inst = vector.tensor_tensor(xv(XDt), xv(Xd), ddt_rep, ALU.mult)
                        sV.inc(inst)
                        vector.drain()
                    tag(f"{CP}xd", sV)

                    if not is_pref:
                        @block.tensor
                        def _(tensor, col=col, CP=CP, P_=P_):
                            wait(tensor, f"{CP}rrtm")
                            inst = tensor.matmul(ps_aux[:, 256:384], B_sl[:, col:col + 128],
                                                 C_sl[:, col:col + 128], start=True, stop=True)
                            sT.inc(inst)
                        tag(f"{CP}gt", sT)

                        @block.vector
                        def _(vector, CP=CP):
                            wait(vector, f"{CP}gt")
                            inst = vector.tensor_tensor(Gt_m[:], ps_aux[:, 256:384], mask01[:], ALU.mult)
                            sV.inc(inst)
                            vector.drain()
                        tag(f"{CP}gtm", sV)

                        @block.vector
                        def _(vector, CP=CP, pcol=pcol):
                            inst = vector.tensor_scalar_mul(npp_rows[:], ones32[:, 0:128], nPa[:, pcol:pcol + 1])
                            sV.inc(inst)
                            vector.drain()
                        tag(f"{CP}npp", sV)

                        for grp in range(4):
                            gpar = grp % 2

                            @block.tensor
                            def _(tensor, grp=grp, CP=CP, pcol=pcol, gch=gch):
                                wait(tensor, f"{CP}gtm" if grp == 0 else f"{CP}mt{grp - 1}")
                                if grp == 0 and gch == nch_pref:
                                    wait(tensor, "sel_done")
                                tensor.matmul(ps_e[:, 0:512], ident_b[:], maskneg8[:, 0:512],
                                              start=True, stop=False, skip_group_check=True)
                                tensor.matmul(ps_e[:, 512:1024], ident_b[:], maskneg8[:, 512:1024],
                                              start=True, stop=False, skip_group_check=True)
                                for hh in range(8):
                                    h = grp * 8 + hh
                                    sl8 = ps_e[:, hh * 128:(hh + 1) * 128]
                                    selh = sel32[:, h * 128:(h + 1) * 128]
                                    tensor.matmul(sl8, selh, Pa[0:32, pcol + 1:pcol + 129],
                                                  start=False, stop=False, skip_group_check=True)
                                    inst = tensor.matmul(sl8, nPa[0:32, pcol + 1:pcol + 129], selh,
                                                         start=False, stop=(hh == 7), skip_group_check=True)
                                sT.inc(inst)
                            tag(f"{CP}emm{grp}", sT)

                            @block.scalar
                            def _(scalar, grp=grp, gpar=gpar, CP=CP):
                                wait(scalar, f"{CP}emm{grp}")
                                if grp >= 2:
                                    wait(scalar, f"{CP}mt{grp - 2}")
                                inst = scalar.activation(Eex[:, gpar * 1024:(gpar + 1) * 1024], ps_e[:], AF.Exp)
                                sS.inc(inst)
                                scalar.drain()
                            tag(f"{CP}eex{grp}", sS)

                            @block.vector
                            def _(vector, grp=grp, gpar=gpar, CP=CP):
                                wait(vector, f"{CP}eex{grp}")
                                gt_rep = Gt_m[:].unsqueeze(1).broadcast_to([128, 8, 128])
                                inst = vector.tensor_tensor(
                                    Mt[:, gpar * 1024:(gpar + 1) * 1024].rearrange("p (h i) -> p h i", h=8),
                                    Eex[:, gpar * 1024:(gpar + 1) * 1024].rearrange("p (h i) -> p h i", h=8),
                                    gt_rep, ALU.mult)
                                sV.inc(inst)
                                vector.drain()
                            tag(f"{CP}mt{grp}", sV)

                            @block.tensor
                            def _(tensor, grp=grp, gpar=gpar, CP=CP):
                                wait(tensor, f"{CP}mt{grp}")
                                if grp == 0:
                                    wait(tensor, f"{CP}xd")
                                    for b4 in range(4):
                                        tensor.matmul(ps_big[:, b4 * 512:(b4 + 1) * 512], ones_row_b[:],
                                                      zero_row[:], start=True, stop=False,
                                                      skip_group_check=True)
                                for hh in range(8):
                                    h = grp * 8 + hh
                                    inst = tensor.matmul(ps_big[:, h * 64:(h + 1) * 64],
                                                         Mt[:, gpar * 1024 + hh * 128: gpar * 1024 + (hh + 1) * 128],
                                                         Xd[:, h * 64:(h + 1) * 64],
                                                         start=False, stop=False, skip_group_check=True)
                                sT.inc(inst)
                            tag(f"{CP}yi{grp}", sT)

                        for grp in range(4):
                            gpar = grp % 2

                            @block.tensor
                            def _(tensor, grp=grp, CP=CP, pcol=pcol):
                                wait(tensor, f"{CP}yi3" if grp == 0 else f"{CP}mts{grp - 1}")
                                if grp == 0:
                                    wait(tensor, f"{CP}npp")
                                for hh in range(8):
                                    h = grp * 8 + hh
                                    sl8 = ps_e[:, hh * 128:(hh + 1) * 128]
                                    selh = sel32[:, h * 128:(h + 1) * 128]
                                    tensor.matmul(sl8, selh, Pa[0:32, pcol + 1:pcol + 129],
                                                  start=True, stop=False, skip_group_check=True)
                                    inst = tensor.matmul(sl8, npp_rows[:], selh,
                                                         start=False, stop=True, skip_group_check=True)
                                sT.inc(inst)
                            tag(f"{CP}esmm{grp}", sT)

                            @block.scalar
                            def _(scalar, grp=grp, gpar=gpar, CP=CP):
                                wait(scalar, f"{CP}esmm{grp}")
                                wait(scalar, f"{CP}mts{grp - 2}" if grp >= 2 else f"{CP}mt{2 + grp}")
                                inst = scalar.activation(Eex[:, gpar * 1024:(gpar + 1) * 1024], ps_e[:], AF.Exp)
                                sS.inc(inst)
                                scalar.drain()
                            tag(f"{CP}esex{grp}", sS)

                            @block.vector
                            def _(vector, grp=grp, gpar=gpar, CP=CP, col=col):
                                wait(vector, f"{CP}esex{grp}")
                                c_rep = C_sl[:, col:col + 128].unsqueeze(1).broadcast_to([128, 8, 128])
                                inst = vector.tensor_tensor(
                                    Mts[:, gpar * 1024:(gpar + 1) * 1024].rearrange("p (h i) -> p h i", h=8),
                                    Eex[:, gpar * 1024:(gpar + 1) * 1024].rearrange("p (h i) -> p h i", h=8),
                                    c_rep, ALU.mult)
                                sV.inc(inst)
                                vector.drain()
                            tag(f"{CP}mts{grp}", sV)

                            @block.tensor
                            def _(tensor, grp=grp, gpar=gpar, CP=CP):
                                wait(tensor, f"{CP}mts{grp}")
                                for hh in range(8):
                                    h = grp * 8 + hh
                                    inst = tensor.matmul(ps_big[:, h * 64:(h + 1) * 64],
                                                         Mts[:, gpar * 1024 + hh * 128: gpar * 1024 + (hh + 1) * 128],
                                                         S_sb[:, h * 64:(h + 1) * 64],
                                                         start=False, stop=(hh == 7), skip_group_check=True)
                                sT.inc(inst)
                            tag(f"{CP}ys{grp}", sT)

                        @block.vector
                        def _(vector, CP=CP):
                            wait(vector, f"{CP}ys3")
                            inst = vector.tensor_tensor(y2[:], ps_big[:], XDt[:], ALU.add)
                            sV.inc(inst)
                            vector.drain()
                        tag(f"{CP}y2", sV)

                        @block.tensor
                        def _(tensor, CP=CP):
                            wait(tensor, f"{CP}y2")
                            psbig_bf = ps_big[:].bitcast(BF16)
                            for ec in range(16):
                                inst = tensor.transpose(psbig_bf[:, ec * 128:(ec + 1) * 128],
                                                        y2[:, ec * 128:(ec + 1) * 128], ident_b[:])
                            sT.inc(inst)
                        tag(f"{CP}y2tr", sT)

                        @block.vector
                        def _(vector, CP=CP, col=col, P_=P_):
                            wait(vector, f"{CP}y2tr")
                            psbig_bf = ps_big[:].bitcast(BF16)[:, 0:2048]
                            inst = vector.tensor_tensor(
                                y3[:].rearrange("p (ec i) -> p ec i", ec=16),
                                psbig_bf.rearrange("p (ec i) -> p ec i", ec=16),
                                zs[:].rearrange("p (ec t) -> p ec t", ec=16)[:, :, col:col + 128],
                                ALU.mult)
                            sV.inc(inst)
                            vector.drain()
                        tag(f"{CP}y3", sV)

                        @block.scalar
                        def _(scalar, CP=CP):
                            wait(scalar, f"{CP}y3")
                            inst = scalar.activation(y2[:], y3[:], AF.Square)
                            sS.inc(inst)
                            scalar.drain()
                        tag(f"{CP}sqy", sS)

                        @block.tensor
                        def _(tensor, CP=CP, gch=gch):
                            wait(tensor, f"{CP}sqy")
                            if gch > 0 and gch - 1 >= nch_pref:
                                wait(tensor, f"c{gch - 1}_psnrm_free")
                            for ec in range(16):
                                inst = tensor.matmul(ps_nrm[:, 0:1], y2[:, ec * 128:(ec + 1) * 128],
                                                     ones_col_b[:], start=(ec == 0), stop=(ec == 15))
                            sT.inc(inst)
                        tag(f"{CP}ms", sT)

                        @block.vector
                        def _(vector, CP=CP):
                            wait(vector, f"{CP}ms")
                            inst = vector.tensor_scalar(nrm_sm[:, 0:1], ps_nrm[:, 0:1], 1.0 / D_INNER,
                                                        EPS, ALU.mult, ALU.add)
                            sV.inc(inst)
                            vector.drain()
                        tag(f"{CP}mse", sV)

                        @block.scalar
                        def _(scalar, CP=CP):
                            wait(scalar, f"{CP}mse")
                            scalar.activation(nrm_sm[:, 1:2], nrm_sm[:, 0:1], AF.Ln)
                            scalar.drain()
                            inst = scalar.activation(nrm_sm[:, 2:3], nrm_sm[:, 1:2], AF.Exp, scale=-0.5)
                            sS.inc(inst)
                            scalar.drain()
                        tag(f"{CP}rstd", sS)

                        @block.tensor
                        def _(tensor, CP=CP):
                            wait(tensor, f"{CP}rstd")
                            inst = tensor.transpose(ps_nrm[0:1, 128:256], nrm_sm[:, 2:3], ident_f[:])
                            sT.inc(inst)
                        tag(f"{CP}rstdtr", sT)

                        @block.vector
                        def _(vector, CP=CP):
                            wait(vector, f"{CP}rstdtr")
                            inst = vector.tensor_copy(rrow[:], ps_nrm[0:1, 128:256])
                            sV.inc(inst)
                            vector.drain()
                        tag(f"{CP}rrow", sV)

                        @block.tensor
                        def _(tensor, CP=CP):
                            wait(tensor, f"{CP}rrow")
                            inst = tensor.matmul(ps_nrm[:, 256:384], ones_row[:], rrow[:],
                                                 start=True, stop=True)
                            sT.inc(inst)
                        tag(f"{CP}rb", sT)

                        @block.vector
                        def _(vector, CP=CP, gch=gch):
                            wait(vector, f"{CP}rb")
                            if gch - 1 >= nch_pref:
                                wait(vector, f"c{gch - 1}_y4out")
                            nv = lambda t: t.rearrange("p (ec i) -> p ec i", ec=16)
                            vector.tensor_tensor(nv(y4asm[:]), nv(y3[:]),
                                                 normw_sb[:].unsqueeze(2).broadcast_to([128, 16, 128]),
                                                 ALU.mult)
                            vector.drain()
                            inst = vector.tensor_tensor(nv(y4asm[:]), nv(y4asm[:]),
                                                        ps_nrm[:, 256:384].unsqueeze(1).broadcast_to([128, 16, 128]),
                                                        ALU.mult)
                            sV.inc(inst)
                            vector.drain()
                        tag(f"{CP}y4", sV)
                        tag(f"{CP}psnrm_free", sV)

                        @block.sync
                        def _(sync, CP=CP, gch=gch):
                            wait(sync, f"{CP}y4")
                            mch = gch - nch_pref
                            inst = sync.dma_start(out=d_y4.rearrange("(c p) f -> c p f", p=128)[mch],
                                                  in_=y4asm[:])
                            dY.inc(inst, 16)
                        tag(f"{CP}y4out", dY)

                        tag_y_done = f"{CP}y3"
                    else:
                        tag_y_done = f"{CP}xd"

                    @block.tensor
                    def _(tensor, CP=CP, tag_y_done=tag_y_done):
                        wait(tensor, tag_y_done)
                        wait(tensor, f"{CP}xd")
                        for j4 in range(4):
                            inst = tensor.matmul(ps_big[:, j4 * 512:(j4 + 1) * 512], B_tm[:],
                                                 Xw[:, j4 * 512:(j4 + 1) * 512], start=True, stop=True)
                        sT.inc(inst)
                    tag(f"{CP}sch", sT)

                    @block.tensor
                    def _(tensor, CP=CP, cl=cl, P_=P_):
                        wait(tensor, f"{P_}decrows")
                        wait(tensor, f"{CP}sch")
                        inst = tensor.matmul(ps_aux[:, 480:512], sel32[0:SL // 128, cl * 128:(cl + 1) * 128],
                                             decrow[0:SL // 128, :], start=True, stop=True)
                        sT.inc(inst)
                    tag(f"{CP}decb", sT)

                    @block.vector
                    def _(vector, CP=CP, gch=gch):
                        wait(vector, f"{CP}decb")
                        dec_rep = ps_aux[:, 480:512].unsqueeze(2).broadcast_to([128, 32, 64])
                        vector.tensor_tensor(S_t1[:].rearrange("p (h e) -> p h e", h=32),
                                             S_sb[:].rearrange("p (h e) -> p h e", h=32), dec_rep, ALU.mult)
                        vector.drain()
                        inst = vector.tensor_tensor(S_sb[:], S_t1[:], ps_big[:], ALU.add)
                        sV.inc(inst)
                        vector.drain()
                        if gch == nch_pref - 1:
                            inst = vector.tensor_scalar_mul(S_sb[:], S_sb[:], flag_sb[:])
                            sV.inc(inst)
                            vector.drain()
                    tag(f"{CP}sup", sV)
                    tag(f"{CP}psbig_free", sV)

            last_sup = f"c{NTOT // Q - 1}_sup"

        ESm = contextlib.ExitStack()
        ffn_acc = ESm.enter_context(nc.sbuf_tensor("ffn_acc", [128, 16 * D_MODEL], BF16))
        ES2 = contextlib.ExitStack()
        hbuf = ES2.enter_context(nc.sbuf_tensor("hbuf", [128, 16 * NT_MAIN], BF16))
        wd_sb = ES2.enter_context(nc.sbuf_tensor("wd_sb", [128, 16 * D_MODEL], BF16))
        wgu_buf = ES2.enter_context(nc.sbuf_tensor("wgu_buf", [128, 2 * 2048], BF16))
        sg_sb = ES2.enter_context(nc.sbuf_tensor("sg_sb", [128, 2 * 512], BF16))
        lgs = ES2.enter_context(nc.sbuf_tensor("lgs", [128, 40], F32))
        with ES2:
         if True:
            for tc in range(n_tc):
                @block.tensor
                def _(tensor, tc=tc):
                    wait(tensor, last_sup)
                    wait(tensor, "p0_dG")
                    if tc > 0:
                        wait(tensor, f"rt{tc - 1}_lg")
                    for dc in range(8):
                        inst = tensor.matmul(ps_aux[:, 0:8],
                                             xn_fm[:, dc * NT_MAIN + tc * 128: dc * NT_MAIN + (tc + 1) * 128],
                                             gwT_sb[:, dc * 8:(dc + 1) * 8],
                                             start=(dc == 0), stop=(dc == 7))
                    sT.inc(inst)
                tag(f"rt{tc}_mm", sT)

                @block.vector
                def _(vector, tc=tc):
                    wait(vector, f"rt{tc}_mm")
                    if tc > 0:
                        wait(vector, f"rt{tc - 1}_sg")
                    inst = vector.tensor_copy(lgs[:, 0:8], ps_aux[:, 0:8])
                    sV.inc(inst)
                    vector.drain()
                tag(f"rt{tc}_lg", sV)

                @block.scalar
                def _(scalar, tc=tc):
                    wait(scalar, f"rt{tc}_lg")
                    inst = scalar.activation(lgs[:, 8:16], lgs[:, 0:8], AF.Sigmoid)
                    sS.inc(inst)
                    scalar.drain()
                tag(f"rt{tc}_sg", sS)

                @block.vector
                def _(vector, tc=tc):
                    wait(vector, f"rt{tc}_sg")
                    cslot = c_tm[:, tc * 8:(tc + 1) * 8]
                    vector.tensor_tensor(cslot, lgs[:, 0:8], beta_sb[:], ALU.add)
                    vector.drain()
                    vector.reduce_max(lgs[:, 32:33], cslot, axis=AX)
                    vector.drain()
                    vector.tensor_scalar(lgs[:, 16:24], cslot, lgs[:, 32:33], None, ALU.is_ge)
                    vector.drain()
                    vector.scalar_tensor_tensor(lgs[:, 24:32], lgs[:, 16:24], -1e30, cslot, ALU.mult, ALU.add)
                    vector.drain()
                    vector.reduce_max(lgs[:, 33:34], lgs[:, 24:32], axis=AX)
                    vector.drain()
                    vector.tensor_scalar(lgs[:, 24:32], lgs[:, 24:32], lgs[:, 33:34], None, ALU.is_ge)
                    vector.drain()
                    vector.tensor_add(lgs[:, 16:24], lgs[:, 16:24], lgs[:, 24:32])
                    vector.drain()
                    inst = vector.tensor_tensor(cslot, lgs[:, 8:16], lgs[:, 16:24], ALU.mult)
                    sV.inc(inst)
                    vector.drain()
                tag(f"rt{tc}_c", sV)
            tag("router_done", sV)

            for e in range(NEXP + 1):
                is_shared = e == NEXP
                EP = f"e{e}_"
                for mc in range(16):
                    par = mc % 2

                    @block.gpsimd
                    def _(gpsimd, e=e, mc=mc, par=par, EP=EP):
                        if mc >= 2:
                            wait(gpsimd, f"{EP}gu_{mc - 2}_{n_tt - 1}")
                        elif e > 0:
                            wait(gpsimd, f"e{e - 1}_gu_{mc + 14}_{n_tt - 1}")
                        else:
                            wait(gpsimd, last_sup)
                        inst = gpsimd.dma_start(
                            out=wgu_buf[:, par * 2048: par * 2048 + 1024].rearrange("p (hc m) -> p hc m", hc=8),
                            in_=d_wg.rearrange("e (hc p) m -> e p hc m", p=128)[e][:, :, mc * 128:(mc + 1) * 128])
                        dM[mc % 2].inc(inst, 16)
                        inst = gpsimd.dma_start(
                            out=wgu_buf[:, par * 2048 + 1024: par * 2048 + 2048].rearrange("p (hc m) -> p hc m", hc=8),
                            in_=d_wu.rearrange("e (hc p) m -> e p hc m", p=128)[e][:, :, mc * 128:(mc + 1) * 128])
                        dM[mc % 2].inc(inst, 16)
                        if e > 0:
                            wait(gpsimd, f"e{e - 1}_dn{2 * n_tc - 1}")
                        inst = gpsimd.dma_start(
                            out=wd_sb[:, mc * 1024:(mc + 1) * 1024],
                            in_=d_wd.rearrange("e (mc p) hh -> e mc p hh", p=128)[e, mc])
                        dD.inc(inst, 16)
                    tag(f"{EP}w{mc}", dM[mc % 2])
                    tag(f"{EP}wd{mc}", dD)

                    for tt in range(n_tt):
                        j = mc * n_tt + tt
                        qpar = j % 2
                        gps = ps_big[:, qpar * 1024: qpar * 1024 + 512]
                        ups = ps_big[:, qpar * 1024 + 512: qpar * 1024 + 1024]

                        @block.tensor
                        def _(tensor, e=e, mc=mc, tt=tt, j=j, gps=gps, ups=ups, par=par, EP=EP):
                            wait(tensor, f"{EP}w{mc}")
                            if j >= 2:
                                wait(tensor, f"{EP}h_{(j - 2) // n_tt}_{(j - 2) % n_tt}")
                            elif e > 0:
                                wait(tensor, f"e{e - 1}_h_15_{n_tt - 1}")
                            for hc in range(8):
                                tensor.matmul(gps, wgu_buf[:, par * 2048 + hc * 128: par * 2048 + (hc + 1) * 128],
                                              xn_fm[:, hc * NT_MAIN + tt * 512: hc * NT_MAIN + tt * 512 + 512],
                                              start=(hc == 0), stop=(hc == 7))
                            for hc in range(8):
                                inst = tensor.matmul(ups,
                                                     wgu_buf[:, par * 2048 + 1024 + hc * 128: par * 2048 + 1024 + (hc + 1) * 128],
                                                     xn_fm[:, hc * NT_MAIN + tt * 512: hc * NT_MAIN + tt * 512 + 512],
                                                     start=(hc == 0), stop=(hc == 7))
                            sT.inc(inst)
                        tag(f"{EP}gu_{mc}_{tt}", sT)

                        @block.scalar
                        def _(scalar, j=j, mc=mc, tt=tt, qpar=qpar, gps=gps, e=e, EP=EP):
                            wait(scalar, f"{EP}gu_{mc}_{tt}")
                            if j >= 2:
                                wait(scalar, f"{EP}h_{(j - 2) // n_tt}_{(j - 2) % n_tt}")
                            elif e > 0:
                                wait(scalar, f"e{e - 1}_h_15_{n_tt - 1}")
                            inst = scalar.activation(sg_sb[:, qpar * 512:(qpar + 1) * 512], gps, AF.Silu)
                            sS.inc(inst)
                            scalar.drain()
                        tag(f"{EP}sg_{mc}_{tt}", sS)

                        @block.vector
                        def _(vector, j=j, mc=mc, tt=tt, qpar=qpar, ups=ups, e=e, EP=EP):
                            wait(vector, f"{EP}sg_{mc}_{tt}")
                            inst = vector.tensor_tensor(
                                hbuf[:, mc * NT_MAIN + tt * 512: mc * NT_MAIN + tt * 512 + 512],
                                sg_sb[:, qpar * 512:(qpar + 1) * 512], ups, ALU.mult)
                            sV.inc(inst)
                            vector.drain()
                        tag(f"{EP}h_{mc}_{tt}", sV)

                for tc in range(n_tc):
                    for nt in range(2):
                        j = tc * 2 + nt
                        dpar_ = j % 2
                        dps = ps_e[:, dpar_ * 512:(dpar_ + 1) * 512]

                        @block.tensor
                        def _(tensor, e=e, tc=tc, nt=nt, j=j, dps=dps, EP=EP):
                            if j == 0:
                                for mc in range(16):
                                    wait(tensor, f"{EP}h_{mc}_{n_tt - 1}")
                                wait(tensor, f"{EP}wd15")
                                wait(tensor, "router_done")
                            if j >= 2:
                                wait(tensor, f"{EP}dnv{j - 2}")
                            elif e > 0:
                                wait(tensor, f"e{e - 1}_dnv{2 * n_tc - 2 + j}")
                            for mc in range(16):
                                inst = tensor.matmul(dps,
                                                     hbuf[:, mc * NT_MAIN + tc * 128: mc * NT_MAIN + (tc + 1) * 128],
                                                     wd_sb[:, mc * 1024 + nt * 512: mc * 1024 + nt * 512 + 512],
                                                     start=(mc == 0), stop=(mc == 15))
                            sT.inc(inst)
                        tag(f"{EP}dn{j}", sT)

                        @block.vector
                        def _(vector, e=e, tc=tc, nt=nt, j=j, dps=dps, EP=EP, is_shared=is_shared):
                            wait(vector, f"{EP}dn{j}")
                            acc = ffn_acc[:, tc * D_MODEL + nt * 512: tc * D_MODEL + nt * 512 + 512]
                            if e == 0:
                                inst = vector.tensor_scalar_mul(acc, dps, c_tm[:, tc * 8 + e: tc * 8 + e + 1])
                            elif is_shared:
                                inst = vector.tensor_tensor(acc, dps, acc, ALU.add)
                            else:
                                inst = vector.scalar_tensor_tensor(acc, dps, c_tm[:, tc * 8 + e: tc * 8 + e + 1],
                                                                   acc, ALU.mult, ALU.add)
                            sV.inc(inst)
                            vector.drain()
                        tag(f"{EP}dnv{j}", sV)
            tag("moe_done", sV)

        ES3 = contextlib.ExitStack()
        woutT_sb = ES3.enter_context(nc.sbuf_tensor("woutT_sb", [128, 16 * D_MODEL], BF16))
        y4t = ES3.enter_context(nc.sbuf_tensor("y4t", [128, 2 * D_INNER], BF16))
        xres = ES3.enter_context(nc.sbuf_tensor("xres", [128, 2 * D_MODEL], F32))
        gsil = ES3.enter_context(nc.sbuf_tensor("gsil", [128, 2 * D_MODEL], BF16))
        sg_f = ES3.enter_context(nc.sbuf_tensor("sg_f", [128, D_MODEL], BF16))
        otile = ES3.enter_context(nc.sbuf_tensor("otile", [128, 2 * D_MODEL], F32))
        with ES3:
         if True:
            @block.gpsimd
            def _(gpsimd):
                wait(gpsimd, "moe_done")
                for ec in range(16):
                    inst = gpsimd.dma_start(out=woutT_sb[:, ec * 1024:(ec + 1) * 1024],
                                            in_=d_woutT.rearrange("(ec p) d -> ec p d", p=128)[ec])
                    dWo.inc(inst, 16)
            tag("wout_ld", dWo)

            for tc in range(n_tc):
                par = tc % 2

                @block.sync
                def _(sync, tc=tc, par=par):
                    wait(sync, "moe_done")
                    wait(sync, f"c{nch_pref + nch_main - 1}_y4out")
                    if tc >= 2:
                        wait(sync, f"fin{tc - 2}_out")
                    inst = sync.dma_start(out=y4t[:, par * 2048:(par + 1) * 2048],
                                          in_=d_y4.rearrange("(c p) f -> c p f", p=128)[tc])
                    dF[par].inc(inst, 16)
                    inst = sync.dma_start(out=xres[:, par * 1024:(par + 1) * 1024],
                                          in_=d_xm[tc * 128:(tc + 1) * 128, :])
                    dF[par].inc(inst, 16)
                tag(f"fin{tc}_ld", dF[par])

                @block.tensor
                def _(tensor, tc=tc, par=par):
                    wait(tensor, f"fin{tc}_ld")
                    wait(tensor, "wout_ld")
                    if tc >= 2:
                        wait(tensor, f"fin{tc - 2}_sev")
                    for half in range(2):
                        dpsF = ps_aux if half == 0 else ps_nrm
                        for ec in range(16):
                            inst = tensor.matmul(dpsF[:],
                                                 y4t[:, par * 2048 + ec * 128: par * 2048 + (ec + 1) * 128],
                                                 woutT_sb[:, ec * 1024 + half * 512: ec * 1024 + half * 512 + 512],
                                                 start=(ec == 0), stop=(ec == 15))
                    sT.inc(inst)
                tag(f"fin{tc}_s", sT)

                @block.scalar
                def _(scalar, tc=tc, par=par):
                    wait(scalar, "moe_done")
                    if tc >= 2:
                        wait(scalar, f"fin{tc - 2}_sev")
                    inst = scalar.activation(gsil[:, par * 1024:(par + 1) * 1024],
                                             ffn_acc[:, tc * D_MODEL:(tc + 1) * D_MODEL], AF.Silu)
                    sS.inc(inst)
                    scalar.drain()
                tag(f"fin{tc}_g", sS)

                @block.vector
                def _(vector, tc=tc, par=par):
                    wait(vector, f"fin{tc}_s")
                    wait(vector, f"fin{tc}_g")
                    if tc >= 2:
                        wait(vector, f"fin{tc - 2}_out")
                    vector.tensor_tensor(sg_f[:, 0:512], ps_aux[:], gsil[:, par * 1024: par * 1024 + 512], ALU.mult)
                    vector.drain()
                    vector.tensor_tensor(sg_f[:, 512:1024], ps_nrm[:], gsil[:, par * 1024 + 512:(par + 1) * 1024], ALU.mult)
                    vector.drain()
                    inst = vector.tensor_tensor(otile[:, par * 1024:(par + 1) * 1024],
                                                xres[:, par * 1024:(par + 1) * 1024], sg_f[:], ALU.add)
                    sV.inc(inst)
                    vector.drain()
                tag(f"fin{tc}_sev", sV)

                @block.sync
                def _(sync, tc=tc, par=par):
                    wait(sync, f"fin{tc}_sev")
                    inst = sync.dma_start(out=d_out[tc * 128:(tc + 1) * 128, :],
                                          in_=otile[:, par * 1024:(par + 1) * 1024])
                    dO[par].inc(inst, 16)
                tag(f"fin{tc}_out", dO[par])

            @block.sync
            def _(sync):
                sync.wait_ge(_dO0, C["dO0"])
                sync.wait_ge(_dO1, C["dO1"])
        ESm.close()

    f = nc.m.functions[0]
    blocks = list(f.blocks)
    head, tail = blocks[0], blocks[-1]
    mid = blocks[1:-1]
    order = {"EngineType.SP": 1, "EngineType.Pool": 2, "EngineType.PE": 3,
             "EngineType.DVE": 4, "EngineType.Activation": 5}

    def eng_of(b):
        for i in b.instructions:
            if str(i.engine) != "EngineType.Unassigned":
                return order.get(str(i.engine), 0)
        return 0

    f.blocks = [head] + sorted(mid, key=eng_of) + [tail]
    return nc


def _prep_inputs(kw):
    x = np.ascontiguousarray(np.asarray(kw["L0_attn_x"], np.float32))
    W_in = np.asarray(kw["L0_attn_in_proj_w"], np.float32)
    conv_w = np.asarray(kw["L0_attn_conv_w"], np.float32)
    conv_b = np.asarray(kw["L0_attn_conv_b"], np.float32)
    norm_w = np.asarray(kw["L0_attn_norm_w"], np.float32)
    W_out = np.asarray(kw["L0_attn_out_proj_w"], np.float32)
    dt_bias = np.asarray(kw["L0_attn_dt_bias"], np.float32)
    A_log = np.asarray(kw["L0_attn_A_log"], np.float32)
    D_param = np.asarray(kw["L0_attn_D_param"], np.float32)
    gate_w = np.asarray(kw["L0_ffn_gate_w"], np.float32)
    beta = np.asarray(kw["L0_ffn_beta"], np.float32)
    com = dict(
        w_inT=np.ascontiguousarray(W_in.T),
        w_outT=np.ascontiguousarray(W_out.T),
        gwT=np.ascontiguousarray(gate_w.T),
        convw=np.ascontiguousarray(conv_w[:, 0, :].reshape(18, 128, 4).transpose(1, 0, 2).reshape(128, 72)),
        convb=np.ascontiguousarray(conv_b.reshape(18, 128).T),
        normw=np.ascontiguousarray(norm_w.reshape(16, 128).T),
        dpar=np.ascontiguousarray(np.tile(D_param[None, :], (128, 1))),
        dtb=np.ascontiguousarray(dt_bias[:, None]),
        alog=np.ascontiguousarray(A_log[:, None]),
        beta_t=np.ascontiguousarray(np.tile(beta[None, :], (128, 1))),
        w_gate=np.ascontiguousarray(np.concatenate(
            [np.asarray(kw["L0_ffn_gate_proj_w"], np.float32),
             np.asarray(kw["L0_ffn_shared_gate_w"], np.float32)], axis=0)),
        w_up=np.ascontiguousarray(np.concatenate(
            [np.asarray(kw["L0_ffn_up_proj_w"], np.float32),
             np.asarray(kw["L0_ffn_shared_up_w"], np.float32)], axis=0)),
        w_down=np.ascontiguousarray(np.concatenate(
            [np.asarray(kw["L0_ffn_down_proj_w"], np.float32),
             np.asarray(kw["L0_ffn_shared_down_w"], np.float32)], axis=0)),
    )
    return x, com


def kernel(**kw):
    x, com = _prep_inputs(kw)
    B, L, D = x.shape
    NT = L // 2
    nc = build_nc(NT_MAIN=NT, NT_PREF=NT)
    in_maps = []
    for core in range(8):
        b, h = core // 2, core % 2
        m = dict(com)
        m["x_main"] = np.ascontiguousarray(x[b, h * NT:(h + 1) * NT])
        m["x_pref"] = np.ascontiguousarray(x[b, 0:NT]) if h == 1 else np.zeros((NT, D), np.float32)
        m["pref_flag"] = np.full((128, 1), float(h), np.float32)
        in_maps.append(m)
    res = run_bass_kernel_spmd(nc, in_maps, core_ids=list(range(8)))
    out = np.empty((B, L, D), np.float32)
    for core in range(8):
        b, h = core // 2, core % 2
        out[b, h * NT:(h + 1) * NT] = res.results[core]["out"]
    return out


# version 08
# speedup vs baseline: 1.3428x; 1.0393x over previous
import contextlib
import numpy as np
import concourse.bass as bass
import concourse.mybir as mybir
from concourse.bass_utils import run_bass_kernel_spmd

F32 = mybir.dt.float32
BF16 = mybir.dt.bfloat16
AF = mybir.ActivationFunctionType
ALU = mybir.AluOpType
AX = mybir.AxisListType.X

D_MODEL = 1024
D_STATE = 128
HEADDIM = 64
D_INNER = 2048
NHEADS = 32
CONV_DIM = 2304
D_IN_PROJ = 4384
EPS = 1e-5
Q = 128
SL = 512
NEXP = 8
M_FF = 2048

OC_X0 = 16
OC_B = 32
OC_C = 33
OC_DT = 34


def build_nc(NT_MAIN=2048, NT_PREF=2048):
    assert NT_MAIN % SL == 0 and NT_PREF % SL == 0 and NT_PREF >= SL
    n_sl_pref = NT_PREF // SL
    n_sl_main = NT_MAIN // SL
    nch_main = NT_MAIN // Q
    nch_pref = NT_PREF // Q
    NTOT = NT_PREF + NT_MAIN
    n_tc = NT_MAIN // 128
    n_tt = NT_MAIN // 512

    nc = bass.Bass("TRN2", target_bir_lowering=False)

    d_xm = nc.dram_tensor("x_main", [NT_MAIN, D_MODEL], F32, kind="ExternalInput").ap()
    d_xp = nc.dram_tensor("x_pref", [NT_PREF, D_MODEL], F32, kind="ExternalInput").ap()
    d_flag = nc.dram_tensor("pref_flag", [128, 1], F32, kind="ExternalInput").ap()
    d_winT = nc.dram_tensor("w_inT", [D_MODEL, D_IN_PROJ], F32, kind="ExternalInput").ap()
    d_woutT = nc.dram_tensor("w_outT", [D_INNER, D_MODEL], F32, kind="ExternalInput").ap()
    d_gwT = nc.dram_tensor("gwT", [D_MODEL, NEXP], F32, kind="ExternalInput").ap()
    d_convw = nc.dram_tensor("convw", [128, 18 * 4], F32, kind="ExternalInput").ap()
    d_convb = nc.dram_tensor("convb", [128, 18], F32, kind="ExternalInput").ap()
    d_normw = nc.dram_tensor("normw", [128, 16], F32, kind="ExternalInput").ap()
    d_dpar = nc.dram_tensor("dpar", [128, NHEADS], F32, kind="ExternalInput").ap()
    d_dtb = nc.dram_tensor("dtb", [NHEADS, 1], F32, kind="ExternalInput").ap()
    d_alog = nc.dram_tensor("alog", [NHEADS, 1], F32, kind="ExternalInput").ap()
    d_beta = nc.dram_tensor("beta_t", [128, NEXP], F32, kind="ExternalInput").ap()
    d_wg = nc.dram_tensor("w_gate", [NEXP + 1, D_MODEL, M_FF], F32, kind="ExternalInput").ap()
    d_wu = nc.dram_tensor("w_up", [NEXP + 1, D_MODEL, M_FF], F32, kind="ExternalInput").ap()
    d_wd = nc.dram_tensor("w_down", [NEXP + 1, M_FF, D_MODEL], F32, kind="ExternalInput").ap()
    d_out = nc.dram_tensor("out", [NT_MAIN, D_MODEL], F32, kind="ExternalOutput").ap()
    d_y4 = nc.dram_tensor("y4scr", [nch_main * 128, D_INNER], BF16, kind="Internal").ap()

    C = {}
    H = {}

    class Sem:
        def __init__(self, handle, name):
            self.h, self.name = handle, name
            C[name] = 0

        def inc(self, instr, n=1):
            instr.then_inc(self.h, n)
            C[self.name] += n

    tags = {}

    ES = contextlib.ExitStack()
    ident_f = ES.enter_context(nc.sbuf_tensor("ident_f", [128, 128], F32))
    ident_b = ES.enter_context(nc.sbuf_tensor("ident_b", [128, 128], BF16))
    ones_row = ES.enter_context(nc.sbuf_tensor("ones_row", [1, 128], F32))
    eps_col = ES.enter_context(nc.sbuf_tensor("eps_col", [128, 1], F32))
    zero_row = ES.enter_context(nc.sbuf_tensor("zero_row", [1, 512], BF16))
    ones_row_b = ES.enter_context(nc.sbuf_tensor("ones_row_b", [1, 128], BF16))
    ones_col_b = ES.enter_context(nc.sbuf_tensor("ones_col_b", [128, 1], BF16))
    ones32 = ES.enter_context(nc.sbuf_tensor("ones32", [32, SL], F32))
    mask01 = ES.enter_context(nc.sbuf_tensor("mask01", [128, 128], F32))
    maskneg8 = ES.enter_context(nc.sbuf_tensor("maskneg8", [128, 8 * 128], BF16))
    convw_sb = ES.enter_context(nc.sbuf_tensor("convw_sb", [128, 18 * 4], F32))
    convb_sb = ES.enter_context(nc.sbuf_tensor("convb_sb", [128, 18], F32))
    normw_sb = ES.enter_context(nc.sbuf_tensor("normw_sb", [128, 16], F32))
    dpar_sb = ES.enter_context(nc.sbuf_tensor("dpar_sb", [128, NHEADS], F32))
    dtb_sb = ES.enter_context(nc.sbuf_tensor("dtb_sb", [NHEADS, 1], F32))
    alog_sb = ES.enter_context(nc.sbuf_tensor("alog_sb", [NHEADS, 1], F32))
    acol_sb = ES.enter_context(nc.sbuf_tensor("acol_sb", [NHEADS, 1], F32))
    beta_sb = ES.enter_context(nc.sbuf_tensor("beta_sb", [128, NEXP], F32))
    gwT_sb = ES.enter_context(nc.sbuf_tensor("gwT_sb", [128, 8 * NEXP], BF16))
    flag_sb = ES.enter_context(nc.sbuf_tensor("flag_sb", [128, 1], F32))
    xn_fm = ES.enter_context(nc.sbuf_tensor("xn_fm", [128, 8 * NT_MAIN], BF16))
    c_tm = ES.enter_context(nc.sbuf_tensor("c_tm", [128, 16 * NEXP], F32))
    ps_big = ES.enter_context(nc.psum_tensor("ps_big", [128, 2048], F32))
    ps_e = ES.enter_context(nc.psum_tensor("ps_e", [128, 1024], F32))
    ps_en2 = ES.enter_context(nc.psum_tensor("ps_en2", [128, 1024], F32))
    _dS = ES.enter_context(nc.semaphore("dS"))
    _dG = ES.enter_context(nc.semaphore("dG"))
    _dC = ES.enter_context(nc.semaphore("dC"))
    _dX0 = ES.enter_context(nc.semaphore("dX0"))
    _dX1 = ES.enter_context(nc.semaphore("dX1"))
    _dY = ES.enter_context(nc.semaphore("dY"))
    _dF0 = ES.enter_context(nc.semaphore("dF0"))
    _dF1 = ES.enter_context(nc.semaphore("dF1"))
    _dO0 = ES.enter_context(nc.semaphore("dO0"))
    _dO1 = ES.enter_context(nc.semaphore("dO1"))
    _dWc = ES.enter_context(nc.semaphore("dWc"))
    _dI0 = ES.enter_context(nc.semaphore("dI0"))
    _dI1 = ES.enter_context(nc.semaphore("dI1"))
    _dM0 = ES.enter_context(nc.semaphore("dM0"))
    _dM1 = ES.enter_context(nc.semaphore("dM1"))
    _dD = ES.enter_context(nc.semaphore("dD"))
    _dWo = ES.enter_context(nc.semaphore("dWo"))
    _sT = ES.enter_context(nc.semaphore("sT"))
    _sV = ES.enter_context(nc.semaphore("sV"))
    _sS = ES.enter_context(nc.semaphore("sS"))
    _sP = ES.enter_context(nc.semaphore("sP"))
    block = ES.enter_context(nc.Block())
    with ES:
     if True:
        dS, dG = Sem(_dS, "dS"), Sem(_dG, "dG")
        dC, dY, dWc, dD, dWo = Sem(_dC, "dC"), Sem(_dY, "dY"), Sem(_dWc, "dWc"), Sem(_dD, "dD"), Sem(_dWo, "dWo")
        dX = [Sem(_dX0, "dX0"), Sem(_dX1, "dX1")]
        dF = [Sem(_dF0, "dF0"), Sem(_dF1, "dF1")]
        dO = [Sem(_dO0, "dO0"), Sem(_dO1, "dO1")]
        dI = [Sem(_dI0, "dI0"), Sem(_dI1, "dI1")]
        dM = [Sem(_dM0, "dM0"), Sem(_dM1, "dM1")]
        sT, sV, sS, sP = Sem(_sT, "sT"), Sem(_sV, "sV"), Sem(_sS, "sS"), Sem(_sP, "sP")
        SEMS = {s.name: s for s in [dS, dG, dC, dY, dWc, dD, dWo] + dX + dF + dO + dI + dM + [sT, sV, sS, sP]}

        def tag(name, sem):
            tags[name] = (sem.name, C[sem.name])

        def wait(eng, name):
            s, v = tags[name]
            if v > 0:
                eng.wait_ge(SEMS[s].h, v)

        @block.sync
        def _(sync):
            for dst, src in [(convw_sb, d_convw), (convb_sb, d_convb), (normw_sb, d_normw),
                             (dpar_sb, d_dpar), (dtb_sb, d_dtb), (alog_sb, d_alog),
                             (beta_sb, d_beta), (flag_sb, d_flag)]:
                inst = sync.dma_start(out=dst[:], in_=src[:])
                dC.inc(inst, 16)
        tag("p0_dS", dC)

        @block.gpsimd
        def _(gpsimd):
            inst = gpsimd.dma_start(
                out=gwT_sb[:].rearrange("p (dc e) -> p dc e", dc=8),
                in_=d_gwT.rearrange("(dc p) e -> p dc e", p=128))
            dWc.inc(inst, 16)
            gpsimd.memset(ident_f[:], 0.0)
            gpsimd.memset(ident_b[:], 0.0)
            gpsimd.memset(mask01[:], 0.0)
            gpsimd.memset(maskneg8[:], 0.0)
            gpsimd.drain()
            gpsimd.affine_select(out=ident_f[:], in_=ident_f[:], compare_op=ALU.not_equal,
                                 fill=1.0, base=0, pattern=[[-1, 128]], channel_multiplier=1)
            gpsimd.affine_select(out=ident_b[:], in_=ident_b[:], compare_op=ALU.not_equal,
                                 fill=1.0, base=0, pattern=[[-1, 128]], channel_multiplier=1)
            gpsimd.affine_select(out=mask01[:], in_=mask01[:], compare_op=ALU.is_gt,
                                 fill=1.0, base=0, pattern=[[-1, 128]], channel_multiplier=1)
            gpsimd.drain()
            for g8 in range(8):
                gpsimd.affine_select(out=maskneg8[:, g8 * 128:(g8 + 1) * 128],
                                     in_=maskneg8[:, g8 * 128:(g8 + 1) * 128],
                                     compare_op=ALU.is_ge, fill=-1e30, base=0,
                                     pattern=[[1, 128]], channel_multiplier=-1)
            gpsimd.drain()
            inst = gpsimd.engine_nop()
            sP.inc(inst)
        tag("p0_consts", sP)
        tag("p0_dG", dWc)

        @block.vector
        def _(vector):
            vector.memset(ones_row[:], 1.0)
            vector.memset(eps_col[:], EPS)
            vector.memset(zero_row[:], 0.0)
            vector.memset(ones_row_b[:], 1.0)
            vector.memset(ones_col_b[:], 1.0)
            vector.memset(ones32[:], 1.0)
            inst = vector.drain()
            sV.inc(inst)
        tag("p0_v", sV)

        @block.scalar
        def _(scalar):
            wait(scalar, "p0_dS")
            inst = scalar.activation(acol_sb[:], alog_sb[:], AF.Exp)
            sS.inc(inst)
            scalar.drain()
        tag("p0_s", sS)

        @block.vector
        def _(vector):
            wait(vector, "p0_s")
            inst = vector.tensor_scalar_mul(acol_sb[:], acol_sb[:], -1.0)
            sV.inc(inst)
            vector.drain()
        tag("p0_acol", sV)

        ES1 = contextlib.ExitStack()
        sel32 = ES1.enter_context(nc.sbuf_tensor("sel32", [32, 32 * 128], F32))
        S_sb = ES1.enter_context(nc.sbuf_tensor("S_sb", [128, D_INNER], BF16))
        S_t1 = ES1.enter_context(nc.sbuf_tensor("S_t1", [128, D_INNER], BF16))
        Pa = ES1.enter_context(nc.sbuf_tensor("Pa_sl", [32, SL + 1], F32))
        nPa = ES1.enter_context(nc.sbuf_tensor("nPa_sl", [32, SL + 1], F32))
        x_tile = ES1.enter_context(nc.sbuf_tensor("x_tile", [128, 2 * D_MODEL], F32))
        xn_tm = ES1.enter_context(nc.sbuf_tensor("xn_tm", [128, 2 * D_MODEL], F32))
        lnst = ES1.enter_context(nc.sbuf_tensor("lnst", [128, 16], F32))
        zs = ES1.enter_context(nc.sbuf_tensor("zs", [128, 16 * SL], BF16))
        xbc = ES1.enter_context(nc.sbuf_tensor("xbc", [128, 18 * (SL + 3)], BF16))
        xsilu = ES1.enter_context(nc.sbuf_tensor("xsilu", [128, 16 * SL], BF16))
        B_sl = ES1.enter_context(nc.sbuf_tensor("B_sl", [128, SL], BF16))
        C_sl = ES1.enter_context(nc.sbuf_tensor("C_sl", [128, SL], BF16))
        cva = ES1.enter_context(nc.sbuf_tensor("cva", [128, 2 * SL], BF16))
        cvb = ES1.enter_context(nc.sbuf_tensor("cvb", [128, 2 * SL], BF16))
        dts = ES1.enter_context(nc.sbuf_tensor("dts", [32, 4 * SL], F32))
        win_buf = ES1.enter_context(nc.sbuf_tensor("win_buf", [128, 2 * 1024], BF16))
        Xd = ES1.enter_context(nc.sbuf_tensor("Xd", [128, D_INNER], BF16))
        Xw = ES1.enter_context(nc.sbuf_tensor("Xw", [128, D_INNER], BF16))
        XDt = ES1.enter_context(nc.sbuf_tensor("XDt", [128, D_INNER], BF16))
        small_tm = ES1.enter_context(nc.sbuf_tensor("small_tm", [128, 128], F32))
        rr_all = ES1.enter_context(nc.sbuf_tensor("rr_all", [32, SL], F32))
        npp_all = ES1.enter_context(nc.sbuf_tensor("npp_all", [32, SL], F32))
        zs_tm = ES1.enter_context(nc.sbuf_tensor("zs_tm", [128, D_INNER], BF16))
        dec_sm = ES1.enter_context(nc.sbuf_tensor("dec_sm", [32, 8], F32))
        decrow = ES1.enter_context(nc.sbuf_tensor("decrow", [4, 32], F32))
        B_tm = ES1.enter_context(nc.sbuf_tensor("B_tm", [128, 128], BF16))
        Gt_m = ES1.enter_context(nc.sbuf_tensor("Gt_m", [128, 128], BF16))
        Eex = ES1.enter_context(nc.sbuf_tensor("Eex", [128, 2 * 1024], BF16))
        Mt = ES1.enter_context(nc.sbuf_tensor("Mt", [128, 2 * 1024], BF16))
        y2 = ES1.enter_context(nc.sbuf_tensor("y2", [128, D_INNER], BF16))
        y3 = ES1.enter_context(nc.sbuf_tensor("y3", [128, D_INNER], BF16))
        y4asm = ES1.enter_context(nc.sbuf_tensor("y4asm", [128, D_INNER], BF16))
        nrm_sm = ES1.enter_context(nc.sbuf_tensor("nrm_sm", [128, 8], F32))
        with ES1:
         if True:
            @block.gpsimd
            def _(gpsimd):
                wait(gpsimd, "p0_consts")
                gpsimd.memset(sel32[:], 0.0)
                gpsimd.drain()
                gpsimd.affine_select(out=sel32[:].rearrange("k (h m) -> k h m", h=32),
                                     in_=sel32[:].rearrange("k (h m) -> k h m", h=32),
                                     compare_op=ALU.not_equal, fill=1.0, base=0,
                                     pattern=[[-1, 32], [0, 128]], channel_multiplier=1)
                gpsimd.drain()
                inst = gpsimd.engine_nop()
                sP.inc(inst)
            tag("sel_done", sP)

            @block.vector
            def _(vector):
                for cc in range(18):
                    vector.memset(xbc[:, cc * (SL + 3):cc * (SL + 3) + 3], 0.0)
                vector.memset(S_sb[:], 0.0)
                vector.memset(Pa[:, 0:1], 0.0)
                vector.memset(nPa[:, 0:1], 0.0)
                inst = vector.drain()
                sV.inc(inst)
            tag("tails_s0", sV)

            def xbc_t(cc):
                return xbc[:, cc * (SL + 3):(cc + 1) * (SL + 3)]

            ipg = 0
            n_slices = n_sl_pref + n_sl_main
            NTT = SL // 128

            for sl_i in range(n_slices):
                is_pref = sl_i < n_sl_pref
                goff = sl_i * SL
                moff = goff - NT_PREF
                x_src = d_xp if is_pref else d_xm
                x_off = goff if is_pref else moff
                P_ = f"s{sl_i}_"
                prev_last = goff // Q - 1

                def xn_cols(dc, c0, cn, is_pref=is_pref, moff=moff):
                    if is_pref:
                        return zs[:, dc * SL + c0: dc * SL + c0 + cn]
                    return xn_fm[:, dc * NT_MAIN + moff + c0: dc * NT_MAIN + moff + c0 + cn]

                for tt in range(NTT):
                    par = tt % 2
                    xt = x_tile[:, par * D_MODEL:(par + 1) * D_MODEL]
                    xnt = xn_tm[:, par * D_MODEL:(par + 1) * D_MODEL]
                    ls = par * 8
                    gtt = sl_i * NTT + tt

                    @block.sync
                    def _(sync, tt=tt, xt=xt, x_src=x_src, x_off=x_off, P_=P_, gtt=gtt):
                        if gtt >= 2:
                            wait(sync, f"g{gtt - 2}_lnx")
                        inst = sync.dma_start(out=xt[:],
                                              in_=x_src[x_off + tt * 128: x_off + (tt + 1) * 128, :])
                        dX[gtt % 2].inc(inst, 16)
                    tag(f"g{gtt}_xld", dX[gtt % 2])

                    @block.vector
                    def _(vector, xt=xt, ls=ls, gtt=gtt):
                        wait(vector, f"g{gtt}_xld")
                        vector.reduce_sum(lnst[:, ls:ls + 1], xt[:], axis=AX)
                        vector.drain()
                        inst = vector.tensor_scalar_mul(lnst[:, ls + 1:ls + 2], lnst[:, ls:ls + 1], 1.0 / D_MODEL)
                        sV.inc(inst)
                        vector.drain()
                    tag(f"g{gtt}_mean", sV)

                    @block.scalar
                    def _(scalar, xt=xt, ls=ls, gtt=gtt):
                        wait(scalar, f"g{gtt}_xld")
                        if gtt >= 2:
                            wait(scalar, f"g{gtt - 2}_var")
                        inst = scalar.activation(xn_tm[:, (ls // 8) * D_MODEL:(ls // 8 + 1) * D_MODEL],
                                                 xt[:], AF.Square, accum_out=lnst[:, ls + 2:ls + 3])
                        sS.inc(inst)
                        scalar.drain()
                    tag(f"g{gtt}_sumsq", sS)

                    @block.vector
                    def _(vector, ls=ls, gtt=gtt):
                        wait(vector, f"g{gtt}_sumsq")
                        wait(vector, f"g{gtt}_mean")
                        vector.tensor_scalar(lnst[:, ls + 3:ls + 4], lnst[:, ls + 1:ls + 2],
                                             lnst[:, ls + 1:ls + 2], None, ALU.mult)
                        vector.drain()
                        inst = vector.scalar_tensor_tensor(lnst[:, ls + 4:ls + 5], lnst[:, ls + 2:ls + 3],
                                                           1.0 / D_MODEL, lnst[:, ls + 3:ls + 4],
                                                           ALU.mult, ALU.subtract)
                        sV.inc(inst)
                        vector.drain()
                    tag(f"g{gtt}_var", sV)

                    @block.scalar
                    def _(scalar, ls=ls, gtt=gtt):
                        wait(scalar, f"g{gtt}_var")
                        scalar.activation(lnst[:, ls + 5:ls + 6], lnst[:, ls + 4:ls + 5], AF.Ln, bias=eps_col[:])
                        scalar.drain()
                        inst = scalar.activation(lnst[:, ls + 6:ls + 7], lnst[:, ls + 5:ls + 6], AF.Exp, scale=-0.5)
                        sS.inc(inst)
                        scalar.drain()
                    tag(f"g{gtt}_sd", sS)

                    @block.vector
                    def _(vector, xt=xt, xnt=xnt, ls=ls, gtt=gtt):
                        wait(vector, f"g{gtt}_sd")
                        if gtt >= 2:
                            wait(vector, f"g{gtt - 2}_xntr")
                        inst = vector.tensor_scalar(xnt[:], xt[:], lnst[:, ls + 1:ls + 2],
                                                    lnst[:, ls + 6:ls + 7], ALU.subtract, ALU.mult)
                        sV.inc(inst)
                        vector.drain()
                    tag(f"g{gtt}_lnx", sV)

                    @block.tensor
                    def _(tensor, xnt=xnt, gtt=gtt):
                        wait(tensor, f"g{gtt}_lnx")
                        wait(tensor, "p0_consts")
                        if gtt >= 1:
                            wait(tensor, f"g{gtt - 1}_xnev")
                        for dc in range(8):
                            inst = tensor.transpose(ps_e[:, dc * 128:(dc + 1) * 128],
                                                    xnt[:, dc * 128:(dc + 1) * 128], ident_f[:])
                        sT.inc(inst)
                    tag(f"g{gtt}_xntr", sT)

                    @block.scalar
                    def _(scalar, tt=tt, gtt=gtt, is_pref=is_pref, moff=moff):
                        wait(scalar, f"g{gtt}_xntr")
                        if is_pref:
                            dst = zs[:].rearrange("p (dc t) -> p dc t", dc=16)[:, 0:8, tt * 128:(tt + 1) * 128]
                        else:
                            dst = xn_fm[:].rearrange("p (dc t) -> p dc t", dc=8)[:, :, moff + tt * 128: moff + (tt + 1) * 128]
                        inst = scalar.copy(dst, ps_e[:].rearrange("p (dc i) -> p dc i", dc=8))
                        sS.inc(inst)
                        scalar.drain()
                    tag(f"g{gtt}_xnev", sS)
                tag(f"{P_}xn_done", sS)

                if sl_i > 0:
                    @block.vector
                    def _(vector):
                        wait(vector, f"s{sl_i - 1}_convdone")
                        for cc in range(18):
                            vector.tensor_copy(xbc_t(cc)[:, 0:3], xbc_t(cc)[:, SL:SL + 3])
                        inst = vector.drain()
                        sV.inc(inst)
                    tag(f"{P_}tails", sV)
                else:
                    tags[f"{P_}tails"] = tags["tails_s0"]

                oc_list = list(range(OC_X0, OC_DT + 1)) if is_pref else list(range(35))
                for j_loc, oc in enumerate(oc_list):
                    par = ipg % 2
                    wb = win_buf[:, par * 1024:(par + 1) * 1024]
                    psd = ps_en2[:, 0:512] if par == 0 else ps_en2[:, 512:1024]
                    jg = ipg

                    @block.gpsimd
                    def _(gpsimd, oc=oc, wb=wb, jg=jg):
                        if jg >= 2:
                            wait(gpsimd, f"ip{jg - 2}_mm")
                        ncols = 128 if oc != OC_DT else 32
                        inst = gpsimd.dma_start(
                            out=wb[:].rearrange("p (ic o) -> p ic o", ic=8)[:, :, 0:ncols],
                            in_=d_winT.rearrange("(ic p) o -> p ic o", p=128)[:, :, oc * 128: oc * 128 + ncols])
                        dI[jg % 2].inc(inst, 16)
                    tag(f"ip{jg}_w", dI[jg % 2])

                    @block.tensor
                    def _(tensor, oc=oc, wb=wb, psd=psd, jg=jg, j_loc=j_loc, P_=P_,
                          prev_last=prev_last, sl_i=sl_i):
                        wait(tensor, f"ip{jg}_w")
                        wait(tensor, f"{P_}xn_done")
                        if jg >= 2:
                            wait(tensor, f"ip{jg - 2}_ev")
                        if j_loc < 2 and prev_last >= 0:
                            wait(tensor, f"c{prev_last}_sup")
                        ncols = 128 if oc != OC_DT else 32
                        for ic in range(8):
                            inst = tensor.matmul(psd[0:ncols, :],
                                                 wb[:].rearrange("p (ic o) -> p ic o", ic=8)[:, ic, 0:ncols],
                                                 xn_cols(ic, 0, SL), start=(ic == 0), stop=(ic == 7))
                        sT.inc(inst)
                    tag(f"ip{jg}_mm", sT)

                    if oc < OC_X0:
                        @block.scalar
                        def _(scalar, oc=oc, psd=psd, jg=jg):
                            wait(scalar, f"ip{jg}_mm")
                            inst = scalar.activation(zs[:, oc * SL:(oc + 1) * SL], psd[:], AF.Silu)
                            sS.inc(inst)
                            scalar.drain()
                        tag(f"ip{jg}_ev", sS)
                    elif oc == OC_DT:
                        @block.vector
                        def _(vector, psd=psd, jg=jg):
                            wait(vector, f"ip{jg}_mm")
                            inst = vector.tensor_copy(dts[:, 0:SL], psd[0:32, :])
                            sV.inc(inst)
                            vector.drain()
                        tag(f"ip{jg}_ev", sV)
                    else:
                        cc = oc - OC_X0

                        @block.scalar
                        def _(scalar, cc=cc, psd=psd, jg=jg, P_=P_):
                            wait(scalar, f"ip{jg}_mm")
                            wait(scalar, f"{P_}tails")
                            inst = scalar.copy(xbc_t(cc)[:, 3:3 + SL], psd[:])
                            sS.inc(inst)
                            scalar.drain()
                        tag(f"ip{jg}_ev", sS)
                    H[f"{P_}ip_{oc}"] = f"ip{jg}_ev"
                    ipg += 1

                cc_list = list(range(17)) if is_pref else list(range(18))
                for cc in cc_list:
                    vpar = cc % 2
                    cvx = cva if vpar == 0 else cvb

                    def conv_body(eng, sem, stt_ok, cc=cc, P_=P_, sl_i=sl_i, cvx=cvx, vpar=vpar):
                        wait(eng, H[f"{P_}ip_{OC_X0 + cc}"])
                        if cc > 1:
                            wait(eng, f"{P_}cvs{cc - 2}")
                        elif sl_i > 0:
                            lastcc = (16 - vpar) if sl_i <= n_sl_pref else (16 + vpar)
                            wait(eng, f"s{sl_i - 1}_cvs{lastcc}")
                        srcp = xbc_t(cc)
                        w = lambda k: convw_sb[:, cc * 4 + k: cc * 4 + k + 1]
                        if stt_ok:
                            eng.tensor_scalar(cvx[:, 0:SL], srcp[:, 0:SL], w(0), None, ALU.mult)
                            eng.drain()
                            eng.scalar_tensor_tensor(cvx[:, SL:2 * SL], srcp[:, 1:1 + SL], w(1), cvx[:, 0:SL], ALU.mult, ALU.add)
                            eng.drain()
                            eng.scalar_tensor_tensor(cvx[:, 0:SL], srcp[:, 2:2 + SL], w(2), cvx[:, SL:2 * SL], ALU.mult, ALU.add)
                            eng.drain()
                            inst = eng.scalar_tensor_tensor(cvx[:, SL:2 * SL], srcp[:, 3:3 + SL], w(3), cvx[:, 0:SL], ALU.mult, ALU.add)
                            sem.inc(inst)
                            eng.drain()
                        else:
                            eng.tensor_scalar(cvx[:, SL:2 * SL], srcp[:, 0:SL], w(0), None, ALU.mult)
                            eng.drain()
                            for k in (1, 2, 3):
                                eng.tensor_scalar(cvx[:, 0:SL], srcp[:, k:k + SL], w(k), None, ALU.mult)
                                eng.drain()
                                inst = eng.tensor_tensor(cvx[:, SL:2 * SL], cvx[:, SL:2 * SL], cvx[:, 0:SL], ALU.add)
                                eng.drain()
                            sem.inc(inst)
                            eng.drain()

                    @block.vector
                    def _(vector, conv_body=conv_body):
                        conv_body(vector, sV, True)
                    tag(f"{P_}cv{cc}", sV)

                    @block.scalar
                    def _(scalar, cc=cc, P_=P_, prev_last=prev_last, cvx=cvx):
                        wait(scalar, f"{P_}cv{cc}")
                        if prev_last >= 0:
                            wait(scalar, f"c{prev_last}_sup")
                        if cc < 16:
                            dst = xsilu[:, cc * SL:(cc + 1) * SL]
                        elif cc == 16:
                            dst = B_sl[:]
                        else:
                            dst = C_sl[:]
                        inst = scalar.activation(dst, cvx[:, SL:2 * SL], AF.Silu, bias=convb_sb[:, cc:cc + 1])
                        sS.inc(inst)
                        scalar.drain()
                    tag(f"{P_}cvs{cc}", sS)
                tag(f"{P_}convdone", sS)

                @block.vector
                def _(vector, P_=P_):
                    wait(vector, H[f"{P_}ip_{OC_DT}"])
                    wait(vector, "p0_acol")
                    inst = vector.tensor_scalar_add(dts[:, SL:2 * SL], dts[:, 0:SL], dtb_sb[:])
                    sV.inc(inst)
                    vector.drain()
                tag(f"{P_}dt0", sV)

                @block.scalar
                def _(scalar, P_=P_):
                    wait(scalar, f"{P_}dt0")
                    scalar.activation(dts[:, 2 * SL:3 * SL], dts[:, SL:2 * SL], AF.Abs)
                    scalar.drain()
                    inst = scalar.activation(dts[:, 2 * SL:3 * SL], dts[:, 2 * SL:3 * SL], AF.Exp, scale=-1.0)
                    sS.inc(inst)
                    scalar.drain()
                tag(f"{P_}dt1", sS)

                @block.vector
                def _(vector, P_=P_):
                    wait(vector, f"{P_}dt1")
                    inst = vector.tensor_scalar_add(dts[:, 2 * SL:3 * SL], dts[:, 2 * SL:3 * SL], 1.0)
                    sV.inc(inst)
                    vector.drain()
                tag(f"{P_}dt2", sV)

                @block.scalar
                def _(scalar, P_=P_):
                    wait(scalar, f"{P_}dt2")
                    inst = scalar.activation(dts[:, 2 * SL:3 * SL], dts[:, 2 * SL:3 * SL], AF.Ln)
                    sS.inc(inst)
                    scalar.drain()
                tag(f"{P_}dt3", sS)

                @block.vector
                def _(vector, P_=P_, sl_i=sl_i, prev_last=prev_last):
                    wait(vector, f"{P_}dt3")
                    if prev_last >= 0:
                        wait(vector, f"c{prev_last}_sup")
                    if sl_i > 0:
                        vector.tensor_copy(Pa[:, 0:1], Pa[:, SL:SL + 1])
                        vector.drain()
                    vector.tensor_scalar_max(dts[:, 3 * SL:4 * SL], dts[:, SL:2 * SL], 0.0)
                    vector.drain()
                    vector.tensor_add(dts[:, 0:SL], dts[:, 3 * SL:4 * SL], dts[:, 2 * SL:3 * SL])
                    vector.drain()
                    vector.tensor_scalar_mul(dts[:, 3 * SL:4 * SL], dts[:, 0:SL], acol_sb[:])
                    vector.drain()
                    vector.tensor_tensor_scan(Pa[:, 1:1 + SL], ones32[:],
                                              dts[:, 3 * SL:4 * SL], Pa[:, 0:1],
                                              ALU.mult, ALU.add)
                    vector.drain()
                    inst = vector.tensor_scalar_mul(nPa[:, 0:1 + SL], Pa[:, 0:1 + SL], -1.0)
                    sV.inc(inst)
                    vector.drain()
                tag(f"{P_}pa", sV)

                @block.vector
                def _(vector, P_=P_):
                    wait(vector, f"{P_}pa")
                    pe = bass.AP(tensor=Pa[:].tensor, offset=128, ap=[[SL + 1, 32], [128, SL // 128]])
                    pb = bass.AP(tensor=Pa[:].tensor, offset=0, ap=[[SL + 1, 32], [128, SL // 128]])
                    inst = vector.tensor_tensor(dec_sm[:, 0:SL // 128], pe, pb, ALU.subtract)
                    sV.inc(inst)
                    vector.drain()
                tag(f"{P_}dec0", sV)

                @block.scalar
                def _(scalar, P_=P_):
                    wait(scalar, f"{P_}dec0")
                    inst = scalar.activation(dec_sm[:, 4:4 + SL // 128], dec_sm[:, 0:SL // 128], AF.Exp)
                    sS.inc(inst)
                    scalar.drain()
                tag(f"{P_}dec1", sS)

                @block.tensor
                def _(tensor, P_=P_):
                    wait(tensor, f"{P_}dec1")
                    if goff > 0:
                        wait(tensor, f"c{goff // Q - 1}_sup")
                    inst = tensor.transpose(ps_en2[0:SL // 128, 448:480], dec_sm[:, 4:4 + SL // 128],
                                            ident_f[0:32, 0:32])
                    sT.inc(inst)
                tag(f"{P_}dectr", sT)

                @block.vector
                def _(vector, P_=P_):
                    wait(vector, f"{P_}dectr")
                    inst = vector.tensor_copy(decrow[0:SL // 128, :], ps_en2[0:SL // 128, 448:480])
                    sV.inc(inst)
                    vector.drain()
                tag(f"{P_}decrows", sV)

                @block.scalar
                def _(scalar, P_=P_, prev_last=prev_last):
                    wait(scalar, f"{P_}pa")
                    if prev_last >= 0:
                        wait(scalar, f"c{prev_last}_rrtm")
                    for c4 in range(SL // Q):
                        inst = scalar.activation(rr_all[:, c4 * 128:(c4 + 1) * 128],
                                                 Pa[:, c4 * 128 + 1:(c4 + 1) * 128 + 1], AF.Exp,
                                                 scale=-1.0,
                                                 bias=Pa[:, (c4 + 1) * 128:(c4 + 1) * 128 + 1])
                        scalar.drain()
                    sS.inc(inst)
                tag(f"{P_}rrall", sS)

                if not is_pref:
                    @block.vector
                    def _(vector, P_=P_, prev_last=prev_last):
                        wait(vector, f"{P_}pa")
                        npc = bass.AP(tensor=nPa[:].tensor, offset=0,
                                      ap=[[SL + 1, 32], [128, SL // 128]]).unsqueeze(2).broadcast_to([32, SL // 128, 128])
                        inst = vector.tensor_tensor(
                            npp_all[:].rearrange("h (c i) -> h c i", c=SL // 128),
                            ones32[:].rearrange("h (c i) -> h c i", c=SL // 128), npc, ALU.mult)
                        sV.inc(inst)
                        vector.drain()
                    tag(f"{P_}nppall", sV)

                for cl in range(SL // Q):
                    gch = goff // Q + cl
                    col = cl * Q
                    CP = f"c{gch}_"
                    pcol = cl * Q
                    psbig_bf = ps_big[:].bitcast(BF16)
                    en2_bf = ps_en2[:].bitcast(BF16)

                    @block.tensor
                    def _(tensor, col=col, CP=CP, P_=P_, gch=gch, is_pref=is_pref):
                        wait(tensor, f"{P_}convdone")
                        wait(tensor, f"{P_}rrall")
                        if gch > 0:
                            wait(tensor, f"c{gch - 1}_psbig_free")
                            wait(tensor, f"c{gch - 1}_sup")
                        for cc in range(16):
                            tensor.transpose(psbig_bf[:, cc * 128:(cc + 1) * 128],
                                             xsilu[:, cc * SL + col: cc * SL + col + 128], ident_b[:])
                        if not is_pref:
                            for cc in range(16):
                                tensor.transpose(psbig_bf[:, 2048 + cc * 128: 2048 + (cc + 1) * 128],
                                                 zs[:, cc * SL + col: cc * SL + col + 128], ident_b[:])
                        tensor.transpose(ps_en2[:, 0:32], dts[:, col:col + 128], ident_f[0:32, 0:32])
                        tensor.transpose(ps_en2[:, 32:64], rr_all[:, col:col + 128], ident_f[0:32, 0:32])
                        inst = tensor.transpose(en2_bf[:, 128:256], B_sl[:, col:col + 128], ident_b[:])
                        if not is_pref:
                            inst = tensor.matmul(ps_en2[:, 256:384], B_sl[:, col:col + 128],
                                                 C_sl[:, col:col + 128], start=True, stop=True)
                        sT.inc(inst)
                    tag(f"{CP}xtr", sT)

                    @block.vector
                    def _(vector, CP=CP, gch=gch, is_pref=is_pref):
                        wait(vector, f"{CP}xtr")
                        vector.tensor_copy(small_tm[:, 0:64], ps_en2[:, 0:64])
                        vector.drain()
                        vector.tensor_copy(B_tm[:], en2_bf[:, 128:256])
                        if not is_pref:
                            vector.tensor_tensor(Gt_m[:], ps_en2[:, 256:384], mask01[:], ALU.mult)
                            vector.drain()
                            vector.tensor_copy(zs_tm[:], psbig_bf[:, 2048:4096])
                        vector.drain()
                        dt_rep = small_tm[:, 0:32].unsqueeze(2).broadcast_to([128, 32, 64])
                        rr_rep = small_tm[:, 32:64].unsqueeze(2).broadcast_to([128, 32, 64])
                        xv = lambda t: t[:].rearrange("p (h e) -> p h e", h=32)
                        vector.tensor_tensor(xv(Xd), psbig_bf[:, 0:2048].rearrange("p (h e) -> p h e", h=32),
                                             dt_rep, ALU.mult)
                        vector.drain()
                        vector.tensor_tensor(xv(Xw), xv(Xd), rr_rep, ALU.mult)
                        vector.drain()
                        vector.reciprocal(small_tm[:, 96:128], small_tm[:, 0:32])
                        vector.drain()
                        vector.tensor_tensor(small_tm[:, 64:96], dpar_sb[:], small_tm[:, 96:128], ALU.mult)
                        vector.drain()
                        ddt_rep = small_tm[:, 64:96].unsqueeze(2).broadcast_to([128, 32, 64])
                        inst = vector.tensor_tensor(xv(XDt), xv(Xd), ddt_rep, ALU.mult)
                        sV.inc(inst)
                        vector.drain()
                    tag(f"{CP}xd", sV)
                    tag(f"{CP}rrtm", sV)

                    if not is_pref:
                        def emm_round(tensor, r, CP=CP, pcol=pcol):
                            ps = ps_e if r % 2 == 0 else ps_en2
                            if r >= 2:
                                wait(tensor, f"{CP}eex{r - 2}")
                            if r < 4:
                                tensor.matmul(ps[:, 0:512], ident_b[:], maskneg8[:, 0:512],
                                              start=True, stop=False, skip_group_check=True)
                                tensor.matmul(ps[:, 512:1024], ident_b[:], maskneg8[:, 512:1024],
                                              start=True, stop=False, skip_group_check=True)
                            for hh in range(8):
                                h = (r % 4) * 8 + hh
                                sl8 = ps[:, hh * 128:(hh + 1) * 128]
                                selh = sel32[:, h * 128:(h + 1) * 128]
                                if r < 4:
                                    tensor.matmul(sl8, selh, Pa[0:32, pcol + 1:pcol + 129],
                                                  start=False, stop=False, skip_group_check=True)
                                    inst = tensor.matmul(sl8, nPa[0:32, pcol + 1:pcol + 129], selh,
                                                         start=False, stop=(hh == 7), skip_group_check=True)
                                else:
                                    tensor.matmul(sl8, selh, Pa[0:32, pcol + 1:pcol + 129],
                                                  start=True, stop=False, skip_group_check=True)
                                    inst = tensor.matmul(sl8, npp_all[:, pcol:pcol + 128], selh,
                                                         start=False, stop=True, skip_group_check=True)
                            return inst

                        def y_round(tensor, r, CP=CP):
                            wait(tensor, f"{CP}mt{r}")
                            if r == 0:
                                wait(tensor, f"{CP}xd")
                                for b4 in range(4):
                                    tensor.matmul(ps_big[:, b4 * 512:(b4 + 1) * 512], ones_row_b[:],
                                                  zero_row[:], start=True, stop=False,
                                                  skip_group_check=True)
                            for hh in range(8):
                                h = (r % 4) * 8 + hh
                                mtb = Mt[:, (r % 2) * 1024 + hh * 128:(r % 2) * 1024 + (hh + 1) * 128]
                                if r < 4:
                                    inst = tensor.matmul(ps_big[:, h * 64:(h + 1) * 64], mtb,
                                                         Xd[:, h * 64:(h + 1) * 64],
                                                         start=False, stop=False, skip_group_check=True)
                                else:
                                    inst = tensor.matmul(ps_big[:, h * 64:(h + 1) * 64], mtb,
                                                         S_sb[:, h * 64:(h + 1) * 64],
                                                         start=False, stop=(r == 7 and hh == 7),
                                                         skip_group_check=True)
                            return inst

                        @block.tensor
                        def _(tensor, CP=CP, gch=gch):
                            wait(tensor, f"{CP}rrtm")
                            if gch == nch_pref:
                                wait(tensor, "sel_done")
                            inst = emm_round(tensor, 0)
                            sT.inc(inst)
                        tag(f"{CP}emm0", sT)

                        @block.scalar
                        def _(scalar, CP=CP):
                            wait(scalar, f"{CP}emm0")
                            inst = scalar.activation(Eex[:, 0:1024], ps_e[:], AF.Exp)
                            sS.inc(inst)
                            scalar.drain()
                        tag(f"{CP}eex0", sS)

                        @block.vector
                        def _(vector, CP=CP, col=col):
                            wait(vector, f"{CP}eex0")
                            gt_rep = Gt_m[:].unsqueeze(1).broadcast_to([128, 8, 128])
                            inst = vector.tensor_tensor(
                                Mt[:, 0:1024].rearrange("p (h i) -> p h i", h=8),
                                Eex[:, 0:1024].rearrange("p (h i) -> p h i", h=8), gt_rep, ALU.mult)
                            sV.inc(inst)
                            vector.drain()
                        tag(f"{CP}mt0", sV)

                        for r in range(1, 8):
                            @block.tensor
                            def _(tensor, r=r, CP=CP):
                                inst = emm_round(tensor, r)
                                sT.inc(inst)
                            tag(f"{CP}emm{r}", sT)

                            @block.scalar
                            def _(scalar, r=r, CP=CP):
                                wait(scalar, f"{CP}emm{r}")
                                if r >= 2:
                                    wait(scalar, f"{CP}mt{r - 2}")
                                ps = ps_e if r % 2 == 0 else ps_en2
                                inst = scalar.activation(Eex[:, (r % 2) * 1024:(r % 2) * 1024 + 1024],
                                                         ps[:], AF.Exp)
                                sS.inc(inst)
                                scalar.drain()
                            tag(f"{CP}eex{r}", sS)

                            @block.vector
                            def _(vector, r=r, CP=CP, col=col):
                                wait(vector, f"{CP}eex{r}")
                                if r >= 2:
                                    wait(vector, f"{CP}y{r - 2}")
                                rep = (Gt_m[:] if r < 4 else C_sl[:, col:col + 128]).unsqueeze(1).broadcast_to([128, 8, 128])
                                inst = vector.tensor_tensor(
                                    Mt[:, (r % 2) * 1024:(r % 2) * 1024 + 1024].rearrange("p (h i) -> p h i", h=8),
                                    Eex[:, (r % 2) * 1024:(r % 2) * 1024 + 1024].rearrange("p (h i) -> p h i", h=8),
                                    rep, ALU.mult)
                                sV.inc(inst)
                                vector.drain()
                            tag(f"{CP}mt{r}", sV)

                            @block.tensor
                            def _(tensor, r=r, CP=CP):
                                inst = y_round(tensor, r - 1)
                                sT.inc(inst)
                            tag(f"{CP}y{r - 1}", sT)

                        @block.tensor
                        def _(tensor, CP=CP):
                            inst = y_round(tensor, 7)
                            sT.inc(inst)
                        tag(f"{CP}y7", sT)

                        @block.vector
                        def _(vector, CP=CP):
                            wait(vector, f"{CP}y7")
                            inst = vector.tensor_tensor(y2[:], ps_big[:], XDt[:], ALU.add)
                            sV.inc(inst)
                            vector.drain()
                        tag(f"{CP}y2", sV)

                        @block.vector
                        def _(vector, CP=CP):
                            inst = vector.tensor_tensor(y3[:], y2[:], zs_tm[:], ALU.mult)
                            sV.inc(inst)
                            vector.drain()
                        tag(f"{CP}y3", sV)

                        @block.scalar
                        def _(scalar, CP=CP):
                            wait(scalar, f"{CP}y3")
                            inst = scalar.activation(y2[:], y3[:], AF.Square, accum_out=nrm_sm[:, 0:1])
                            sS.inc(inst)
                            scalar.drain()
                        tag(f"{CP}sqy", sS)

                        @block.vector
                        def _(vector, CP=CP):
                            wait(vector, f"{CP}sqy")
                            inst = vector.tensor_scalar(nrm_sm[:, 1:2], nrm_sm[:, 0:1], 1.0 / D_INNER,
                                                        EPS, ALU.mult, ALU.add)
                            sV.inc(inst)
                            vector.drain()
                        tag(f"{CP}mse", sV)

                        @block.scalar
                        def _(scalar, CP=CP):
                            wait(scalar, f"{CP}mse")
                            scalar.activation(nrm_sm[:, 2:3], nrm_sm[:, 1:2], AF.Ln)
                            scalar.drain()
                            inst = scalar.activation(nrm_sm[:, 3:4], nrm_sm[:, 2:3], AF.Exp, scale=-0.5)
                            sS.inc(inst)
                            scalar.drain()
                        tag(f"{CP}rstd", sS)

                        @block.vector
                        def _(vector, CP=CP, gch=gch):
                            wait(vector, f"{CP}rstd")
                            inst = vector.tensor_scalar(y3[:], y3[:], nrm_sm[:, 3:4], None, ALU.mult)
                            sV.inc(inst)
                            vector.drain()
                        tag(f"{CP}y4", sV)

                        tag_schunk_ready = f"{CP}y2"
                    else:
                        tag_schunk_ready = f"{CP}xd"

                    @block.tensor
                    def _(tensor, CP=CP, tag_schunk_ready=tag_schunk_ready):
                        wait(tensor, tag_schunk_ready)
                        for j4 in range(4):
                            inst = tensor.matmul(ps_big[:, j4 * 512:(j4 + 1) * 512], B_tm[:],
                                                 Xw[:, j4 * 512:(j4 + 1) * 512], start=True, stop=True)
                        sT.inc(inst)
                    tag(f"{CP}sch", sT)

                    @block.tensor
                    def _(tensor, CP=CP, cl=cl, P_=P_):
                        wait(tensor, f"{P_}decrows")
                        wait(tensor, f"{CP}sch")
                        inst = tensor.matmul(ps_en2[:, 480:512], sel32[0:SL // 128, cl * 128:(cl + 1) * 128],
                                             decrow[0:SL // 128, :], start=True, stop=True)
                        sT.inc(inst)
                    tag(f"{CP}decb", sT)

                    @block.vector
                    def _(vector, CP=CP, gch=gch):
                        wait(vector, f"{CP}decb")
                        dec_rep = ps_en2[:, 480:512].unsqueeze(2).broadcast_to([128, 32, 64])
                        vector.tensor_tensor(S_t1[:].rearrange("p (h e) -> p h e", h=32),
                                             S_sb[:].rearrange("p (h e) -> p h e", h=32), dec_rep, ALU.mult)
                        vector.drain()
                        inst = vector.tensor_tensor(S_sb[:], S_t1[:], ps_big[:], ALU.add)
                        sV.inc(inst)
                        vector.drain()
                        if gch == nch_pref - 1:
                            inst = vector.tensor_scalar_mul(S_sb[:], S_sb[:], flag_sb[:])
                            sV.inc(inst)
                            vector.drain()
                    tag(f"{CP}sup", sV)

                    if not is_pref:
                        @block.tensor
                        def _(tensor, CP=CP, gch=gch):
                            wait(tensor, f"{CP}y4")
                            wait(tensor, f"{CP}sup")
                            for ec in range(16):
                                inst = tensor.transpose(psbig_bf[:, ec * 128:(ec + 1) * 128],
                                                        y3[:, ec * 128:(ec + 1) * 128], ident_b[:])
                            sT.inc(inst)
                        tag(f"{CP}y4tr", sT)

                        @block.scalar
                        def _(scalar, CP=CP, gch=gch):
                            wait(scalar, f"{CP}y4tr")
                            if gch - 1 >= nch_pref:
                                wait(scalar, f"c{gch - 1}_y4out")
                            inst = scalar.copy(y4asm[:], psbig_bf[:, 0:2048])
                            sS.inc(inst)
                            scalar.drain()
                        tag(f"{CP}y4ev", sS)
                        tag(f"{CP}psbig_free2", sS)

                        @block.sync
                        def _(sync, CP=CP, gch=gch):
                            wait(sync, f"{CP}y4ev")
                            mch = gch - nch_pref
                            inst = sync.dma_start(out=d_y4.rearrange("(c p) f -> c p f", p=128)[mch],
                                                  in_=y4asm[:])
                            dY.inc(inst, 16)
                        tag(f"{CP}y4out", dY)

                    if is_pref:
                        tags[f"{CP}psbig_free"] = tags[f"{CP}sup"]
                    else:
                        tags[f"{CP}psbig_free"] = tags[f"{CP}psbig_free2"]

            last_sup = f"c{NTOT // Q - 1}_sup"

        ESm = contextlib.ExitStack()
        ffn_acc = ESm.enter_context(nc.sbuf_tensor("ffn_acc", [128, 16 * D_MODEL], BF16))
        ES2 = contextlib.ExitStack()
        hbuf = ES2.enter_context(nc.sbuf_tensor("hbuf", [128, 16 * NT_MAIN], BF16))
        wd_sb = ES2.enter_context(nc.sbuf_tensor("wd_sb", [128, 16 * D_MODEL], BF16))
        wgu_buf = ES2.enter_context(nc.sbuf_tensor("wgu_buf", [128, 2 * 2048], BF16))
        sg_sb = ES2.enter_context(nc.sbuf_tensor("sg_sb", [128, 2 * 512], BF16))
        lgs = ES2.enter_context(nc.sbuf_tensor("lgs", [128, 40], F32))
        with ES2:
         if True:
            for tc in range(n_tc):
                @block.tensor
                def _(tensor, tc=tc):
                    wait(tensor, last_sup)
                    wait(tensor, "p0_dG")
                    if tc > 0:
                        wait(tensor, f"rt{tc - 1}_lg")
                    for dc in range(8):
                        inst = tensor.matmul(ps_en2[:, 0:8],
                                             xn_fm[:, dc * NT_MAIN + tc * 128: dc * NT_MAIN + (tc + 1) * 128],
                                             gwT_sb[:, dc * 8:(dc + 1) * 8],
                                             start=(dc == 0), stop=(dc == 7))
                    sT.inc(inst)
                tag(f"rt{tc}_mm", sT)

                @block.vector
                def _(vector, tc=tc):
                    wait(vector, f"rt{tc}_mm")
                    if tc > 0:
                        wait(vector, f"rt{tc - 1}_sg")
                    inst = vector.tensor_copy(lgs[:, 0:8], ps_en2[:, 0:8])
                    sV.inc(inst)
                    vector.drain()
                tag(f"rt{tc}_lg", sV)

                @block.scalar
                def _(scalar, tc=tc):
                    wait(scalar, f"rt{tc}_lg")
                    inst = scalar.activation(lgs[:, 8:16], lgs[:, 0:8], AF.Sigmoid)
                    sS.inc(inst)
                    scalar.drain()
                tag(f"rt{tc}_sg", sS)

                @block.vector
                def _(vector, tc=tc):
                    wait(vector, f"rt{tc}_sg")
                    cslot = c_tm[:, tc * 8:(tc + 1) * 8]
                    vector.tensor_tensor(cslot, lgs[:, 0:8], beta_sb[:], ALU.add)
                    vector.drain()
                    vector.reduce_max(lgs[:, 32:33], cslot, axis=AX)
                    vector.drain()
                    vector.tensor_scalar(lgs[:, 16:24], cslot, lgs[:, 32:33], None, ALU.is_ge)
                    vector.drain()
                    vector.scalar_tensor_tensor(lgs[:, 24:32], lgs[:, 16:24], -1e30, cslot, ALU.mult, ALU.add)
                    vector.drain()
                    vector.reduce_max(lgs[:, 33:34], lgs[:, 24:32], axis=AX)
                    vector.drain()
                    vector.tensor_scalar(lgs[:, 24:32], lgs[:, 24:32], lgs[:, 33:34], None, ALU.is_ge)
                    vector.drain()
                    vector.tensor_add(lgs[:, 16:24], lgs[:, 16:24], lgs[:, 24:32])
                    vector.drain()
                    inst = vector.tensor_tensor(cslot, lgs[:, 8:16], lgs[:, 16:24], ALU.mult)
                    sV.inc(inst)
                    vector.drain()
                tag(f"rt{tc}_c", sV)
            tag("router_done", sV)

            for e in range(NEXP + 1):
                is_shared = e == NEXP
                EP = f"e{e}_"
                for mc in range(16):
                    par = mc % 2

                    @block.gpsimd
                    def _(gpsimd, e=e, mc=mc, par=par, EP=EP):
                        if mc >= 2:
                            wait(gpsimd, f"{EP}gu_{mc - 2}_{n_tt - 1}")
                        elif e > 0:
                            wait(gpsimd, f"e{e - 1}_gu_{mc + 14}_{n_tt - 1}")
                        else:
                            wait(gpsimd, last_sup)
                        inst = gpsimd.dma_start(
                            out=wgu_buf[:, par * 2048: par * 2048 + 1024].rearrange("p (hc m) -> p hc m", hc=8),
                            in_=d_wg.rearrange("e (hc p) m -> e p hc m", p=128)[e][:, :, mc * 128:(mc + 1) * 128])
                        dM[mc % 2].inc(inst, 16)
                        inst = gpsimd.dma_start(
                            out=wgu_buf[:, par * 2048 + 1024: par * 2048 + 2048].rearrange("p (hc m) -> p hc m", hc=8),
                            in_=d_wu.rearrange("e (hc p) m -> e p hc m", p=128)[e][:, :, mc * 128:(mc + 1) * 128])
                        dM[mc % 2].inc(inst, 16)
                        if e > 0:
                            wait(gpsimd, f"e{e - 1}_dn{2 * n_tc - 1}")
                        inst = gpsimd.dma_start(
                            out=wd_sb[:, mc * 1024:(mc + 1) * 1024],
                            in_=d_wd.rearrange("e (mc p) hh -> e mc p hh", p=128)[e, mc])
                        dD.inc(inst, 16)
                    tag(f"{EP}w{mc}", dM[mc % 2])
                    tag(f"{EP}wd{mc}", dD)

                    for tt in range(n_tt):
                        j = mc * n_tt + tt
                        qpar = j % 2
                        gps = ps_big[:, qpar * 1024: qpar * 1024 + 512]
                        ups = ps_big[:, qpar * 1024 + 512: qpar * 1024 + 1024]

                        @block.tensor
                        def _(tensor, e=e, mc=mc, tt=tt, j=j, gps=gps, ups=ups, par=par, EP=EP):
                            wait(tensor, f"{EP}w{mc}")
                            if j >= 2:
                                wait(tensor, f"{EP}h_{(j - 2) // n_tt}_{(j - 2) % n_tt}")
                            elif e > 0:
                                wait(tensor, f"e{e - 1}_h_15_{n_tt - 1}")
                            for hc in range(8):
                                tensor.matmul(gps, wgu_buf[:, par * 2048 + hc * 128: par * 2048 + (hc + 1) * 128],
                                              xn_fm[:, hc * NT_MAIN + tt * 512: hc * NT_MAIN + tt * 512 + 512],
                                              start=(hc == 0), stop=(hc == 7))
                            for hc in range(8):
                                inst = tensor.matmul(ups,
                                                     wgu_buf[:, par * 2048 + 1024 + hc * 128: par * 2048 + 1024 + (hc + 1) * 128],
                                                     xn_fm[:, hc * NT_MAIN + tt * 512: hc * NT_MAIN + tt * 512 + 512],
                                                     start=(hc == 0), stop=(hc == 7))
                            sT.inc(inst)
                        tag(f"{EP}gu_{mc}_{tt}", sT)

                        @block.scalar
                        def _(scalar, j=j, mc=mc, tt=tt, qpar=qpar, gps=gps, e=e, EP=EP):
                            wait(scalar, f"{EP}gu_{mc}_{tt}")
                            if j >= 2:
                                wait(scalar, f"{EP}h_{(j - 2) // n_tt}_{(j - 2) % n_tt}")
                            elif e > 0:
                                wait(scalar, f"e{e - 1}_h_15_{n_tt - 1}")
                            inst = scalar.activation(sg_sb[:, qpar * 512:(qpar + 1) * 512], gps, AF.Silu)
                            sS.inc(inst)
                            scalar.drain()
                        tag(f"{EP}sg_{mc}_{tt}", sS)

                        @block.vector
                        def _(vector, j=j, mc=mc, tt=tt, qpar=qpar, ups=ups, e=e, EP=EP):
                            wait(vector, f"{EP}sg_{mc}_{tt}")
                            inst = vector.tensor_tensor(
                                hbuf[:, mc * NT_MAIN + tt * 512: mc * NT_MAIN + tt * 512 + 512],
                                sg_sb[:, qpar * 512:(qpar + 1) * 512], ups, ALU.mult)
                            sV.inc(inst)
                            vector.drain()
                        tag(f"{EP}h_{mc}_{tt}", sV)

                for tc in range(n_tc):
                    for nt in range(2):
                        j = tc * 2 + nt
                        dpar_ = j % 2
                        dps = ps_e[:, dpar_ * 512:(dpar_ + 1) * 512]

                        @block.tensor
                        def _(tensor, e=e, tc=tc, nt=nt, j=j, dps=dps, EP=EP):
                            if j == 0:
                                for mc in range(16):
                                    wait(tensor, f"{EP}h_{mc}_{n_tt - 1}")
                                wait(tensor, f"{EP}wd15")
                                wait(tensor, "router_done")
                            if j >= 2:
                                wait(tensor, f"{EP}dnv{j - 2}")
                            elif e > 0:
                                wait(tensor, f"e{e - 1}_dnv{2 * n_tc - 2 + j}")
                            for mc in range(16):
                                inst = tensor.matmul(dps,
                                                     hbuf[:, mc * NT_MAIN + tc * 128: mc * NT_MAIN + (tc + 1) * 128],
                                                     wd_sb[:, mc * 1024 + nt * 512: mc * 1024 + nt * 512 + 512],
                                                     start=(mc == 0), stop=(mc == 15))
                            sT.inc(inst)
                        tag(f"{EP}dn{j}", sT)

                        @block.vector
                        def _(vector, e=e, tc=tc, nt=nt, j=j, dps=dps, EP=EP, is_shared=is_shared):
                            wait(vector, f"{EP}dn{j}")
                            acc = ffn_acc[:, tc * D_MODEL + nt * 512: tc * D_MODEL + nt * 512 + 512]
                            if e == 0:
                                inst = vector.tensor_scalar_mul(acc, dps, c_tm[:, tc * 8 + e: tc * 8 + e + 1])
                            elif is_shared:
                                inst = vector.tensor_tensor(acc, dps, acc, ALU.add)
                            else:
                                inst = vector.scalar_tensor_tensor(acc, dps, c_tm[:, tc * 8 + e: tc * 8 + e + 1],
                                                                   acc, ALU.mult, ALU.add)
                            sV.inc(inst)
                            vector.drain()
                        tag(f"{EP}dnv{j}", sV)
            tag("moe_done", sV)

        ES3 = contextlib.ExitStack()
        woutT_sb = ES3.enter_context(nc.sbuf_tensor("woutT_sb", [128, 16 * D_MODEL], BF16))
        y4t = ES3.enter_context(nc.sbuf_tensor("y4t", [128, 2 * D_INNER], BF16))
        xres = ES3.enter_context(nc.sbuf_tensor("xres", [128, 2 * D_MODEL], F32))
        gsil = ES3.enter_context(nc.sbuf_tensor("gsil", [128, 2 * D_MODEL], BF16))
        sg_f = ES3.enter_context(nc.sbuf_tensor("sg_f", [128, D_MODEL], BF16))
        otile = ES3.enter_context(nc.sbuf_tensor("otile", [128, 2 * D_MODEL], F32))
        with ES3:
         if True:
            @block.gpsimd
            def _(gpsimd):
                wait(gpsimd, f"e{NEXP}_dn{2 * n_tc - 1}")
                for ec in range(16):
                    inst = gpsimd.dma_start(out=woutT_sb[:, ec * 1024:(ec + 1) * 1024],
                                            in_=d_woutT.rearrange("(ec p) d -> ec p d", p=128)[ec])
                    dWo.inc(inst, 16)
            tag("wout_raw", dWo)

            @block.vector
            def _(vector):
                wait(vector, "wout_raw")
                inst = vector.tensor_tensor(
                    woutT_sb[:].rearrange("p (ec d) -> p ec d", ec=16),
                    woutT_sb[:].rearrange("p (ec d) -> p ec d", ec=16),
                    normw_sb[:].unsqueeze(2).broadcast_to([128, 16, D_MODEL]), ALU.mult)
                sV.inc(inst)
                vector.drain()
            tag("wout_ld", sV)

            for tc in range(n_tc):
                par = tc % 2

                @block.sync
                def _(sync, tc=tc, par=par):
                    wait(sync, f"e{NEXP}_dn{2 * n_tc - 1}")
                    wait(sync, f"c{nch_pref + nch_main - 1}_y4out")
                    if tc >= 2:
                        wait(sync, f"fin{tc - 2}_out")
                    inst = sync.dma_start(out=y4t[:, par * 2048:(par + 1) * 2048],
                                          in_=d_y4.rearrange("(c p) f -> c p f", p=128)[tc])
                    dF[par].inc(inst, 16)
                    inst = sync.dma_start(out=xres[:, par * 1024:(par + 1) * 1024],
                                          in_=d_xm[tc * 128:(tc + 1) * 128, :])
                    dF[par].inc(inst, 16)
                tag(f"fin{tc}_ld", dF[par])

                psF = ps_en2 if par == 0 else ps_e

                @block.tensor
                def _(tensor, tc=tc, par=par, psF=psF):
                    wait(tensor, f"fin{tc}_ld")
                    wait(tensor, "wout_ld")
                    if tc >= 2:
                        wait(tensor, f"fin{tc - 2}_sev")
                    for half in range(2):
                        dpsF = psF[:, 0:512] if half == 0 else psF[:, 512:1024]
                        for ec in range(16):
                            inst = tensor.matmul(dpsF[:],
                                                 y4t[:, par * 2048 + ec * 128: par * 2048 + (ec + 1) * 128],
                                                 woutT_sb[:, ec * 1024 + half * 512: ec * 1024 + half * 512 + 512],
                                                 start=(ec == 0), stop=(ec == 15))
                    sT.inc(inst)
                tag(f"fin{tc}_s", sT)

                @block.scalar
                def _(scalar, tc=tc, par=par):
                    wait(scalar, "moe_done")
                    if tc >= 2:
                        wait(scalar, f"fin{tc - 2}_sev")
                    inst = scalar.activation(gsil[:, par * 1024:(par + 1) * 1024],
                                             ffn_acc[:, tc * D_MODEL:(tc + 1) * D_MODEL], AF.Silu)
                    sS.inc(inst)
                    scalar.drain()
                tag(f"fin{tc}_g", sS)

                @block.vector
                def _(vector, tc=tc, par=par, psF=psF):
                    wait(vector, f"fin{tc}_s")
                    wait(vector, f"fin{tc}_g")
                    if tc >= 2:
                        wait(vector, f"fin{tc - 2}_out")
                    vector.tensor_tensor(sg_f[:, 0:512], psF[:, 0:512], gsil[:, par * 1024: par * 1024 + 512], ALU.mult)
                    vector.drain()
                    vector.tensor_tensor(sg_f[:, 512:1024], psF[:, 512:1024], gsil[:, par * 1024 + 512:(par + 1) * 1024], ALU.mult)
                    vector.drain()
                    inst = vector.tensor_tensor(otile[:, par * 1024:(par + 1) * 1024],
                                                xres[:, par * 1024:(par + 1) * 1024], sg_f[:], ALU.add)
                    sV.inc(inst)
                    vector.drain()
                tag(f"fin{tc}_sev", sV)

                @block.sync
                def _(sync, tc=tc, par=par):
                    wait(sync, f"fin{tc}_sev")
                    inst = sync.dma_start(out=d_out[tc * 128:(tc + 1) * 128, :],
                                          in_=otile[:, par * 1024:(par + 1) * 1024])
                    dO[par].inc(inst, 16)
                tag(f"fin{tc}_out", dO[par])

            @block.sync
            def _(sync):
                sync.wait_ge(_dO0, C["dO0"])
                sync.wait_ge(_dO1, C["dO1"])
        ESm.close()

    f = nc.m.functions[0]
    blocks = list(f.blocks)
    head, tail = blocks[0], blocks[-1]
    mid = blocks[1:-1]
    order = {"EngineType.SP": 1, "EngineType.Pool": 2, "EngineType.PE": 3,
             "EngineType.DVE": 4, "EngineType.Activation": 5}

    def eng_of(b):
        for i in b.instructions:
            if str(i.engine) != "EngineType.Unassigned":
                return order.get(str(i.engine), 0)
        return 0

    f.blocks = [head] + sorted(mid, key=eng_of) + [tail]
    return nc


def _prep_inputs(kw):
    x = np.ascontiguousarray(np.asarray(kw["L0_attn_x"], np.float32))
    W_in = np.asarray(kw["L0_attn_in_proj_w"], np.float32)
    conv_w = np.asarray(kw["L0_attn_conv_w"], np.float32)
    conv_b = np.asarray(kw["L0_attn_conv_b"], np.float32)
    norm_w = np.asarray(kw["L0_attn_norm_w"], np.float32)
    W_out = np.asarray(kw["L0_attn_out_proj_w"], np.float32)
    dt_bias = np.asarray(kw["L0_attn_dt_bias"], np.float32)
    A_log = np.asarray(kw["L0_attn_A_log"], np.float32)
    D_param = np.asarray(kw["L0_attn_D_param"], np.float32)
    gate_w = np.asarray(kw["L0_ffn_gate_w"], np.float32)
    beta = np.asarray(kw["L0_ffn_beta"], np.float32)
    com = dict(
        w_inT=np.ascontiguousarray(W_in.T),
        w_outT=np.ascontiguousarray(W_out.T),
        gwT=np.ascontiguousarray(gate_w.T),
        convw=np.ascontiguousarray(conv_w[:, 0, :].reshape(18, 128, 4).transpose(1, 0, 2).reshape(128, 72)),
        convb=np.ascontiguousarray(conv_b.reshape(18, 128).T),
        normw=np.ascontiguousarray(norm_w.reshape(16, 128).T),
        dpar=np.ascontiguousarray(np.tile(D_param[None, :], (128, 1))),
        dtb=np.ascontiguousarray(dt_bias[:, None]),
        alog=np.ascontiguousarray(A_log[:, None]),
        beta_t=np.ascontiguousarray(np.tile(beta[None, :], (128, 1))),
        w_gate=np.ascontiguousarray(np.concatenate(
            [np.asarray(kw["L0_ffn_gate_proj_w"], np.float32),
             np.asarray(kw["L0_ffn_shared_gate_w"], np.float32)], axis=0)),
        w_up=np.ascontiguousarray(np.concatenate(
            [np.asarray(kw["L0_ffn_up_proj_w"], np.float32),
             np.asarray(kw["L0_ffn_shared_up_w"], np.float32)], axis=0)),
        w_down=np.ascontiguousarray(np.concatenate(
            [np.asarray(kw["L0_ffn_down_proj_w"], np.float32),
             np.asarray(kw["L0_ffn_shared_down_w"], np.float32)], axis=0)),
    )
    return x, com


def kernel(**kw):
    x, com = _prep_inputs(kw)
    B, L, D = x.shape
    NT = L // 2
    nc = build_nc(NT_MAIN=NT, NT_PREF=NT)
    in_maps = []
    for core in range(8):
        b, h = core // 2, core % 2
        m = dict(com)
        m["x_main"] = np.ascontiguousarray(x[b, h * NT:(h + 1) * NT])
        m["x_pref"] = np.ascontiguousarray(x[b, 0:NT]) if h == 1 else np.zeros((NT, D), np.float32)
        m["pref_flag"] = np.full((128, 1), float(h), np.float32)
        in_maps.append(m)
    res = run_bass_kernel_spmd(nc, in_maps, core_ids=list(range(8)))
    out = np.empty((B, L, D), np.float32)
    for core in range(8):
        b, h = core // 2, core % 2
        out[b, h * NT:(h + 1) * NT] = res.results[core]["out"]
    return out


# version 09
# speedup vs baseline: 1.3512x; 1.0062x over previous
import contextlib
import numpy as np
import concourse.bass as bass
import concourse.mybir as mybir
from concourse.bass_utils import run_bass_kernel_spmd

F32 = mybir.dt.float32
BF16 = mybir.dt.bfloat16
AF = mybir.ActivationFunctionType
ALU = mybir.AluOpType
AX = mybir.AxisListType.X

D_MODEL = 1024
D_STATE = 128
HEADDIM = 64
D_INNER = 2048
NHEADS = 32
CONV_DIM = 2304
D_IN_PROJ = 4384
EPS = 1e-5
Q = 128
SL = 512
NEXP = 8
M_FF = 2048

OC_X0 = 16
OC_B = 32
OC_C = 33
OC_DT = 34


def build_nc(NT_MAIN=2048, NT_PREF=2048):
    assert NT_MAIN % SL == 0 and NT_PREF % SL == 0 and NT_PREF >= SL
    n_sl_pref = NT_PREF // SL
    n_sl_main = NT_MAIN // SL
    nch_main = NT_MAIN // Q
    nch_pref = NT_PREF // Q
    NTOT = NT_PREF + NT_MAIN
    n_tc = NT_MAIN // 128
    n_tt = NT_MAIN // 512

    nc = bass.Bass("TRN2", target_bir_lowering=False)

    d_xm = nc.dram_tensor("x_main", [NT_MAIN, D_MODEL], F32, kind="ExternalInput").ap()
    d_xp = nc.dram_tensor("x_pref", [NT_PREF, D_MODEL], F32, kind="ExternalInput").ap()
    d_flag = nc.dram_tensor("pref_flag", [128, 1], F32, kind="ExternalInput").ap()
    d_winT = nc.dram_tensor("w_inT", [D_MODEL, D_IN_PROJ], F32, kind="ExternalInput").ap()
    d_woutT = nc.dram_tensor("w_outT", [D_INNER, D_MODEL], F32, kind="ExternalInput").ap()
    d_gwT = nc.dram_tensor("gwT", [D_MODEL, NEXP], F32, kind="ExternalInput").ap()
    d_convw = nc.dram_tensor("convw", [128, 18 * 4], F32, kind="ExternalInput").ap()
    d_convb = nc.dram_tensor("convb", [128, 18], F32, kind="ExternalInput").ap()
    d_normw = nc.dram_tensor("normw", [128, 16], F32, kind="ExternalInput").ap()
    d_dpar = nc.dram_tensor("dpar", [128, NHEADS], F32, kind="ExternalInput").ap()
    d_dtb = nc.dram_tensor("dtb", [NHEADS, 1], F32, kind="ExternalInput").ap()
    d_alog = nc.dram_tensor("alog", [NHEADS, 1], F32, kind="ExternalInput").ap()
    d_beta = nc.dram_tensor("beta_t", [128, NEXP], F32, kind="ExternalInput").ap()
    d_wg = nc.dram_tensor("w_gate", [NEXP + 1, D_MODEL, M_FF], F32, kind="ExternalInput").ap()
    d_wu = nc.dram_tensor("w_up", [NEXP + 1, D_MODEL, M_FF], F32, kind="ExternalInput").ap()
    d_wd = nc.dram_tensor("w_down", [NEXP + 1, M_FF, D_MODEL], F32, kind="ExternalInput").ap()
    d_out = nc.dram_tensor("out", [NT_MAIN, D_MODEL], F32, kind="ExternalOutput").ap()
    d_y4 = nc.dram_tensor("y4scr", [nch_main * 128, D_INNER], BF16, kind="Internal").ap()

    C = {}
    H = {}

    class Sem:
        def __init__(self, handle, name):
            self.h, self.name = handle, name
            C[name] = 0

        def inc(self, instr, n=1):
            instr.then_inc(self.h, n)
            C[self.name] += n

    tags = {}

    ES = contextlib.ExitStack()
    ident_f = ES.enter_context(nc.sbuf_tensor("ident_f", [128, 128], F32))
    ident_b = ES.enter_context(nc.sbuf_tensor("ident_b", [128, 128], BF16))
    ones_row = ES.enter_context(nc.sbuf_tensor("ones_row", [1, 128], F32))
    eps_col = ES.enter_context(nc.sbuf_tensor("eps_col", [128, 1], F32))
    zero_row = ES.enter_context(nc.sbuf_tensor("zero_row", [1, 512], BF16))
    ones_row_b = ES.enter_context(nc.sbuf_tensor("ones_row_b", [1, 128], BF16))
    ones_col_b = ES.enter_context(nc.sbuf_tensor("ones_col_b", [128, 1], BF16))
    ones32 = ES.enter_context(nc.sbuf_tensor("ones32", [32, SL], F32))
    mask01 = ES.enter_context(nc.sbuf_tensor("mask01", [128, 128], F32))
    maskneg8 = ES.enter_context(nc.sbuf_tensor("maskneg8", [128, 8 * 128], BF16))
    convw_sb = ES.enter_context(nc.sbuf_tensor("convw_sb", [128, 18 * 4], F32))
    convb_sb = ES.enter_context(nc.sbuf_tensor("convb_sb", [128, 18], F32))
    normw_sb = ES.enter_context(nc.sbuf_tensor("normw_sb", [128, 16], F32))
    dpar_sb = ES.enter_context(nc.sbuf_tensor("dpar_sb", [128, NHEADS], F32))
    dtb_sb = ES.enter_context(nc.sbuf_tensor("dtb_sb", [NHEADS, 1], F32))
    alog_sb = ES.enter_context(nc.sbuf_tensor("alog_sb", [NHEADS, 1], F32))
    acol_sb = ES.enter_context(nc.sbuf_tensor("acol_sb", [NHEADS, 1], F32))
    beta_sb = ES.enter_context(nc.sbuf_tensor("beta_sb", [128, NEXP], F32))
    gwT_sb = ES.enter_context(nc.sbuf_tensor("gwT_sb", [128, 8 * NEXP], BF16))
    flag_sb = ES.enter_context(nc.sbuf_tensor("flag_sb", [128, 1], F32))
    xn_fm = ES.enter_context(nc.sbuf_tensor("xn_fm", [128, 8 * NT_MAIN], BF16))
    c_tm = ES.enter_context(nc.sbuf_tensor("c_tm", [128, 16 * NEXP], F32))
    ps_big = ES.enter_context(nc.psum_tensor("ps_big", [128, 2048], F32))
    ps_e = ES.enter_context(nc.psum_tensor("ps_e", [128, 1024], F32))
    ps_en2 = ES.enter_context(nc.psum_tensor("ps_en2", [128, 1024], F32))
    _dS = ES.enter_context(nc.semaphore("dS"))
    _dG = ES.enter_context(nc.semaphore("dG"))
    _dC = ES.enter_context(nc.semaphore("dC"))
    _dX0 = ES.enter_context(nc.semaphore("dX0"))
    _dX1 = ES.enter_context(nc.semaphore("dX1"))
    _dY = ES.enter_context(nc.semaphore("dY"))
    _dF0 = ES.enter_context(nc.semaphore("dF0"))
    _dF1 = ES.enter_context(nc.semaphore("dF1"))
    _dO0 = ES.enter_context(nc.semaphore("dO0"))
    _dO1 = ES.enter_context(nc.semaphore("dO1"))
    _dWc = ES.enter_context(nc.semaphore("dWc"))
    _dI0 = ES.enter_context(nc.semaphore("dI0"))
    _dI1 = ES.enter_context(nc.semaphore("dI1"))
    _dM0 = ES.enter_context(nc.semaphore("dM0"))
    _dM1 = ES.enter_context(nc.semaphore("dM1"))
    _dD = ES.enter_context(nc.semaphore("dD"))
    _dWo = ES.enter_context(nc.semaphore("dWo"))
    _sT = ES.enter_context(nc.semaphore("sT"))
    _sV = ES.enter_context(nc.semaphore("sV"))
    _sS = ES.enter_context(nc.semaphore("sS"))
    _sP = ES.enter_context(nc.semaphore("sP"))
    block = ES.enter_context(nc.Block())
    with ES:
     if True:
        dS, dG = Sem(_dS, "dS"), Sem(_dG, "dG")
        dC, dY, dWc, dD, dWo = Sem(_dC, "dC"), Sem(_dY, "dY"), Sem(_dWc, "dWc"), Sem(_dD, "dD"), Sem(_dWo, "dWo")
        dX = [Sem(_dX0, "dX0"), Sem(_dX1, "dX1")]
        dF = [Sem(_dF0, "dF0"), Sem(_dF1, "dF1")]
        dO = [Sem(_dO0, "dO0"), Sem(_dO1, "dO1")]
        dI = [Sem(_dI0, "dI0"), Sem(_dI1, "dI1")]
        dM = [Sem(_dM0, "dM0"), Sem(_dM1, "dM1")]
        sT, sV, sS, sP = Sem(_sT, "sT"), Sem(_sV, "sV"), Sem(_sS, "sS"), Sem(_sP, "sP")
        SEMS = {s.name: s for s in [dS, dG, dC, dY, dWc, dD, dWo] + dX + dF + dO + dI + dM + [sT, sV, sS, sP]}

        def tag(name, sem):
            tags[name] = (sem.name, C[sem.name])

        def wait(eng, name):
            s, v = tags[name]
            if v > 0:
                eng.wait_ge(SEMS[s].h, v)

        @block.sync
        def _(sync):
            for dst, src in [(convw_sb, d_convw), (convb_sb, d_convb), (normw_sb, d_normw),
                             (dpar_sb, d_dpar), (dtb_sb, d_dtb), (alog_sb, d_alog),
                             (beta_sb, d_beta), (flag_sb, d_flag)]:
                inst = sync.dma_start(out=dst[:], in_=src[:])
                dC.inc(inst, 16)
        tag("p0_dS", dC)

        @block.gpsimd
        def _(gpsimd):
            inst = gpsimd.dma_start(
                out=gwT_sb[:].rearrange("p (dc e) -> p dc e", dc=8),
                in_=d_gwT.rearrange("(dc p) e -> p dc e", p=128))
            dWc.inc(inst, 16)
            gpsimd.memset(ident_f[:], 0.0)
            gpsimd.memset(ident_b[:], 0.0)
            gpsimd.memset(mask01[:], 0.0)
            gpsimd.memset(maskneg8[:], 0.0)
            gpsimd.drain()
            gpsimd.affine_select(out=ident_f[:], in_=ident_f[:], compare_op=ALU.not_equal,
                                 fill=1.0, base=0, pattern=[[-1, 128]], channel_multiplier=1)
            gpsimd.affine_select(out=ident_b[:], in_=ident_b[:], compare_op=ALU.not_equal,
                                 fill=1.0, base=0, pattern=[[-1, 128]], channel_multiplier=1)
            gpsimd.affine_select(out=mask01[:], in_=mask01[:], compare_op=ALU.is_gt,
                                 fill=1.0, base=0, pattern=[[-1, 128]], channel_multiplier=1)
            gpsimd.drain()
            for g8 in range(8):
                gpsimd.affine_select(out=maskneg8[:, g8 * 128:(g8 + 1) * 128],
                                     in_=maskneg8[:, g8 * 128:(g8 + 1) * 128],
                                     compare_op=ALU.is_ge, fill=-1e30, base=0,
                                     pattern=[[1, 128]], channel_multiplier=-1)
            gpsimd.drain()
            inst = gpsimd.engine_nop()
            sP.inc(inst)
        tag("p0_consts", sP)
        tag("p0_dG", dWc)

        @block.vector
        def _(vector):
            vector.memset(ones_row[:], 1.0)
            vector.memset(eps_col[:], EPS)
            vector.memset(zero_row[:], 0.0)
            vector.memset(ones_row_b[:], 1.0)
            vector.memset(ones_col_b[:], 1.0)
            vector.memset(ones32[:], 1.0)
            inst = vector.drain()
            sV.inc(inst)
        tag("p0_v", sV)

        @block.scalar
        def _(scalar):
            wait(scalar, "p0_dS")
            inst = scalar.activation(acol_sb[:], alog_sb[:], AF.Exp)
            sS.inc(inst)
            scalar.drain()
        tag("p0_s", sS)

        @block.vector
        def _(vector):
            wait(vector, "p0_s")
            inst = vector.tensor_scalar_mul(acol_sb[:], acol_sb[:], -1.0)
            sV.inc(inst)
            vector.drain()
        tag("p0_acol", sV)

        ES1 = contextlib.ExitStack()
        sel32 = ES1.enter_context(nc.sbuf_tensor("sel32", [32, 32 * 128], F32))
        S_sb = ES1.enter_context(nc.sbuf_tensor("S_sb", [128, D_INNER], BF16))
        S_t1 = ES1.enter_context(nc.sbuf_tensor("S_t1", [128, D_INNER], BF16))
        Pa = ES1.enter_context(nc.sbuf_tensor("Pa_sl", [32, SL + 1], F32))
        nPa = ES1.enter_context(nc.sbuf_tensor("nPa_sl", [32, SL + 1], F32))
        x_tile = ES1.enter_context(nc.sbuf_tensor("x_tile", [128, 2 * D_MODEL], F32))
        xn_tm = ES1.enter_context(nc.sbuf_tensor("xn_tm", [128, 2 * D_MODEL], F32))
        lnst = ES1.enter_context(nc.sbuf_tensor("lnst", [128, 16], F32))
        zs = ES1.enter_context(nc.sbuf_tensor("zs", [128, 16 * SL], BF16))
        xbc = ES1.enter_context(nc.sbuf_tensor("xbc", [128, 18 * (SL + 3)], BF16))
        xsilu = ES1.enter_context(nc.sbuf_tensor("xsilu", [128, 16 * SL], BF16))
        B_sl = ES1.enter_context(nc.sbuf_tensor("B_sl", [128, SL], BF16))
        C_sl = ES1.enter_context(nc.sbuf_tensor("C_sl", [128, SL], BF16))
        cva = ES1.enter_context(nc.sbuf_tensor("cva", [128, 2 * SL], BF16))
        cvb = ES1.enter_context(nc.sbuf_tensor("cvb", [128, 2 * SL], BF16))
        dts = ES1.enter_context(nc.sbuf_tensor("dts", [32, 4 * SL], F32))
        win_buf = ES1.enter_context(nc.sbuf_tensor("win_buf", [128, 2 * 1024], BF16))
        Xd = ES1.enter_context(nc.sbuf_tensor("Xd", [128, D_INNER], BF16))
        Xw = ES1.enter_context(nc.sbuf_tensor("Xw", [128, D_INNER], BF16))
        XDt = ES1.enter_context(nc.sbuf_tensor("XDt", [128, D_INNER], BF16))
        small_tm = ES1.enter_context(nc.sbuf_tensor("small_tm", [128, 128], F32))
        rr_all = ES1.enter_context(nc.sbuf_tensor("rr_all", [32, SL], F32))
        npp_all = ES1.enter_context(nc.sbuf_tensor("npp_all", [32, SL], F32))
        zs_tm = ES1.enter_context(nc.sbuf_tensor("zs_tm", [128, D_INNER], BF16))
        dec_sm = ES1.enter_context(nc.sbuf_tensor("dec_sm", [32, 8], F32))
        decrow = ES1.enter_context(nc.sbuf_tensor("decrow", [4, 32], F32))
        B_tm = ES1.enter_context(nc.sbuf_tensor("B_tm", [128, 128], BF16))
        Gt_m = ES1.enter_context(nc.sbuf_tensor("Gt_m", [128, 128], BF16))
        Eex = ES1.enter_context(nc.sbuf_tensor("Eex", [128, 2 * 1024], BF16))
        Mt = ES1.enter_context(nc.sbuf_tensor("Mt", [128, 2 * 1024], BF16))
        y2 = ES1.enter_context(nc.sbuf_tensor("y2", [128, D_INNER], BF16))
        y3 = ES1.enter_context(nc.sbuf_tensor("y3", [128, D_INNER], BF16))
        y4asm = ES1.enter_context(nc.sbuf_tensor("y4asm", [128, D_INNER], BF16))
        nrm_sm = ES1.enter_context(nc.sbuf_tensor("nrm_sm", [128, 8], F32))
        with ES1:
         if True:
            @block.gpsimd
            def _(gpsimd):
                wait(gpsimd, "p0_consts")
                gpsimd.memset(sel32[:], 0.0)
                gpsimd.drain()
                gpsimd.affine_select(out=sel32[:].rearrange("k (h m) -> k h m", h=32),
                                     in_=sel32[:].rearrange("k (h m) -> k h m", h=32),
                                     compare_op=ALU.not_equal, fill=1.0, base=0,
                                     pattern=[[-1, 32], [0, 128]], channel_multiplier=1)
                gpsimd.drain()
                inst = gpsimd.engine_nop()
                sP.inc(inst)
            tag("sel_done", sP)

            @block.vector
            def _(vector):
                for cc in range(18):
                    vector.memset(xbc[:, cc * (SL + 3):cc * (SL + 3) + 3], 0.0)
                vector.memset(S_sb[:], 0.0)
                vector.memset(Pa[:, 0:1], 0.0)
                vector.memset(nPa[:, 0:1], 0.0)
                inst = vector.drain()
                sV.inc(inst)
            tag("tails_s0", sV)

            def xbc_t(cc):
                return xbc[:, cc * (SL + 3):(cc + 1) * (SL + 3)]

            ipg = 0
            n_slices = n_sl_pref + n_sl_main
            NTT = SL // 128

            for sl_i in range(n_slices):
                is_pref = sl_i < n_sl_pref
                goff = sl_i * SL
                moff = goff - NT_PREF
                x_src = d_xp if is_pref else d_xm
                x_off = goff if is_pref else moff
                P_ = f"s{sl_i}_"
                prev_last = goff // Q - 1

                def xn_cols(dc, c0, cn, is_pref=is_pref, moff=moff):
                    if is_pref:
                        return zs[:, dc * SL + c0: dc * SL + c0 + cn]
                    return xn_fm[:, dc * NT_MAIN + moff + c0: dc * NT_MAIN + moff + c0 + cn]

                for tt in range(NTT):
                    par = tt % 2
                    xt = x_tile[:, par * D_MODEL:(par + 1) * D_MODEL]
                    xnt = xn_tm[:, par * D_MODEL:(par + 1) * D_MODEL]
                    ls = par * 8
                    gtt = sl_i * NTT + tt

                    @block.sync
                    def _(sync, tt=tt, xt=xt, x_src=x_src, x_off=x_off, P_=P_, gtt=gtt):
                        if gtt >= 2:
                            wait(sync, f"g{gtt - 2}_lnx")
                        inst = sync.dma_start(out=xt[:],
                                              in_=x_src[x_off + tt * 128: x_off + (tt + 1) * 128, :])
                        dX[gtt % 2].inc(inst, 16)
                    tag(f"g{gtt}_xld", dX[gtt % 2])

                    @block.vector
                    def _(vector, xt=xt, ls=ls, gtt=gtt):
                        wait(vector, f"g{gtt}_xld")
                        vector.reduce_sum(lnst[:, ls:ls + 1], xt[:], axis=AX)
                        vector.drain()
                        inst = vector.tensor_scalar_mul(lnst[:, ls + 1:ls + 2], lnst[:, ls:ls + 1], 1.0 / D_MODEL)
                        sV.inc(inst)
                        vector.drain()
                    tag(f"g{gtt}_mean", sV)

                    @block.scalar
                    def _(scalar, xt=xt, ls=ls, gtt=gtt):
                        wait(scalar, f"g{gtt}_xld")
                        if gtt >= 2:
                            wait(scalar, f"g{gtt - 2}_var")
                        inst = scalar.activation(xn_tm[:, (ls // 8) * D_MODEL:(ls // 8 + 1) * D_MODEL],
                                                 xt[:], AF.Square, accum_out=lnst[:, ls + 2:ls + 3])
                        sS.inc(inst)
                        scalar.drain()
                    tag(f"g{gtt}_sumsq", sS)

                    @block.vector
                    def _(vector, ls=ls, gtt=gtt):
                        wait(vector, f"g{gtt}_sumsq")
                        wait(vector, f"g{gtt}_mean")
                        vector.tensor_scalar(lnst[:, ls + 3:ls + 4], lnst[:, ls + 1:ls + 2],
                                             lnst[:, ls + 1:ls + 2], None, ALU.mult)
                        vector.drain()
                        inst = vector.scalar_tensor_tensor(lnst[:, ls + 4:ls + 5], lnst[:, ls + 2:ls + 3],
                                                           1.0 / D_MODEL, lnst[:, ls + 3:ls + 4],
                                                           ALU.mult, ALU.subtract)
                        sV.inc(inst)
                        vector.drain()
                    tag(f"g{gtt}_var", sV)

                    @block.scalar
                    def _(scalar, ls=ls, gtt=gtt):
                        wait(scalar, f"g{gtt}_var")
                        scalar.activation(lnst[:, ls + 5:ls + 6], lnst[:, ls + 4:ls + 5], AF.Ln, bias=eps_col[:])
                        scalar.drain()
                        inst = scalar.activation(lnst[:, ls + 6:ls + 7], lnst[:, ls + 5:ls + 6], AF.Exp, scale=-0.5)
                        sS.inc(inst)
                        scalar.drain()
                    tag(f"g{gtt}_sd", sS)

                    @block.vector
                    def _(vector, xt=xt, xnt=xnt, ls=ls, gtt=gtt):
                        wait(vector, f"g{gtt}_sd")
                        if gtt >= 2:
                            wait(vector, f"g{gtt - 2}_xntr")
                        inst = vector.tensor_scalar(xnt[:], xt[:], lnst[:, ls + 1:ls + 2],
                                                    lnst[:, ls + 6:ls + 7], ALU.subtract, ALU.mult)
                        sV.inc(inst)
                        vector.drain()
                    tag(f"g{gtt}_lnx", sV)

                    @block.tensor
                    def _(tensor, xnt=xnt, gtt=gtt):
                        wait(tensor, f"g{gtt}_lnx")
                        wait(tensor, "p0_consts")
                        if gtt >= 1:
                            wait(tensor, f"g{gtt - 1}_xnev")
                        for dc in range(8):
                            inst = tensor.transpose(ps_e[:, dc * 128:(dc + 1) * 128],
                                                    xnt[:, dc * 128:(dc + 1) * 128], ident_f[:])
                        sT.inc(inst)
                    tag(f"g{gtt}_xntr", sT)

                    @block.scalar
                    def _(scalar, tt=tt, gtt=gtt, is_pref=is_pref, moff=moff):
                        wait(scalar, f"g{gtt}_xntr")
                        if is_pref:
                            dst = zs[:].rearrange("p (dc t) -> p dc t", dc=16)[:, 0:8, tt * 128:(tt + 1) * 128]
                        else:
                            dst = xn_fm[:].rearrange("p (dc t) -> p dc t", dc=8)[:, :, moff + tt * 128: moff + (tt + 1) * 128]
                        inst = scalar.copy(dst, ps_e[:].rearrange("p (dc i) -> p dc i", dc=8))
                        sS.inc(inst)
                        scalar.drain()
                    tag(f"g{gtt}_xnev", sS)
                tag(f"{P_}xn_done", sS)

                if sl_i > 0:
                    @block.vector
                    def _(vector):
                        wait(vector, f"s{sl_i - 1}_convdone")
                        for cc in range(18):
                            vector.tensor_copy(xbc_t(cc)[:, 0:3], xbc_t(cc)[:, SL:SL + 3])
                        inst = vector.drain()
                        sV.inc(inst)
                    tag(f"{P_}tails", sV)
                else:
                    tags[f"{P_}tails"] = tags["tails_s0"]

                cc_list = list(range(17)) if is_pref else list(range(18))

                def emit_conv(cc, P_=P_, sl_i=sl_i, prev_last=prev_last):
                    vpar = cc % 2
                    cvx = cva if vpar == 0 else cvb

                    def conv_body(eng, sem, stt_ok, cc=cc, P_=P_, sl_i=sl_i, cvx=cvx, vpar=vpar):
                        wait(eng, H[f"{P_}ip_{OC_X0 + cc}"])
                        if cc > 1:
                            wait(eng, f"{P_}cvs{cc - 2}")
                        elif sl_i > 0:
                            lastcc = (16 - vpar) if sl_i <= n_sl_pref else (16 + vpar)
                            wait(eng, f"s{sl_i - 1}_cvs{lastcc}")
                        srcp = xbc_t(cc)
                        w = lambda k: convw_sb[:, cc * 4 + k: cc * 4 + k + 1]
                        if stt_ok:
                            eng.tensor_scalar(cvx[:, 0:SL], srcp[:, 0:SL], w(0), None, ALU.mult)
                            eng.drain()
                            eng.scalar_tensor_tensor(cvx[:, SL:2 * SL], srcp[:, 1:1 + SL], w(1), cvx[:, 0:SL], ALU.mult, ALU.add)
                            eng.drain()
                            eng.scalar_tensor_tensor(cvx[:, 0:SL], srcp[:, 2:2 + SL], w(2), cvx[:, SL:2 * SL], ALU.mult, ALU.add)
                            eng.drain()
                            inst = eng.scalar_tensor_tensor(cvx[:, SL:2 * SL], srcp[:, 3:3 + SL], w(3), cvx[:, 0:SL], ALU.mult, ALU.add)
                            sem.inc(inst)
                            eng.drain()
                        else:
                            eng.tensor_scalar(cvx[:, SL:2 * SL], srcp[:, 0:SL], w(0), None, ALU.mult)
                            eng.drain()
                            for k in (1, 2, 3):
                                eng.tensor_scalar(cvx[:, 0:SL], srcp[:, k:k + SL], w(k), None, ALU.mult)
                                eng.drain()
                                inst = eng.tensor_tensor(cvx[:, SL:2 * SL], cvx[:, SL:2 * SL], cvx[:, 0:SL], ALU.add)
                                eng.drain()
                            sem.inc(inst)
                            eng.drain()

                    @block.vector
                    def _(vector, conv_body=conv_body):
                        conv_body(vector, sV, True)
                    tag(f"{P_}cv{cc}", sV)

                    @block.scalar
                    def _(scalar, cc=cc, P_=P_, prev_last=prev_last, cvx=cvx):
                        wait(scalar, f"{P_}cv{cc}")
                        if prev_last >= 0:
                            wait(scalar, f"c{prev_last}_sup")
                        if cc < 16:
                            dst = xsilu[:, cc * SL:(cc + 1) * SL]
                        elif cc == 16:
                            dst = B_sl[:]
                        else:
                            dst = C_sl[:]
                        inst = scalar.activation(dst, cvx[:, SL:2 * SL], AF.Silu, bias=convb_sb[:, cc:cc + 1])
                        sS.inc(inst)
                        scalar.drain()
                    tag(f"{P_}cvs{cc}", sS)

                def emit_dt_chain(P_=P_, sl_i=sl_i, prev_last=prev_last, is_pref=is_pref):
                    @block.vector
                    def _(vector, P_=P_):
                        wait(vector, H[f"{P_}ip_{OC_DT}"])
                        wait(vector, "p0_acol")
                        inst = vector.tensor_scalar_add(dts[:, SL:2 * SL], dts[:, 0:SL], dtb_sb[:])
                        sV.inc(inst)
                        vector.drain()
                    tag(f"{P_}dt0", sV)

                    @block.scalar
                    def _(scalar, P_=P_):
                        wait(scalar, f"{P_}dt0")
                        scalar.activation(dts[:, 2 * SL:3 * SL], dts[:, SL:2 * SL], AF.Abs)
                        scalar.drain()
                        inst = scalar.activation(dts[:, 2 * SL:3 * SL], dts[:, 2 * SL:3 * SL], AF.Exp, scale=-1.0)
                        sS.inc(inst)
                        scalar.drain()
                    tag(f"{P_}dt1", sS)

                    @block.vector
                    def _(vector, P_=P_):
                        wait(vector, f"{P_}dt1")
                        inst = vector.tensor_scalar_add(dts[:, 2 * SL:3 * SL], dts[:, 2 * SL:3 * SL], 1.0)
                        sV.inc(inst)
                        vector.drain()
                    tag(f"{P_}dt2", sV)

                    @block.scalar
                    def _(scalar, P_=P_):
                        wait(scalar, f"{P_}dt2")
                        inst = scalar.activation(dts[:, 2 * SL:3 * SL], dts[:, 2 * SL:3 * SL], AF.Ln)
                        sS.inc(inst)
                        scalar.drain()
                    tag(f"{P_}dt3", sS)

                    @block.vector
                    def _(vector, P_=P_, sl_i=sl_i, prev_last=prev_last):
                        wait(vector, f"{P_}dt3")
                        if prev_last >= 0:
                            wait(vector, f"c{prev_last}_sup")
                        if sl_i > 0:
                            vector.tensor_copy(Pa[:, 0:1], Pa[:, SL:SL + 1])
                            vector.drain()
                        vector.tensor_scalar_max(dts[:, 3 * SL:4 * SL], dts[:, SL:2 * SL], 0.0)
                        vector.drain()
                        vector.tensor_add(dts[:, 0:SL], dts[:, 3 * SL:4 * SL], dts[:, 2 * SL:3 * SL])
                        vector.drain()
                        vector.tensor_scalar_mul(dts[:, 3 * SL:4 * SL], dts[:, 0:SL], acol_sb[:])
                        vector.drain()
                        vector.tensor_tensor_scan(Pa[:, 1:1 + SL], ones32[:],
                                                  dts[:, 3 * SL:4 * SL], Pa[:, 0:1],
                                                  ALU.mult, ALU.add)
                        vector.drain()
                        inst = vector.tensor_scalar_mul(nPa[:, 0:1 + SL], Pa[:, 0:1 + SL], -1.0)
                        sV.inc(inst)
                        vector.drain()
                    tag(f"{P_}pa", sV)

                    if not is_pref:
                        @block.vector
                        def _(vector, P_=P_):
                            wait(vector, f"{P_}pa")
                            npc = bass.AP(tensor=nPa[:].tensor, offset=0,
                                          ap=[[SL + 1, 32], [128, SL // 128]]).unsqueeze(2).broadcast_to([32, SL // 128, 128])
                            inst = vector.tensor_tensor(
                                npp_all[:].rearrange("h (c i) -> h c i", c=SL // 128),
                                ones32[:].rearrange("h (c i) -> h c i", c=SL // 128), npc, ALU.mult)
                            sV.inc(inst)
                            vector.drain()
                        tag(f"{P_}nppall", sV)

                    @block.vector
                    def _(vector, P_=P_):
                        wait(vector, f"{P_}pa")
                        pe = bass.AP(tensor=Pa[:].tensor, offset=128, ap=[[SL + 1, 32], [128, SL // 128]])
                        pb = bass.AP(tensor=Pa[:].tensor, offset=0, ap=[[SL + 1, 32], [128, SL // 128]])
                        inst = vector.tensor_tensor(dec_sm[:, 0:SL // 128], pe, pb, ALU.subtract)
                        sV.inc(inst)
                        vector.drain()
                    tag(f"{P_}dec0", sV)

                oc_list = [OC_DT] + list(range(OC_X0, OC_DT)) + ([] if is_pref else list(range(OC_X0)))
                for j_loc, oc in enumerate(oc_list):
                    par = ipg % 2
                    wb = win_buf[:, par * 1024:(par + 1) * 1024]
                    psd = ps_en2[:, 0:512] if par == 0 else ps_en2[:, 512:1024]
                    jg = ipg

                    @block.gpsimd
                    def _(gpsimd, oc=oc, wb=wb, jg=jg):
                        if jg >= 2:
                            wait(gpsimd, f"ip{jg - 2}_mm")
                        ncols = 128 if oc != OC_DT else 32
                        inst = gpsimd.dma_start(
                            out=wb[:].rearrange("p (ic o) -> p ic o", ic=8)[:, :, 0:ncols],
                            in_=d_winT.rearrange("(ic p) o -> p ic o", p=128)[:, :, oc * 128: oc * 128 + ncols])
                        dI[jg % 2].inc(inst, 16)
                    tag(f"ip{jg}_w", dI[jg % 2])

                    @block.tensor
                    def _(tensor, oc=oc, wb=wb, psd=psd, jg=jg, j_loc=j_loc, P_=P_,
                          prev_last=prev_last, sl_i=sl_i):
                        wait(tensor, f"ip{jg}_w")
                        wait(tensor, f"{P_}xn_done")
                        if jg >= 2:
                            wait(tensor, f"ip{jg - 2}_ev")
                        if j_loc < 2 and prev_last >= 0:
                            wait(tensor, f"c{prev_last}_sup")
                        ncols = 128 if oc != OC_DT else 32
                        for ic in range(8):
                            inst = tensor.matmul(psd[0:ncols, :],
                                                 wb[:].rearrange("p (ic o) -> p ic o", ic=8)[:, ic, 0:ncols],
                                                 xn_cols(ic, 0, SL), start=(ic == 0), stop=(ic == 7))
                        sT.inc(inst)
                    tag(f"ip{jg}_mm", sT)

                    if oc < OC_X0:
                        @block.scalar
                        def _(scalar, oc=oc, psd=psd, jg=jg):
                            wait(scalar, f"ip{jg}_mm")
                            inst = scalar.activation(zs[:, oc * SL:(oc + 1) * SL], psd[:], AF.Silu)
                            sS.inc(inst)
                            scalar.drain()
                        tag(f"ip{jg}_ev", sS)
                    elif oc == OC_DT:
                        @block.vector
                        def _(vector, psd=psd, jg=jg):
                            wait(vector, f"ip{jg}_mm")
                            inst = vector.tensor_copy(dts[:, 0:SL], psd[0:32, :])
                            sV.inc(inst)
                            vector.drain()
                        tag(f"ip{jg}_ev", sV)
                    else:
                        cc = oc - OC_X0

                        @block.scalar
                        def _(scalar, cc=cc, psd=psd, jg=jg, P_=P_):
                            wait(scalar, f"ip{jg}_mm")
                            wait(scalar, f"{P_}tails")
                            inst = scalar.copy(xbc_t(cc)[:, 3:3 + SL], psd[:])
                            sS.inc(inst)
                            scalar.drain()
                        tag(f"ip{jg}_ev", sS)
                    H[f"{P_}ip_{oc}"] = f"ip{jg}_ev"
                    ipg += 1

                cc_list = list(range(17)) if is_pref else list(range(18))
                for cc in cc_list:
                    vpar = cc % 2
                    cvx = cva if vpar == 0 else cvb

                    def conv_body(eng, sem, stt_ok, cc=cc, P_=P_, sl_i=sl_i, cvx=cvx, vpar=vpar):
                        wait(eng, H[f"{P_}ip_{OC_X0 + cc}"])
                        if cc > 1:
                            wait(eng, f"{P_}cvs{cc - 2}")
                        elif sl_i > 0:
                            lastcc = (16 - vpar) if sl_i <= n_sl_pref else (16 + vpar)
                            wait(eng, f"s{sl_i - 1}_cvs{lastcc}")
                        srcp = xbc_t(cc)
                        w = lambda k: convw_sb[:, cc * 4 + k: cc * 4 + k + 1]
                        if stt_ok:
                            eng.tensor_scalar(cvx[:, 0:SL], srcp[:, 0:SL], w(0), None, ALU.mult)
                            eng.drain()
                            eng.scalar_tensor_tensor(cvx[:, SL:2 * SL], srcp[:, 1:1 + SL], w(1), cvx[:, 0:SL], ALU.mult, ALU.add)
                            eng.drain()
                            eng.scalar_tensor_tensor(cvx[:, 0:SL], srcp[:, 2:2 + SL], w(2), cvx[:, SL:2 * SL], ALU.mult, ALU.add)
                            eng.drain()
                            inst = eng.scalar_tensor_tensor(cvx[:, SL:2 * SL], srcp[:, 3:3 + SL], w(3), cvx[:, 0:SL], ALU.mult, ALU.add)
                            sem.inc(inst)
                            eng.drain()
                        else:
                            eng.tensor_scalar(cvx[:, SL:2 * SL], srcp[:, 0:SL], w(0), None, ALU.mult)
                            eng.drain()
                            for k in (1, 2, 3):
                                eng.tensor_scalar(cvx[:, 0:SL], srcp[:, k:k + SL], w(k), None, ALU.mult)
                                eng.drain()
                                inst = eng.tensor_tensor(cvx[:, SL:2 * SL], cvx[:, SL:2 * SL], cvx[:, 0:SL], ALU.add)
                                eng.drain()
                            sem.inc(inst)
                            eng.drain()

                    @block.vector
                    def _(vector, conv_body=conv_body):
                        conv_body(vector, sV, True)
                    tag(f"{P_}cv{cc}", sV)

                    @block.scalar
                    def _(scalar, cc=cc, P_=P_, prev_last=prev_last, cvx=cvx):
                        wait(scalar, f"{P_}cv{cc}")
                        if prev_last >= 0:
                            wait(scalar, f"c{prev_last}_sup")
                        if cc < 16:
                            dst = xsilu[:, cc * SL:(cc + 1) * SL]
                        elif cc == 16:
                            dst = B_sl[:]
                        else:
                            dst = C_sl[:]
                        inst = scalar.activation(dst, cvx[:, SL:2 * SL], AF.Silu, bias=convb_sb[:, cc:cc + 1])
                        sS.inc(inst)
                        scalar.drain()
                    tag(f"{P_}cvs{cc}", sS)
                tag(f"{P_}convdone", sS)

                @block.vector
                def _(vector, P_=P_):
                    wait(vector, H[f"{P_}ip_{OC_DT}"])
                    wait(vector, "p0_acol")
                    inst = vector.tensor_scalar_add(dts[:, SL:2 * SL], dts[:, 0:SL], dtb_sb[:])
                    sV.inc(inst)
                    vector.drain()
                tag(f"{P_}dt0", sV)

                @block.scalar
                def _(scalar, P_=P_):
                    wait(scalar, f"{P_}dt0")
                    scalar.activation(dts[:, 2 * SL:3 * SL], dts[:, SL:2 * SL], AF.Abs)
                    scalar.drain()
                    inst = scalar.activation(dts[:, 2 * SL:3 * SL], dts[:, 2 * SL:3 * SL], AF.Exp, scale=-1.0)
                    sS.inc(inst)
                    scalar.drain()
                tag(f"{P_}dt1", sS)

                @block.vector
                def _(vector, P_=P_):
                    wait(vector, f"{P_}dt1")
                    inst = vector.tensor_scalar_add(dts[:, 2 * SL:3 * SL], dts[:, 2 * SL:3 * SL], 1.0)
                    sV.inc(inst)
                    vector.drain()
                tag(f"{P_}dt2", sV)

                @block.scalar
                def _(scalar, P_=P_):
                    wait(scalar, f"{P_}dt2")
                    inst = scalar.activation(dts[:, 2 * SL:3 * SL], dts[:, 2 * SL:3 * SL], AF.Ln)
                    sS.inc(inst)
                    scalar.drain()
                tag(f"{P_}dt3", sS)

                @block.vector
                def _(vector, P_=P_, sl_i=sl_i, prev_last=prev_last):
                    wait(vector, f"{P_}dt3")
                    if prev_last >= 0:
                        wait(vector, f"c{prev_last}_sup")
                    if sl_i > 0:
                        vector.tensor_copy(Pa[:, 0:1], Pa[:, SL:SL + 1])
                        vector.drain()
                    vector.tensor_scalar_max(dts[:, 3 * SL:4 * SL], dts[:, SL:2 * SL], 0.0)
                    vector.drain()
                    vector.tensor_add(dts[:, 0:SL], dts[:, 3 * SL:4 * SL], dts[:, 2 * SL:3 * SL])
                    vector.drain()
                    vector.tensor_scalar_mul(dts[:, 3 * SL:4 * SL], dts[:, 0:SL], acol_sb[:])
                    vector.drain()
                    vector.tensor_tensor_scan(Pa[:, 1:1 + SL], ones32[:],
                                              dts[:, 3 * SL:4 * SL], Pa[:, 0:1],
                                              ALU.mult, ALU.add)
                    vector.drain()
                    inst = vector.tensor_scalar_mul(nPa[:, 0:1 + SL], Pa[:, 0:1 + SL], -1.0)
                    sV.inc(inst)
                    vector.drain()
                tag(f"{P_}pa", sV)

                @block.vector
                def _(vector, P_=P_):
                    wait(vector, f"{P_}pa")
                    pe = bass.AP(tensor=Pa[:].tensor, offset=128, ap=[[SL + 1, 32], [128, SL // 128]])
                    pb = bass.AP(tensor=Pa[:].tensor, offset=0, ap=[[SL + 1, 32], [128, SL // 128]])
                    inst = vector.tensor_tensor(dec_sm[:, 0:SL // 128], pe, pb, ALU.subtract)
                    sV.inc(inst)
                    vector.drain()
                tag(f"{P_}dec0", sV)

                @block.scalar
                def _(scalar, P_=P_):
                    wait(scalar, f"{P_}dec0")
                    inst = scalar.activation(dec_sm[:, 4:4 + SL // 128], dec_sm[:, 0:SL // 128], AF.Exp)
                    sS.inc(inst)
                    scalar.drain()
                tag(f"{P_}dec1", sS)

                @block.tensor
                def _(tensor, P_=P_):
                    wait(tensor, f"{P_}dec1")
                    if goff > 0:
                        wait(tensor, f"c{goff // Q - 1}_sup")
                    inst = tensor.transpose(ps_en2[0:SL // 128, 448:480], dec_sm[:, 4:4 + SL // 128],
                                            ident_f[0:32, 0:32])
                    sT.inc(inst)
                tag(f"{P_}dectr", sT)

                @block.vector
                def _(vector, P_=P_):
                    wait(vector, f"{P_}dectr")
                    inst = vector.tensor_copy(decrow[0:SL // 128, :], ps_en2[0:SL // 128, 448:480])
                    sV.inc(inst)
                    vector.drain()
                tag(f"{P_}decrows", sV)

                @block.scalar
                def _(scalar, P_=P_, prev_last=prev_last):
                    wait(scalar, f"{P_}pa")
                    if prev_last >= 0:
                        wait(scalar, f"c{prev_last}_rrtm")
                    for c4 in range(SL // Q):
                        inst = scalar.activation(rr_all[:, c4 * 128:(c4 + 1) * 128],
                                                 Pa[:, c4 * 128 + 1:(c4 + 1) * 128 + 1], AF.Exp,
                                                 scale=-1.0,
                                                 bias=Pa[:, (c4 + 1) * 128:(c4 + 1) * 128 + 1])
                        scalar.drain()
                    sS.inc(inst)
                tag(f"{P_}rrall", sS)

                if not is_pref:
                    @block.vector
                    def _(vector, P_=P_, prev_last=prev_last):
                        wait(vector, f"{P_}pa")
                        npc = bass.AP(tensor=nPa[:].tensor, offset=0,
                                      ap=[[SL + 1, 32], [128, SL // 128]]).unsqueeze(2).broadcast_to([32, SL // 128, 128])
                        inst = vector.tensor_tensor(
                            npp_all[:].rearrange("h (c i) -> h c i", c=SL // 128),
                            ones32[:].rearrange("h (c i) -> h c i", c=SL // 128), npc, ALU.mult)
                        sV.inc(inst)
                        vector.drain()
                    tag(f"{P_}nppall", sV)

                for cl in range(SL // Q):
                    gch = goff // Q + cl
                    col = cl * Q
                    CP = f"c{gch}_"
                    pcol = cl * Q
                    psbig_bf = ps_big[:].bitcast(BF16)
                    en2_bf = ps_en2[:].bitcast(BF16)

                    @block.tensor
                    def _(tensor, col=col, CP=CP, P_=P_, gch=gch, is_pref=is_pref):
                        wait(tensor, f"{P_}convdone")
                        wait(tensor, f"{P_}rrall")
                        if gch > 0:
                            wait(tensor, f"c{gch - 1}_psbig_free")
                            wait(tensor, f"c{gch - 1}_sup")
                        for cc in range(16):
                            tensor.transpose(psbig_bf[:, cc * 128:(cc + 1) * 128],
                                             xsilu[:, cc * SL + col: cc * SL + col + 128], ident_b[:])
                        if not is_pref:
                            for cc in range(16):
                                tensor.transpose(psbig_bf[:, 2048 + cc * 128: 2048 + (cc + 1) * 128],
                                                 zs[:, cc * SL + col: cc * SL + col + 128], ident_b[:])
                        tensor.transpose(ps_en2[:, 0:32], dts[:, col:col + 128], ident_f[0:32, 0:32])
                        tensor.transpose(ps_en2[:, 32:64], rr_all[:, col:col + 128], ident_f[0:32, 0:32])
                        inst = tensor.transpose(en2_bf[:, 128:256], B_sl[:, col:col + 128], ident_b[:])
                        if not is_pref:
                            inst = tensor.matmul(ps_en2[:, 256:384], B_sl[:, col:col + 128],
                                                 C_sl[:, col:col + 128], start=True, stop=True)
                        sT.inc(inst)
                    tag(f"{CP}xtr", sT)

                    @block.vector
                    def _(vector, CP=CP, gch=gch, is_pref=is_pref):
                        wait(vector, f"{CP}xtr")
                        vector.tensor_copy(small_tm[:, 0:64], ps_en2[:, 0:64])
                        vector.drain()
                        vector.tensor_copy(B_tm[:], en2_bf[:, 128:256])
                        if not is_pref:
                            vector.tensor_tensor(Gt_m[:], ps_en2[:, 256:384], mask01[:], ALU.mult)
                            vector.drain()
                            vector.tensor_copy(zs_tm[:], psbig_bf[:, 2048:4096])
                        vector.drain()
                        dt_rep = small_tm[:, 0:32].unsqueeze(2).broadcast_to([128, 32, 64])
                        rr_rep = small_tm[:, 32:64].unsqueeze(2).broadcast_to([128, 32, 64])
                        xv = lambda t: t[:].rearrange("p (h e) -> p h e", h=32)
                        vector.tensor_tensor(xv(Xd), psbig_bf[:, 0:2048].rearrange("p (h e) -> p h e", h=32),
                                             dt_rep, ALU.mult)
                        vector.drain()
                        vector.tensor_tensor(xv(Xw), xv(Xd), rr_rep, ALU.mult)
                        vector.drain()
                        vector.reciprocal(small_tm[:, 96:128], small_tm[:, 0:32])
                        vector.drain()
                        vector.tensor_tensor(small_tm[:, 64:96], dpar_sb[:], small_tm[:, 96:128], ALU.mult)
                        vector.drain()
                        ddt_rep = small_tm[:, 64:96].unsqueeze(2).broadcast_to([128, 32, 64])
                        inst = vector.tensor_tensor(xv(XDt), xv(Xd), ddt_rep, ALU.mult)
                        sV.inc(inst)
                        vector.drain()
                    tag(f"{CP}xd", sV)
                    tag(f"{CP}rrtm", sV)

                    if not is_pref:
                        def emm_round(tensor, r, CP=CP, pcol=pcol):
                            ps = ps_e if r % 2 == 0 else ps_en2
                            if r >= 2:
                                wait(tensor, f"{CP}eex{r - 2}")
                            if r < 4:
                                tensor.matmul(ps[:, 0:512], ident_b[:], maskneg8[:, 0:512],
                                              start=True, stop=False, skip_group_check=True)
                                tensor.matmul(ps[:, 512:1024], ident_b[:], maskneg8[:, 512:1024],
                                              start=True, stop=False, skip_group_check=True)
                            for hh in range(8):
                                h = (r % 4) * 8 + hh
                                sl8 = ps[:, hh * 128:(hh + 1) * 128]
                                selh = sel32[:, h * 128:(h + 1) * 128]
                                if r < 4:
                                    tensor.matmul(sl8, selh, Pa[0:32, pcol + 1:pcol + 129],
                                                  start=False, stop=False, skip_group_check=True)
                                    inst = tensor.matmul(sl8, nPa[0:32, pcol + 1:pcol + 129], selh,
                                                         start=False, stop=(hh == 7), skip_group_check=True)
                                else:
                                    tensor.matmul(sl8, selh, Pa[0:32, pcol + 1:pcol + 129],
                                                  start=True, stop=False, skip_group_check=True)
                                    inst = tensor.matmul(sl8, npp_all[:, pcol:pcol + 128], selh,
                                                         start=False, stop=True, skip_group_check=True)
                            return inst

                        def y_round(tensor, r, CP=CP):
                            wait(tensor, f"{CP}mt{r}")
                            if r == 0:
                                wait(tensor, f"{CP}xd")
                                for b4 in range(4):
                                    tensor.matmul(ps_big[:, b4 * 512:(b4 + 1) * 512], ones_row_b[:],
                                                  zero_row[:], start=True, stop=False,
                                                  skip_group_check=True)
                            for hh in range(8):
                                h = (r % 4) * 8 + hh
                                mtb = Mt[:, (r % 2) * 1024 + hh * 128:(r % 2) * 1024 + (hh + 1) * 128]
                                if r < 4:
                                    inst = tensor.matmul(ps_big[:, h * 64:(h + 1) * 64], mtb,
                                                         Xd[:, h * 64:(h + 1) * 64],
                                                         start=False, stop=False, skip_group_check=True)
                                else:
                                    inst = tensor.matmul(ps_big[:, h * 64:(h + 1) * 64], mtb,
                                                         S_sb[:, h * 64:(h + 1) * 64],
                                                         start=False, stop=(r == 7 and hh == 7),
                                                         skip_group_check=True)
                            return inst

                        @block.tensor
                        def _(tensor, CP=CP, gch=gch):
                            wait(tensor, f"{CP}rrtm")
                            if gch == nch_pref:
                                wait(tensor, "sel_done")
                            inst = emm_round(tensor, 0)
                            sT.inc(inst)
                        tag(f"{CP}emm0", sT)

                        @block.scalar
                        def _(scalar, CP=CP):
                            wait(scalar, f"{CP}emm0")
                            inst = scalar.activation(Eex[:, 0:1024], ps_e[:], AF.Exp)
                            sS.inc(inst)
                            scalar.drain()
                        tag(f"{CP}eex0", sS)

                        @block.vector
                        def _(vector, CP=CP, col=col):
                            wait(vector, f"{CP}eex0")
                            gt_rep = Gt_m[:].unsqueeze(1).broadcast_to([128, 8, 128])
                            inst = vector.tensor_tensor(
                                Mt[:, 0:1024].rearrange("p (h i) -> p h i", h=8),
                                Eex[:, 0:1024].rearrange("p (h i) -> p h i", h=8), gt_rep, ALU.mult)
                            sV.inc(inst)
                            vector.drain()
                        tag(f"{CP}mt0", sV)

                        for r in range(1, 8):
                            @block.tensor
                            def _(tensor, r=r, CP=CP):
                                inst = emm_round(tensor, r)
                                sT.inc(inst)
                            tag(f"{CP}emm{r}", sT)

                            @block.scalar
                            def _(scalar, r=r, CP=CP):
                                wait(scalar, f"{CP}emm{r}")
                                if r >= 2:
                                    wait(scalar, f"{CP}mt{r - 2}")
                                ps = ps_e if r % 2 == 0 else ps_en2
                                inst = scalar.activation(Eex[:, (r % 2) * 1024:(r % 2) * 1024 + 1024],
                                                         ps[:], AF.Exp)
                                sS.inc(inst)
                                scalar.drain()
                            tag(f"{CP}eex{r}", sS)

                            @block.vector
                            def _(vector, r=r, CP=CP, col=col):
                                wait(vector, f"{CP}eex{r}")
                                if r >= 2:
                                    wait(vector, f"{CP}y{r - 2}")
                                rep = (Gt_m[:] if r < 4 else C_sl[:, col:col + 128]).unsqueeze(1).broadcast_to([128, 8, 128])
                                inst = vector.tensor_tensor(
                                    Mt[:, (r % 2) * 1024:(r % 2) * 1024 + 1024].rearrange("p (h i) -> p h i", h=8),
                                    Eex[:, (r % 2) * 1024:(r % 2) * 1024 + 1024].rearrange("p (h i) -> p h i", h=8),
                                    rep, ALU.mult)
                                sV.inc(inst)
                                vector.drain()
                            tag(f"{CP}mt{r}", sV)

                            @block.tensor
                            def _(tensor, r=r, CP=CP):
                                inst = y_round(tensor, r - 1)
                                sT.inc(inst)
                            tag(f"{CP}y{r - 1}", sT)

                        @block.tensor
                        def _(tensor, CP=CP):
                            inst = y_round(tensor, 7)
                            sT.inc(inst)
                        tag(f"{CP}y7", sT)

                        @block.vector
                        def _(vector, CP=CP):
                            wait(vector, f"{CP}y7")
                            inst = vector.tensor_tensor(y2[:], ps_big[:], XDt[:], ALU.add)
                            sV.inc(inst)
                            vector.drain()
                        tag(f"{CP}y2", sV)

                        @block.vector
                        def _(vector, CP=CP):
                            inst = vector.tensor_tensor(y3[:], y2[:], zs_tm[:], ALU.mult)
                            sV.inc(inst)
                            vector.drain()
                        tag(f"{CP}y3", sV)

                        @block.scalar
                        def _(scalar, CP=CP):
                            wait(scalar, f"{CP}y3")
                            inst = scalar.activation(y2[:], y3[:], AF.Square, accum_out=nrm_sm[:, 0:1])
                            sS.inc(inst)
                            scalar.drain()
                        tag(f"{CP}sqy", sS)

                        @block.vector
                        def _(vector, CP=CP):
                            wait(vector, f"{CP}sqy")
                            inst = vector.tensor_scalar(nrm_sm[:, 1:2], nrm_sm[:, 0:1], 1.0 / D_INNER,
                                                        EPS, ALU.mult, ALU.add)
                            sV.inc(inst)
                            vector.drain()
                        tag(f"{CP}mse", sV)

                        @block.scalar
                        def _(scalar, CP=CP):
                            wait(scalar, f"{CP}mse")
                            scalar.activation(nrm_sm[:, 2:3], nrm_sm[:, 1:2], AF.Ln)
                            scalar.drain()
                            inst = scalar.activation(nrm_sm[:, 3:4], nrm_sm[:, 2:3], AF.Exp, scale=-0.5)
                            sS.inc(inst)
                            scalar.drain()
                        tag(f"{CP}rstd", sS)

                        @block.vector
                        def _(vector, CP=CP, gch=gch):
                            wait(vector, f"{CP}rstd")
                            inst = vector.tensor_scalar(y3[:], y3[:], nrm_sm[:, 3:4], None, ALU.mult)
                            sV.inc(inst)
                            vector.drain()
                        tag(f"{CP}y4", sV)

                        tag_schunk_ready = f"{CP}y2"
                    else:
                        tag_schunk_ready = f"{CP}xd"

                    @block.tensor
                    def _(tensor, CP=CP, tag_schunk_ready=tag_schunk_ready):
                        wait(tensor, tag_schunk_ready)
                        for j4 in range(4):
                            inst = tensor.matmul(ps_big[:, j4 * 512:(j4 + 1) * 512], B_tm[:],
                                                 Xw[:, j4 * 512:(j4 + 1) * 512], start=True, stop=True)
                        sT.inc(inst)
                    tag(f"{CP}sch", sT)

                    @block.tensor
                    def _(tensor, CP=CP, cl=cl, P_=P_):
                        wait(tensor, f"{P_}decrows")
                        wait(tensor, f"{CP}sch")
                        inst = tensor.matmul(ps_en2[:, 480:512], sel32[0:SL // 128, cl * 128:(cl + 1) * 128],
                                             decrow[0:SL // 128, :], start=True, stop=True)
                        sT.inc(inst)
                    tag(f"{CP}decb", sT)

                    @block.vector
                    def _(vector, CP=CP, gch=gch):
                        wait(vector, f"{CP}decb")
                        dec_rep = ps_en2[:, 480:512].unsqueeze(2).broadcast_to([128, 32, 64])
                        vector.tensor_tensor(S_t1[:].rearrange("p (h e) -> p h e", h=32),
                                             S_sb[:].rearrange("p (h e) -> p h e", h=32), dec_rep, ALU.mult)
                        vector.drain()
                        inst = vector.tensor_tensor(S_sb[:], S_t1[:], ps_big[:], ALU.add)
                        sV.inc(inst)
                        vector.drain()
                        if gch == nch_pref - 1:
                            inst = vector.tensor_scalar_mul(S_sb[:], S_sb[:], flag_sb[:])
                            sV.inc(inst)
                            vector.drain()
                    tag(f"{CP}sup", sV)

                    if not is_pref:
                        @block.tensor
                        def _(tensor, CP=CP, gch=gch):
                            wait(tensor, f"{CP}y4")
                            wait(tensor, f"{CP}sup")
                            for ec in range(16):
                                inst = tensor.transpose(psbig_bf[:, ec * 128:(ec + 1) * 128],
                                                        y3[:, ec * 128:(ec + 1) * 128], ident_b[:])
                            sT.inc(inst)
                        tag(f"{CP}y4tr", sT)

                        @block.scalar
                        def _(scalar, CP=CP, gch=gch):
                            wait(scalar, f"{CP}y4tr")
                            if gch - 1 >= nch_pref:
                                wait(scalar, f"c{gch - 1}_y4out")
                            inst = scalar.copy(y4asm[:], psbig_bf[:, 0:2048])
                            sS.inc(inst)
                            scalar.drain()
                        tag(f"{CP}y4ev", sS)
                        tag(f"{CP}psbig_free2", sS)

                        @block.sync
                        def _(sync, CP=CP, gch=gch):
                            wait(sync, f"{CP}y4ev")
                            mch = gch - nch_pref
                            inst = sync.dma_start(out=d_y4.rearrange("(c p) f -> c p f", p=128)[mch],
                                                  in_=y4asm[:])
                            dY.inc(inst, 16)
                        tag(f"{CP}y4out", dY)

                    if is_pref:
                        tags[f"{CP}psbig_free"] = tags[f"{CP}sup"]
                    else:
                        tags[f"{CP}psbig_free"] = tags[f"{CP}psbig_free2"]

            last_sup = f"c{NTOT // Q - 1}_sup"

        ESm = contextlib.ExitStack()
        ffn_acc = ESm.enter_context(nc.sbuf_tensor("ffn_acc", [128, 16 * D_MODEL], BF16))
        ES2 = contextlib.ExitStack()
        hbuf = ES2.enter_context(nc.sbuf_tensor("hbuf", [128, 16 * NT_MAIN], BF16))
        wd_sb = ES2.enter_context(nc.sbuf_tensor("wd_sb", [128, 16 * D_MODEL], BF16))
        wgu_buf = ES2.enter_context(nc.sbuf_tensor("wgu_buf", [128, 2 * 2048], BF16))
        sg_sb = ES2.enter_context(nc.sbuf_tensor("sg_sb", [128, 2 * 512], BF16))
        lgs = ES2.enter_context(nc.sbuf_tensor("lgs", [128, 40], F32))
        with ES2:
         if True:
            for tc in range(n_tc):
                @block.tensor
                def _(tensor, tc=tc):
                    wait(tensor, last_sup)
                    wait(tensor, "p0_dG")
                    if tc > 0:
                        wait(tensor, f"rt{tc - 1}_lg")
                    for dc in range(8):
                        inst = tensor.matmul(ps_en2[:, 0:8],
                                             xn_fm[:, dc * NT_MAIN + tc * 128: dc * NT_MAIN + (tc + 1) * 128],
                                             gwT_sb[:, dc * 8:(dc + 1) * 8],
                                             start=(dc == 0), stop=(dc == 7))
                    sT.inc(inst)
                tag(f"rt{tc}_mm", sT)

                @block.vector
                def _(vector, tc=tc):
                    wait(vector, f"rt{tc}_mm")
                    if tc > 0:
                        wait(vector, f"rt{tc - 1}_sg")
                    inst = vector.tensor_copy(lgs[:, 0:8], ps_en2[:, 0:8])
                    sV.inc(inst)
                    vector.drain()
                tag(f"rt{tc}_lg", sV)

                @block.scalar
                def _(scalar, tc=tc):
                    wait(scalar, f"rt{tc}_lg")
                    inst = scalar.activation(lgs[:, 8:16], lgs[:, 0:8], AF.Sigmoid)
                    sS.inc(inst)
                    scalar.drain()
                tag(f"rt{tc}_sg", sS)

                @block.vector
                def _(vector, tc=tc):
                    wait(vector, f"rt{tc}_sg")
                    cslot = c_tm[:, tc * 8:(tc + 1) * 8]
                    vector.tensor_tensor(cslot, lgs[:, 0:8], beta_sb[:], ALU.add)
                    vector.drain()
                    vector.reduce_max(lgs[:, 32:33], cslot, axis=AX)
                    vector.drain()
                    vector.tensor_scalar(lgs[:, 16:24], cslot, lgs[:, 32:33], None, ALU.is_ge)
                    vector.drain()
                    vector.scalar_tensor_tensor(lgs[:, 24:32], lgs[:, 16:24], -1e30, cslot, ALU.mult, ALU.add)
                    vector.drain()
                    vector.reduce_max(lgs[:, 33:34], lgs[:, 24:32], axis=AX)
                    vector.drain()
                    vector.tensor_scalar(lgs[:, 24:32], lgs[:, 24:32], lgs[:, 33:34], None, ALU.is_ge)
                    vector.drain()
                    vector.tensor_add(lgs[:, 16:24], lgs[:, 16:24], lgs[:, 24:32])
                    vector.drain()
                    inst = vector.tensor_tensor(cslot, lgs[:, 8:16], lgs[:, 16:24], ALU.mult)
                    sV.inc(inst)
                    vector.drain()
                tag(f"rt{tc}_c", sV)
            tag("router_done", sV)

            for e in range(NEXP + 1):
                is_shared = e == NEXP
                EP = f"e{e}_"
                for mc in range(16):
                    par = mc % 2

                    @block.gpsimd
                    def _(gpsimd, e=e, mc=mc, par=par, EP=EP):
                        if mc >= 2:
                            wait(gpsimd, f"{EP}gu_{mc - 2}_{n_tt - 1}")
                        elif e > 0:
                            wait(gpsimd, f"e{e - 1}_gu_{mc + 14}_{n_tt - 1}")
                        else:
                            wait(gpsimd, last_sup)
                        inst = gpsimd.dma_start(
                            out=wgu_buf[:, par * 2048: par * 2048 + 1024].rearrange("p (hc m) -> p hc m", hc=8),
                            in_=d_wg.rearrange("e (hc p) m -> e p hc m", p=128)[e][:, :, mc * 128:(mc + 1) * 128])
                        dM[mc % 2].inc(inst, 16)
                        inst = gpsimd.dma_start(
                            out=wgu_buf[:, par * 2048 + 1024: par * 2048 + 2048].rearrange("p (hc m) -> p hc m", hc=8),
                            in_=d_wu.rearrange("e (hc p) m -> e p hc m", p=128)[e][:, :, mc * 128:(mc + 1) * 128])
                        dM[mc % 2].inc(inst, 16)
                        if e > 0:
                            wait(gpsimd, f"e{e - 1}_dn{2 * n_tc - 1}")
                        inst = gpsimd.dma_start(
                            out=wd_sb[:, mc * 1024:(mc + 1) * 1024],
                            in_=d_wd.rearrange("e (mc p) hh -> e mc p hh", p=128)[e, mc])
                        dD.inc(inst, 16)
                    tag(f"{EP}w{mc}", dM[mc % 2])
                    tag(f"{EP}wd{mc}", dD)

                    for tt in range(n_tt):
                        j = mc * n_tt + tt
                        qpar = j % 2
                        gps = ps_big[:, qpar * 1024: qpar * 1024 + 512]
                        ups = ps_big[:, qpar * 1024 + 512: qpar * 1024 + 1024]

                        @block.tensor
                        def _(tensor, e=e, mc=mc, tt=tt, j=j, gps=gps, ups=ups, par=par, EP=EP):
                            wait(tensor, f"{EP}w{mc}")
                            if j >= 2:
                                wait(tensor, f"{EP}h_{(j - 2) // n_tt}_{(j - 2) % n_tt}")
                            elif e > 0:
                                wait(tensor, f"e{e - 1}_h_15_{n_tt - 1}")
                            for hc in range(8):
                                tensor.matmul(gps, wgu_buf[:, par * 2048 + hc * 128: par * 2048 + (hc + 1) * 128],
                                              xn_fm[:, hc * NT_MAIN + tt * 512: hc * NT_MAIN + tt * 512 + 512],
                                              start=(hc == 0), stop=(hc == 7))
                            for hc in range(8):
                                inst = tensor.matmul(ups,
                                                     wgu_buf[:, par * 2048 + 1024 + hc * 128: par * 2048 + 1024 + (hc + 1) * 128],
                                                     xn_fm[:, hc * NT_MAIN + tt * 512: hc * NT_MAIN + tt * 512 + 512],
                                                     start=(hc == 0), stop=(hc == 7))
                            sT.inc(inst)
                        tag(f"{EP}gu_{mc}_{tt}", sT)

                        @block.scalar
                        def _(scalar, j=j, mc=mc, tt=tt, qpar=qpar, gps=gps, e=e, EP=EP):
                            wait(scalar, f"{EP}gu_{mc}_{tt}")
                            if j >= 2:
                                wait(scalar, f"{EP}h_{(j - 2) // n_tt}_{(j - 2) % n_tt}")
                            elif e > 0:
                                wait(scalar, f"e{e - 1}_h_15_{n_tt - 1}")
                            inst = scalar.activation(sg_sb[:, qpar * 512:(qpar + 1) * 512], gps, AF.Silu)
                            sS.inc(inst)
                            scalar.drain()
                        tag(f"{EP}sg_{mc}_{tt}", sS)

                        @block.vector
                        def _(vector, j=j, mc=mc, tt=tt, qpar=qpar, ups=ups, e=e, EP=EP):
                            wait(vector, f"{EP}sg_{mc}_{tt}")
                            inst = vector.tensor_tensor(
                                hbuf[:, mc * NT_MAIN + tt * 512: mc * NT_MAIN + tt * 512 + 512],
                                sg_sb[:, qpar * 512:(qpar + 1) * 512], ups, ALU.mult)
                            sV.inc(inst)
                            vector.drain()
                        tag(f"{EP}h_{mc}_{tt}", sV)

                for tc in range(n_tc):
                    for nt in range(2):
                        j = tc * 2 + nt
                        dpar_ = j % 2
                        dps = ps_e[:, dpar_ * 512:(dpar_ + 1) * 512]

                        @block.tensor
                        def _(tensor, e=e, tc=tc, nt=nt, j=j, dps=dps, EP=EP):
                            if j == 0:
                                for mc in range(16):
                                    wait(tensor, f"{EP}h_{mc}_{n_tt - 1}")
                                wait(tensor, f"{EP}wd15")
                                wait(tensor, "router_done")
                            if j >= 2:
                                wait(tensor, f"{EP}dnv{j - 2}")
                            elif e > 0:
                                wait(tensor, f"e{e - 1}_dnv{2 * n_tc - 2 + j}")
                            for mc in range(16):
                                inst = tensor.matmul(dps,
                                                     hbuf[:, mc * NT_MAIN + tc * 128: mc * NT_MAIN + (tc + 1) * 128],
                                                     wd_sb[:, mc * 1024 + nt * 512: mc * 1024 + nt * 512 + 512],
                                                     start=(mc == 0), stop=(mc == 15))
                            sT.inc(inst)
                        tag(f"{EP}dn{j}", sT)

                        @block.vector
                        def _(vector, e=e, tc=tc, nt=nt, j=j, dps=dps, EP=EP, is_shared=is_shared):
                            wait(vector, f"{EP}dn{j}")
                            acc = ffn_acc[:, tc * D_MODEL + nt * 512: tc * D_MODEL + nt * 512 + 512]
                            if e == 0:
                                inst = vector.tensor_scalar_mul(acc, dps, c_tm[:, tc * 8 + e: tc * 8 + e + 1])
                            elif is_shared:
                                inst = vector.tensor_tensor(acc, dps, acc, ALU.add)
                            else:
                                inst = vector.scalar_tensor_tensor(acc, dps, c_tm[:, tc * 8 + e: tc * 8 + e + 1],
                                                                   acc, ALU.mult, ALU.add)
                            sV.inc(inst)
                            vector.drain()
                        tag(f"{EP}dnv{j}", sV)
            tag("moe_done", sV)

        ES3 = contextlib.ExitStack()
        woutT_sb = ES3.enter_context(nc.sbuf_tensor("woutT_sb", [128, 16 * D_MODEL], BF16))
        y4t = ES3.enter_context(nc.sbuf_tensor("y4t", [128, 2 * D_INNER], BF16))
        xres = ES3.enter_context(nc.sbuf_tensor("xres", [128, 2 * D_MODEL], F32))
        gsil = ES3.enter_context(nc.sbuf_tensor("gsil", [128, 2 * D_MODEL], BF16))
        sg_f = ES3.enter_context(nc.sbuf_tensor("sg_f", [128, D_MODEL], BF16))
        otile = ES3.enter_context(nc.sbuf_tensor("otile", [128, 2 * D_MODEL], F32))
        with ES3:
         if True:
            @block.gpsimd
            def _(gpsimd):
                wait(gpsimd, f"e{NEXP}_dn{2 * n_tc - 1}")
                for ec in range(16):
                    inst = gpsimd.dma_start(out=woutT_sb[:, ec * 1024:(ec + 1) * 1024],
                                            in_=d_woutT.rearrange("(ec p) d -> ec p d", p=128)[ec])
                    dWo.inc(inst, 16)
            tag("wout_raw", dWo)

            @block.vector
            def _(vector):
                wait(vector, "wout_raw")
                inst = vector.tensor_tensor(
                    woutT_sb[:].rearrange("p (ec d) -> p ec d", ec=16),
                    woutT_sb[:].rearrange("p (ec d) -> p ec d", ec=16),
                    normw_sb[:].unsqueeze(2).broadcast_to([128, 16, D_MODEL]), ALU.mult)
                sV.inc(inst)
                vector.drain()
            tag("wout_ld", sV)

            for tc in range(n_tc):
                par = tc % 2

                @block.sync
                def _(sync, tc=tc, par=par):
                    wait(sync, f"e{NEXP}_dn{2 * n_tc - 1}")
                    wait(sync, f"c{nch_pref + nch_main - 1}_y4out")
                    if tc >= 2:
                        wait(sync, f"fin{tc - 2}_out")
                    inst = sync.dma_start(out=y4t[:, par * 2048:(par + 1) * 2048],
                                          in_=d_y4.rearrange("(c p) f -> c p f", p=128)[tc])
                    dF[par].inc(inst, 16)
                    inst = sync.dma_start(out=xres[:, par * 1024:(par + 1) * 1024],
                                          in_=d_xm[tc * 128:(tc + 1) * 128, :])
                    dF[par].inc(inst, 16)
                tag(f"fin{tc}_ld", dF[par])

                psF = ps_en2 if par == 0 else ps_e

                @block.tensor
                def _(tensor, tc=tc, par=par, psF=psF):
                    wait(tensor, f"fin{tc}_ld")
                    wait(tensor, "wout_ld")
                    if tc >= 2:
                        wait(tensor, f"fin{tc - 2}_sev")
                    for half in range(2):
                        dpsF = psF[:, 0:512] if half == 0 else psF[:, 512:1024]
                        for ec in range(16):
                            inst = tensor.matmul(dpsF[:],
                                                 y4t[:, par * 2048 + ec * 128: par * 2048 + (ec + 1) * 128],
                                                 woutT_sb[:, ec * 1024 + half * 512: ec * 1024 + half * 512 + 512],
                                                 start=(ec == 0), stop=(ec == 15))
                    sT.inc(inst)
                tag(f"fin{tc}_s", sT)

                @block.scalar
                def _(scalar, tc=tc, par=par):
                    wait(scalar, "moe_done")
                    if tc >= 2:
                        wait(scalar, f"fin{tc - 2}_sev")
                    inst = scalar.activation(gsil[:, par * 1024:(par + 1) * 1024],
                                             ffn_acc[:, tc * D_MODEL:(tc + 1) * D_MODEL], AF.Silu)
                    sS.inc(inst)
                    scalar.drain()
                tag(f"fin{tc}_g", sS)

                @block.vector
                def _(vector, tc=tc, par=par, psF=psF):
                    wait(vector, f"fin{tc}_s")
                    wait(vector, f"fin{tc}_g")
                    if tc >= 2:
                        wait(vector, f"fin{tc - 2}_out")
                    vector.tensor_tensor(sg_f[:, 0:512], psF[:, 0:512], gsil[:, par * 1024: par * 1024 + 512], ALU.mult)
                    vector.drain()
                    vector.tensor_tensor(sg_f[:, 512:1024], psF[:, 512:1024], gsil[:, par * 1024 + 512:(par + 1) * 1024], ALU.mult)
                    vector.drain()
                    inst = vector.tensor_tensor(otile[:, par * 1024:(par + 1) * 1024],
                                                xres[:, par * 1024:(par + 1) * 1024], sg_f[:], ALU.add)
                    sV.inc(inst)
                    vector.drain()
                tag(f"fin{tc}_sev", sV)

                @block.sync
                def _(sync, tc=tc, par=par):
                    wait(sync, f"fin{tc}_sev")
                    inst = sync.dma_start(out=d_out[tc * 128:(tc + 1) * 128, :],
                                          in_=otile[:, par * 1024:(par + 1) * 1024])
                    dO[par].inc(inst, 16)
                tag(f"fin{tc}_out", dO[par])

            @block.sync
            def _(sync):
                sync.wait_ge(_dO0, C["dO0"])
                sync.wait_ge(_dO1, C["dO1"])
        ESm.close()

    f = nc.m.functions[0]
    blocks = list(f.blocks)
    head, tail = blocks[0], blocks[-1]
    mid = blocks[1:-1]
    order = {"EngineType.SP": 1, "EngineType.Pool": 2, "EngineType.PE": 3,
             "EngineType.DVE": 4, "EngineType.Activation": 5}

    def eng_of(b):
        for i in b.instructions:
            if str(i.engine) != "EngineType.Unassigned":
                return order.get(str(i.engine), 0)
        return 0

    f.blocks = [head] + sorted(mid, key=eng_of) + [tail]
    return nc


def _prep_inputs(kw):
    x = np.ascontiguousarray(np.asarray(kw["L0_attn_x"], np.float32))
    W_in = np.asarray(kw["L0_attn_in_proj_w"], np.float32)
    conv_w = np.asarray(kw["L0_attn_conv_w"], np.float32)
    conv_b = np.asarray(kw["L0_attn_conv_b"], np.float32)
    norm_w = np.asarray(kw["L0_attn_norm_w"], np.float32)
    W_out = np.asarray(kw["L0_attn_out_proj_w"], np.float32)
    dt_bias = np.asarray(kw["L0_attn_dt_bias"], np.float32)
    A_log = np.asarray(kw["L0_attn_A_log"], np.float32)
    D_param = np.asarray(kw["L0_attn_D_param"], np.float32)
    gate_w = np.asarray(kw["L0_ffn_gate_w"], np.float32)
    beta = np.asarray(kw["L0_ffn_beta"], np.float32)
    com = dict(
        w_inT=np.ascontiguousarray(W_in.T),
        w_outT=np.ascontiguousarray(W_out.T),
        gwT=np.ascontiguousarray(gate_w.T),
        convw=np.ascontiguousarray(conv_w[:, 0, :].reshape(18, 128, 4).transpose(1, 0, 2).reshape(128, 72)),
        convb=np.ascontiguousarray(conv_b.reshape(18, 128).T),
        normw=np.ascontiguousarray(norm_w.reshape(16, 128).T),
        dpar=np.ascontiguousarray(np.tile(D_param[None, :], (128, 1))),
        dtb=np.ascontiguousarray(dt_bias[:, None]),
        alog=np.ascontiguousarray(A_log[:, None]),
        beta_t=np.ascontiguousarray(np.tile(beta[None, :], (128, 1))),
        w_gate=np.ascontiguousarray(np.concatenate(
            [np.asarray(kw["L0_ffn_gate_proj_w"], np.float32),
             np.asarray(kw["L0_ffn_shared_gate_w"], np.float32)], axis=0)),
        w_up=np.ascontiguousarray(np.concatenate(
            [np.asarray(kw["L0_ffn_up_proj_w"], np.float32),
             np.asarray(kw["L0_ffn_shared_up_w"], np.float32)], axis=0)),
        w_down=np.ascontiguousarray(np.concatenate(
            [np.asarray(kw["L0_ffn_down_proj_w"], np.float32),
             np.asarray(kw["L0_ffn_shared_down_w"], np.float32)], axis=0)),
    )
    return x, com


def kernel(**kw):
    x, com = _prep_inputs(kw)
    B, L, D = x.shape
    NT = L // 2
    nc = build_nc(NT_MAIN=NT, NT_PREF=NT)
    in_maps = []
    for core in range(8):
        b, h = core // 2, core % 2
        m = dict(com)
        m["x_main"] = np.ascontiguousarray(x[b, h * NT:(h + 1) * NT])
        m["x_pref"] = np.ascontiguousarray(x[b, 0:NT]) if h == 1 else np.zeros((NT, D), np.float32)
        m["pref_flag"] = np.full((128, 1), float(h), np.float32)
        in_maps.append(m)
    res = run_bass_kernel_spmd(nc, in_maps, core_ids=list(range(8)))
    out = np.empty((B, L, D), np.float32)
    for core in range(8):
        b, h = core // 2, core % 2
        out[b, h * NT:(h + 1) * NT] = res.results[core]["out"]
    return out


# version 10 (final)
# speedup vs baseline: 1.3539x; 1.0021x over previous
import contextlib
import numpy as np
import concourse.bass as bass
import concourse.mybir as mybir
from concourse.bass_utils import run_bass_kernel_spmd

F32 = mybir.dt.float32
BF16 = mybir.dt.bfloat16
AF = mybir.ActivationFunctionType
ALU = mybir.AluOpType
AX = mybir.AxisListType.X

D_MODEL = 1024
D_STATE = 128
HEADDIM = 64
D_INNER = 2048
NHEADS = 32
CONV_DIM = 2304
D_IN_PROJ = 4384
EPS = 1e-5
Q = 128
SL = 512
NEXP = 8
M_FF = 2048

OC_X0 = 16
OC_B = 32
OC_C = 33
OC_DT = 34


def build_nc(NT_MAIN=2048, NT_PREF=2048):
    assert NT_MAIN % SL == 0 and NT_PREF % SL == 0 and NT_PREF >= SL
    n_sl_pref = NT_PREF // SL
    n_sl_main = NT_MAIN // SL
    nch_main = NT_MAIN // Q
    nch_pref = NT_PREF // Q
    NTOT = NT_PREF + NT_MAIN
    n_tc = NT_MAIN // 128
    n_tt = NT_MAIN // 512

    nc = bass.Bass("TRN2", target_bir_lowering=False)

    d_xm = nc.dram_tensor("x_main", [NT_MAIN, D_MODEL], F32, kind="ExternalInput").ap()
    d_xp = nc.dram_tensor("x_pref", [NT_PREF, D_MODEL], F32, kind="ExternalInput").ap()
    d_flag = nc.dram_tensor("pref_flag", [128, 1], F32, kind="ExternalInput").ap()
    d_winT = nc.dram_tensor("w_inT", [D_MODEL, D_IN_PROJ], F32, kind="ExternalInput").ap()
    d_woutT = nc.dram_tensor("w_outT", [D_INNER, D_MODEL], F32, kind="ExternalInput").ap()
    d_gwT = nc.dram_tensor("gwT", [D_MODEL, NEXP], F32, kind="ExternalInput").ap()
    d_convw = nc.dram_tensor("convw", [128, 18 * 4], F32, kind="ExternalInput").ap()
    d_convb = nc.dram_tensor("convb", [128, 18], F32, kind="ExternalInput").ap()
    d_normw = nc.dram_tensor("normw", [128, 16], F32, kind="ExternalInput").ap()
    d_dpar = nc.dram_tensor("dpar", [128, NHEADS], F32, kind="ExternalInput").ap()
    d_dtb = nc.dram_tensor("dtb", [NHEADS, 1], F32, kind="ExternalInput").ap()
    d_alog = nc.dram_tensor("alog", [NHEADS, 1], F32, kind="ExternalInput").ap()
    d_beta = nc.dram_tensor("beta_t", [128, NEXP], F32, kind="ExternalInput").ap()
    d_wg = nc.dram_tensor("w_gate", [NEXP + 1, D_MODEL, M_FF], F32, kind="ExternalInput").ap()
    d_wu = nc.dram_tensor("w_up", [NEXP + 1, D_MODEL, M_FF], F32, kind="ExternalInput").ap()
    d_wd = nc.dram_tensor("w_down", [NEXP + 1, M_FF, D_MODEL], F32, kind="ExternalInput").ap()
    d_out = nc.dram_tensor("out", [NT_MAIN, D_MODEL], F32, kind="ExternalOutput").ap()
    d_y4 = nc.dram_tensor("y4scr", [nch_main * 128, D_INNER], BF16, kind="Internal").ap()

    C = {}
    H = {}

    class Sem:
        def __init__(self, handle, name):
            self.h, self.name = handle, name
            C[name] = 0

        def inc(self, instr, n=1):
            instr.then_inc(self.h, n)
            C[self.name] += n

    tags = {}

    ES = contextlib.ExitStack()
    ident_f = ES.enter_context(nc.sbuf_tensor("ident_f", [128, 128], F32))
    ident_b = ES.enter_context(nc.sbuf_tensor("ident_b", [128, 128], BF16))
    ones_row = ES.enter_context(nc.sbuf_tensor("ones_row", [1, 128], F32))
    eps_col = ES.enter_context(nc.sbuf_tensor("eps_col", [128, 1], F32))
    zero_row = ES.enter_context(nc.sbuf_tensor("zero_row", [1, 512], BF16))
    ones_row_b = ES.enter_context(nc.sbuf_tensor("ones_row_b", [1, 128], BF16))
    ones_col_b = ES.enter_context(nc.sbuf_tensor("ones_col_b", [128, 1], BF16))
    ones32 = ES.enter_context(nc.sbuf_tensor("ones32", [32, SL], F32))
    mask01 = ES.enter_context(nc.sbuf_tensor("mask01", [128, 128], F32))
    maskneg8 = ES.enter_context(nc.sbuf_tensor("maskneg8", [128, 8 * 128], BF16))
    convw_sb = ES.enter_context(nc.sbuf_tensor("convw_sb", [128, 18 * 4], F32))
    convb_sb = ES.enter_context(nc.sbuf_tensor("convb_sb", [128, 18], F32))
    normw_sb = ES.enter_context(nc.sbuf_tensor("normw_sb", [128, 16], F32))
    dpar_sb = ES.enter_context(nc.sbuf_tensor("dpar_sb", [128, NHEADS], F32))
    dtb_sb = ES.enter_context(nc.sbuf_tensor("dtb_sb", [NHEADS, 1], F32))
    alog_sb = ES.enter_context(nc.sbuf_tensor("alog_sb", [NHEADS, 1], F32))
    acol_sb = ES.enter_context(nc.sbuf_tensor("acol_sb", [NHEADS, 1], F32))
    beta_sb = ES.enter_context(nc.sbuf_tensor("beta_sb", [128, NEXP], F32))
    gwT_sb = ES.enter_context(nc.sbuf_tensor("gwT_sb", [128, 8 * NEXP], BF16))
    flag_sb = ES.enter_context(nc.sbuf_tensor("flag_sb", [128, 1], F32))
    xn_fm = ES.enter_context(nc.sbuf_tensor("xn_fm", [128, 8 * NT_MAIN], BF16))
    c_tm = ES.enter_context(nc.sbuf_tensor("c_tm", [128, 16 * NEXP], F32))
    ps_big = ES.enter_context(nc.psum_tensor("ps_big", [128, 2048], F32))
    ps_e = ES.enter_context(nc.psum_tensor("ps_e", [128, 1024], F32))
    ps_en2 = ES.enter_context(nc.psum_tensor("ps_en2", [128, 1024], F32))
    _dS = ES.enter_context(nc.semaphore("dS"))
    _dG = ES.enter_context(nc.semaphore("dG"))
    _dC = ES.enter_context(nc.semaphore("dC"))
    _dX0 = ES.enter_context(nc.semaphore("dX0"))
    _dX1 = ES.enter_context(nc.semaphore("dX1"))
    _dY = ES.enter_context(nc.semaphore("dY"))
    _dF0 = ES.enter_context(nc.semaphore("dF0"))
    _dF1 = ES.enter_context(nc.semaphore("dF1"))
    _dO0 = ES.enter_context(nc.semaphore("dO0"))
    _dO1 = ES.enter_context(nc.semaphore("dO1"))
    _dWc = ES.enter_context(nc.semaphore("dWc"))
    _dI0 = ES.enter_context(nc.semaphore("dI0"))
    _dI1 = ES.enter_context(nc.semaphore("dI1"))
    _dM0 = ES.enter_context(nc.semaphore("dM0"))
    _dM1 = ES.enter_context(nc.semaphore("dM1"))
    _dD = ES.enter_context(nc.semaphore("dD"))
    _dWo = ES.enter_context(nc.semaphore("dWo"))
    _sT = ES.enter_context(nc.semaphore("sT"))
    _sV = ES.enter_context(nc.semaphore("sV"))
    _sS = ES.enter_context(nc.semaphore("sS"))
    _sP = ES.enter_context(nc.semaphore("sP"))
    block = ES.enter_context(nc.Block())
    with ES:
     if True:
        dS, dG = Sem(_dS, "dS"), Sem(_dG, "dG")
        dC, dY, dWc, dD, dWo = Sem(_dC, "dC"), Sem(_dY, "dY"), Sem(_dWc, "dWc"), Sem(_dD, "dD"), Sem(_dWo, "dWo")
        dX = [Sem(_dX0, "dX0"), Sem(_dX1, "dX1")]
        dF = [Sem(_dF0, "dF0"), Sem(_dF1, "dF1")]
        dO = [Sem(_dO0, "dO0"), Sem(_dO1, "dO1")]
        dI = [Sem(_dI0, "dI0"), Sem(_dI1, "dI1")]
        dM = [Sem(_dM0, "dM0"), Sem(_dM1, "dM1")]
        sT, sV, sS, sP = Sem(_sT, "sT"), Sem(_sV, "sV"), Sem(_sS, "sS"), Sem(_sP, "sP")
        SEMS = {s.name: s for s in [dS, dG, dC, dY, dWc, dD, dWo] + dX + dF + dO + dI + dM + [sT, sV, sS, sP]}

        def tag(name, sem):
            tags[name] = (sem.name, C[sem.name])

        def wait(eng, name):
            s, v = tags[name]
            if v > 0:
                eng.wait_ge(SEMS[s].h, v)

        @block.sync
        def _(sync):
            for dst, src in [(convw_sb, d_convw), (convb_sb, d_convb), (normw_sb, d_normw),
                             (dpar_sb, d_dpar), (dtb_sb, d_dtb), (alog_sb, d_alog),
                             (beta_sb, d_beta), (flag_sb, d_flag)]:
                inst = sync.dma_start(out=dst[:], in_=src[:])
                dC.inc(inst, 16)
        tag("p0_dS", dC)

        @block.gpsimd
        def _(gpsimd):
            inst = gpsimd.dma_start(
                out=gwT_sb[:].rearrange("p (dc e) -> p dc e", dc=8),
                in_=d_gwT.rearrange("(dc p) e -> p dc e", p=128))
            dWc.inc(inst, 16)
            gpsimd.memset(ident_f[:], 0.0)
            gpsimd.memset(ident_b[:], 0.0)
            gpsimd.memset(mask01[:], 0.0)
            gpsimd.memset(maskneg8[:], 0.0)
            gpsimd.drain()
            gpsimd.affine_select(out=ident_f[:], in_=ident_f[:], compare_op=ALU.not_equal,
                                 fill=1.0, base=0, pattern=[[-1, 128]], channel_multiplier=1)
            gpsimd.affine_select(out=ident_b[:], in_=ident_b[:], compare_op=ALU.not_equal,
                                 fill=1.0, base=0, pattern=[[-1, 128]], channel_multiplier=1)
            gpsimd.affine_select(out=mask01[:], in_=mask01[:], compare_op=ALU.is_gt,
                                 fill=1.0, base=0, pattern=[[-1, 128]], channel_multiplier=1)
            gpsimd.drain()
            for g8 in range(8):
                gpsimd.affine_select(out=maskneg8[:, g8 * 128:(g8 + 1) * 128],
                                     in_=maskneg8[:, g8 * 128:(g8 + 1) * 128],
                                     compare_op=ALU.is_ge, fill=-1e30, base=0,
                                     pattern=[[1, 128]], channel_multiplier=-1)
            gpsimd.drain()
            inst = gpsimd.engine_nop()
            sP.inc(inst)
        tag("p0_consts", sP)
        tag("p0_dG", dWc)

        @block.vector
        def _(vector):
            vector.memset(ones_row[:], 1.0)
            vector.memset(eps_col[:], EPS)
            vector.memset(zero_row[:], 0.0)
            vector.memset(ones_row_b[:], 1.0)
            vector.memset(ones_col_b[:], 1.0)
            vector.memset(ones32[:], 1.0)
            inst = vector.drain()
            sV.inc(inst)
        tag("p0_v", sV)

        @block.scalar
        def _(scalar):
            wait(scalar, "p0_dS")
            inst = scalar.activation(acol_sb[:], alog_sb[:], AF.Exp)
            sS.inc(inst)
            scalar.drain()
        tag("p0_s", sS)

        @block.vector
        def _(vector):
            wait(vector, "p0_s")
            inst = vector.tensor_scalar_mul(acol_sb[:], acol_sb[:], -1.0)
            sV.inc(inst)
            vector.drain()
        tag("p0_acol", sV)

        ES1 = contextlib.ExitStack()
        sel32 = ES1.enter_context(nc.sbuf_tensor("sel32", [32, 32 * 128], F32))
        S_sb = ES1.enter_context(nc.sbuf_tensor("S_sb", [128, D_INNER], BF16))
        S_t1 = ES1.enter_context(nc.sbuf_tensor("S_t1", [128, D_INNER], BF16))
        Pa = ES1.enter_context(nc.sbuf_tensor("Pa_sl", [32, SL + 1], F32))
        nPa = ES1.enter_context(nc.sbuf_tensor("nPa_sl", [32, SL + 1], F32))
        x_tile = ES1.enter_context(nc.sbuf_tensor("x_tile", [128, 2 * D_MODEL], F32))
        xn_tm = ES1.enter_context(nc.sbuf_tensor("xn_tm", [128, 2 * D_MODEL], F32))
        lnst = ES1.enter_context(nc.sbuf_tensor("lnst", [128, 16], F32))
        zs = ES1.enter_context(nc.sbuf_tensor("zs", [128, 16 * SL], BF16))
        xbc = ES1.enter_context(nc.sbuf_tensor("xbc", [128, 18 * (SL + 3)], BF16))
        xsilu = ES1.enter_context(nc.sbuf_tensor("xsilu", [128, 16 * SL], BF16))
        B_sl = ES1.enter_context(nc.sbuf_tensor("B_sl", [128, SL], BF16))
        C_sl = ES1.enter_context(nc.sbuf_tensor("C_sl", [128, SL], BF16))
        cva = ES1.enter_context(nc.sbuf_tensor("cva", [128, 2 * SL], BF16))
        cvb = ES1.enter_context(nc.sbuf_tensor("cvb", [128, 2 * SL], BF16))
        dts = ES1.enter_context(nc.sbuf_tensor("dts", [32, 4 * SL], F32))
        win_buf = ES1.enter_context(nc.sbuf_tensor("win_buf", [128, 2 * 1024], BF16))
        Xd = ES1.enter_context(nc.sbuf_tensor("Xd", [128, D_INNER], BF16))
        Xw = ES1.enter_context(nc.sbuf_tensor("Xw", [128, D_INNER], BF16))
        XDt = ES1.enter_context(nc.sbuf_tensor("XDt", [128, D_INNER], BF16))
        small_tm = ES1.enter_context(nc.sbuf_tensor("small_tm", [128, 128], F32))
        rr_all = ES1.enter_context(nc.sbuf_tensor("rr_all", [32, SL], F32))
        npp_all = ES1.enter_context(nc.sbuf_tensor("npp_all", [32, SL], F32))
        zs_tm = ES1.enter_context(nc.sbuf_tensor("zs_tm", [128, D_INNER], BF16))
        dec_sm = ES1.enter_context(nc.sbuf_tensor("dec_sm", [32, 8], F32))
        decrow = ES1.enter_context(nc.sbuf_tensor("decrow", [4, 32], F32))
        B_tm = ES1.enter_context(nc.sbuf_tensor("B_tm", [128, 128], BF16))
        Gt_m = ES1.enter_context(nc.sbuf_tensor("Gt_m", [128, 128], BF16))
        Eex = ES1.enter_context(nc.sbuf_tensor("Eex", [128, 2 * 1024], BF16))
        Mt = ES1.enter_context(nc.sbuf_tensor("Mt", [128, 2 * 1024], BF16))
        y2 = ES1.enter_context(nc.sbuf_tensor("y2", [128, D_INNER], BF16))
        y3 = ES1.enter_context(nc.sbuf_tensor("y3", [128, D_INNER], BF16))
        y4asm = ES1.enter_context(nc.sbuf_tensor("y4asm", [128, D_INNER], BF16))
        nrm_sm = ES1.enter_context(nc.sbuf_tensor("nrm_sm", [128, 8], F32))
        with ES1:
         if True:
            @block.gpsimd
            def _(gpsimd):
                wait(gpsimd, "p0_consts")
                gpsimd.memset(sel32[:], 0.0)
                gpsimd.drain()
                gpsimd.affine_select(out=sel32[:].rearrange("k (h m) -> k h m", h=32),
                                     in_=sel32[:].rearrange("k (h m) -> k h m", h=32),
                                     compare_op=ALU.not_equal, fill=1.0, base=0,
                                     pattern=[[-1, 32], [0, 128]], channel_multiplier=1)
                gpsimd.drain()
                inst = gpsimd.engine_nop()
                sP.inc(inst)
            tag("sel_done", sP)

            @block.vector
            def _(vector):
                for cc in range(18):
                    vector.memset(xbc[:, cc * (SL + 3):cc * (SL + 3) + 3], 0.0)
                vector.memset(S_sb[:], 0.0)
                vector.memset(Pa[:, 0:1], 0.0)
                vector.memset(nPa[:, 0:1], 0.0)
                inst = vector.drain()
                sV.inc(inst)
            tag("tails_s0", sV)

            def xbc_t(cc):
                return xbc[:, cc * (SL + 3):(cc + 1) * (SL + 3)]

            ipg = 0
            n_slices = n_sl_pref + n_sl_main
            NTT = SL // 128

            for sl_i in range(n_slices):
                is_pref = sl_i < n_sl_pref
                goff = sl_i * SL
                moff = goff - NT_PREF
                x_src = d_xp if is_pref else d_xm
                x_off = goff if is_pref else moff
                P_ = f"s{sl_i}_"
                prev_last = goff // Q - 1

                def xn_cols(dc, c0, cn, is_pref=is_pref, moff=moff):
                    if is_pref:
                        return zs[:, dc * SL + c0: dc * SL + c0 + cn]
                    return xn_fm[:, dc * NT_MAIN + moff + c0: dc * NT_MAIN + moff + c0 + cn]

                for tt in range(NTT):
                    par = tt % 2
                    xt = x_tile[:, par * D_MODEL:(par + 1) * D_MODEL]
                    xnt = xn_tm[:, par * D_MODEL:(par + 1) * D_MODEL]
                    ls = par * 8
                    gtt = sl_i * NTT + tt

                    @block.sync
                    def _(sync, tt=tt, xt=xt, x_src=x_src, x_off=x_off, P_=P_, gtt=gtt):
                        if gtt >= 2:
                            wait(sync, f"g{gtt - 2}_lnx")
                        inst = sync.dma_start(out=xt[:],
                                              in_=x_src[x_off + tt * 128: x_off + (tt + 1) * 128, :])
                        dX[gtt % 2].inc(inst, 16)
                    tag(f"g{gtt}_xld", dX[gtt % 2])

                    @block.vector
                    def _(vector, xt=xt, ls=ls, gtt=gtt):
                        wait(vector, f"g{gtt}_xld")
                        vector.reduce_sum(lnst[:, ls:ls + 1], xt[:], axis=AX)
                        vector.drain()
                        inst = vector.tensor_scalar_mul(lnst[:, ls + 1:ls + 2], lnst[:, ls:ls + 1], 1.0 / D_MODEL)
                        sV.inc(inst)
                        vector.drain()
                    tag(f"g{gtt}_mean", sV)

                    @block.scalar
                    def _(scalar, xt=xt, ls=ls, gtt=gtt):
                        wait(scalar, f"g{gtt}_xld")
                        if gtt >= 2:
                            wait(scalar, f"g{gtt - 2}_var")
                        inst = scalar.activation(xn_tm[:, (ls // 8) * D_MODEL:(ls // 8 + 1) * D_MODEL],
                                                 xt[:], AF.Square, accum_out=lnst[:, ls + 2:ls + 3])
                        sS.inc(inst)
                        scalar.drain()
                    tag(f"g{gtt}_sumsq", sS)

                    @block.vector
                    def _(vector, ls=ls, gtt=gtt):
                        wait(vector, f"g{gtt}_sumsq")
                        wait(vector, f"g{gtt}_mean")
                        vector.tensor_scalar(lnst[:, ls + 3:ls + 4], lnst[:, ls + 1:ls + 2],
                                             lnst[:, ls + 1:ls + 2], None, ALU.mult)
                        vector.drain()
                        inst = vector.scalar_tensor_tensor(lnst[:, ls + 4:ls + 5], lnst[:, ls + 2:ls + 3],
                                                           1.0 / D_MODEL, lnst[:, ls + 3:ls + 4],
                                                           ALU.mult, ALU.subtract)
                        sV.inc(inst)
                        vector.drain()
                    tag(f"g{gtt}_var", sV)

                    @block.scalar
                    def _(scalar, ls=ls, gtt=gtt):
                        wait(scalar, f"g{gtt}_var")
                        scalar.activation(lnst[:, ls + 5:ls + 6], lnst[:, ls + 4:ls + 5], AF.Ln, bias=eps_col[:])
                        scalar.drain()
                        inst = scalar.activation(lnst[:, ls + 6:ls + 7], lnst[:, ls + 5:ls + 6], AF.Exp, scale=-0.5)
                        sS.inc(inst)
                        scalar.drain()
                    tag(f"g{gtt}_sd", sS)

                    @block.vector
                    def _(vector, xt=xt, xnt=xnt, ls=ls, gtt=gtt):
                        wait(vector, f"g{gtt}_sd")
                        if gtt >= 2:
                            wait(vector, f"g{gtt - 2}_xntr")
                        inst = vector.tensor_scalar(xnt[:], xt[:], lnst[:, ls + 1:ls + 2],
                                                    lnst[:, ls + 6:ls + 7], ALU.subtract, ALU.mult)
                        sV.inc(inst)
                        vector.drain()
                    tag(f"g{gtt}_lnx", sV)

                    @block.tensor
                    def _(tensor, xnt=xnt, gtt=gtt):
                        wait(tensor, f"g{gtt}_lnx")
                        wait(tensor, "p0_consts")
                        if gtt >= 1:
                            wait(tensor, f"g{gtt - 1}_xnev")
                        for dc in range(8):
                            inst = tensor.transpose(ps_e[:, dc * 128:(dc + 1) * 128],
                                                    xnt[:, dc * 128:(dc + 1) * 128], ident_f[:])
                        sT.inc(inst)
                    tag(f"g{gtt}_xntr", sT)

                    @block.scalar
                    def _(scalar, tt=tt, gtt=gtt, is_pref=is_pref, moff=moff):
                        wait(scalar, f"g{gtt}_xntr")
                        if is_pref:
                            dst = zs[:].rearrange("p (dc t) -> p dc t", dc=16)[:, 0:8, tt * 128:(tt + 1) * 128]
                        else:
                            dst = xn_fm[:].rearrange("p (dc t) -> p dc t", dc=8)[:, :, moff + tt * 128: moff + (tt + 1) * 128]
                        inst = scalar.copy(dst, ps_e[:].rearrange("p (dc i) -> p dc i", dc=8))
                        sS.inc(inst)
                        scalar.drain()
                    tag(f"g{gtt}_xnev", sS)
                tag(f"{P_}xn_done", sS)

                if sl_i > 0:
                    @block.vector
                    def _(vector):
                        wait(vector, f"s{sl_i - 1}_convdone")
                        for cc in range(18):
                            vector.tensor_copy(xbc_t(cc)[:, 0:3], xbc_t(cc)[:, SL:SL + 3])
                        inst = vector.drain()
                        sV.inc(inst)
                    tag(f"{P_}tails", sV)
                else:
                    tags[f"{P_}tails"] = tags["tails_s0"]

                cc_list = list(range(17)) if is_pref else list(range(18))

                def emit_conv(cc, P_=P_, sl_i=sl_i, prev_last=prev_last):
                    vpar = cc % 2
                    cvx = cva if vpar == 0 else cvb

                    def conv_body(eng, sem, stt_ok, cc=cc, P_=P_, sl_i=sl_i, cvx=cvx, vpar=vpar):
                        wait(eng, H[f"{P_}ip_{OC_X0 + cc}"])
                        if cc > 1:
                            wait(eng, f"{P_}cvs{cc - 2}")
                        elif sl_i > 0:
                            lastcc = (16 - vpar) if sl_i <= n_sl_pref else (16 + vpar)
                            wait(eng, f"s{sl_i - 1}_cvs{lastcc}")
                        srcp = xbc_t(cc)
                        w = lambda k: convw_sb[:, cc * 4 + k: cc * 4 + k + 1]
                        if stt_ok:
                            eng.tensor_scalar(cvx[:, 0:SL], srcp[:, 0:SL], w(0), None, ALU.mult)
                            eng.drain()
                            eng.scalar_tensor_tensor(cvx[:, SL:2 * SL], srcp[:, 1:1 + SL], w(1), cvx[:, 0:SL], ALU.mult, ALU.add)
                            eng.drain()
                            eng.scalar_tensor_tensor(cvx[:, 0:SL], srcp[:, 2:2 + SL], w(2), cvx[:, SL:2 * SL], ALU.mult, ALU.add)
                            eng.drain()
                            inst = eng.scalar_tensor_tensor(cvx[:, SL:2 * SL], srcp[:, 3:3 + SL], w(3), cvx[:, 0:SL], ALU.mult, ALU.add)
                            sem.inc(inst)
                            eng.drain()
                        else:
                            eng.tensor_scalar(cvx[:, SL:2 * SL], srcp[:, 0:SL], w(0), None, ALU.mult)
                            eng.drain()
                            for k in (1, 2, 3):
                                eng.tensor_scalar(cvx[:, 0:SL], srcp[:, k:k + SL], w(k), None, ALU.mult)
                                eng.drain()
                                inst = eng.tensor_tensor(cvx[:, SL:2 * SL], cvx[:, SL:2 * SL], cvx[:, 0:SL], ALU.add)
                                eng.drain()
                            sem.inc(inst)
                            eng.drain()

                    @block.vector
                    def _(vector, conv_body=conv_body):
                        conv_body(vector, sV, True)
                    tag(f"{P_}cv{cc}", sV)

                    @block.scalar
                    def _(scalar, cc=cc, P_=P_, prev_last=prev_last, cvx=cvx):
                        wait(scalar, f"{P_}cv{cc}")
                        if prev_last >= 0:
                            wait(scalar, f"c{prev_last}_sup")
                        if cc < 16:
                            dst = xsilu[:, cc * SL:(cc + 1) * SL]
                        elif cc == 16:
                            dst = B_sl[:]
                        else:
                            dst = C_sl[:]
                        inst = scalar.activation(dst, cvx[:, SL:2 * SL], AF.Silu, bias=convb_sb[:, cc:cc + 1])
                        sS.inc(inst)
                        scalar.drain()
                    tag(f"{P_}cvs{cc}", sS)

                def emit_dt_chain(P_=P_, sl_i=sl_i, prev_last=prev_last, is_pref=is_pref):
                    @block.vector
                    def _(vector, P_=P_):
                        wait(vector, H[f"{P_}ip_{OC_DT}"])
                        wait(vector, "p0_acol")
                        inst = vector.tensor_scalar_add(dts[:, SL:2 * SL], dts[:, 0:SL], dtb_sb[:])
                        sV.inc(inst)
                        vector.drain()
                    tag(f"{P_}dt0", sV)

                    @block.scalar
                    def _(scalar, P_=P_):
                        wait(scalar, f"{P_}dt0")
                        scalar.activation(dts[:, 2 * SL:3 * SL], dts[:, SL:2 * SL], AF.Abs)
                        scalar.drain()
                        inst = scalar.activation(dts[:, 2 * SL:3 * SL], dts[:, 2 * SL:3 * SL], AF.Exp, scale=-1.0)
                        sS.inc(inst)
                        scalar.drain()
                    tag(f"{P_}dt1", sS)

                    @block.vector
                    def _(vector, P_=P_):
                        wait(vector, f"{P_}dt1")
                        inst = vector.tensor_scalar_add(dts[:, 2 * SL:3 * SL], dts[:, 2 * SL:3 * SL], 1.0)
                        sV.inc(inst)
                        vector.drain()
                    tag(f"{P_}dt2", sV)

                    @block.scalar
                    def _(scalar, P_=P_):
                        wait(scalar, f"{P_}dt2")
                        inst = scalar.activation(dts[:, 2 * SL:3 * SL], dts[:, 2 * SL:3 * SL], AF.Ln)
                        sS.inc(inst)
                        scalar.drain()
                    tag(f"{P_}dt3", sS)

                    @block.vector
                    def _(vector, P_=P_, sl_i=sl_i, prev_last=prev_last):
                        wait(vector, f"{P_}dt3")
                        if prev_last >= 0:
                            wait(vector, f"c{prev_last}_sup")
                        if sl_i > 0:
                            vector.tensor_copy(Pa[:, 0:1], Pa[:, SL:SL + 1])
                            vector.drain()
                        vector.tensor_scalar_max(dts[:, 3 * SL:4 * SL], dts[:, SL:2 * SL], 0.0)
                        vector.drain()
                        vector.tensor_add(dts[:, 0:SL], dts[:, 3 * SL:4 * SL], dts[:, 2 * SL:3 * SL])
                        vector.drain()
                        vector.tensor_scalar_mul(dts[:, 3 * SL:4 * SL], dts[:, 0:SL], acol_sb[:])
                        vector.drain()
                        vector.tensor_tensor_scan(Pa[:, 1:1 + SL], ones32[:],
                                                  dts[:, 3 * SL:4 * SL], Pa[:, 0:1],
                                                  ALU.mult, ALU.add)
                        vector.drain()
                        inst = vector.tensor_scalar_mul(nPa[:, 0:1 + SL], Pa[:, 0:1 + SL], -1.0)
                        sV.inc(inst)
                        vector.drain()
                    tag(f"{P_}pa", sV)

                    if not is_pref:
                        @block.vector
                        def _(vector, P_=P_):
                            wait(vector, f"{P_}pa")
                            npc = bass.AP(tensor=nPa[:].tensor, offset=0,
                                          ap=[[SL + 1, 32], [128, SL // 128]]).unsqueeze(2).broadcast_to([32, SL // 128, 128])
                            inst = vector.tensor_tensor(
                                npp_all[:].rearrange("h (c i) -> h c i", c=SL // 128),
                                ones32[:].rearrange("h (c i) -> h c i", c=SL // 128), npc, ALU.mult)
                            sV.inc(inst)
                            vector.drain()
                        tag(f"{P_}nppall", sV)

                    @block.vector
                    def _(vector, P_=P_):
                        wait(vector, f"{P_}pa")
                        pe = bass.AP(tensor=Pa[:].tensor, offset=128, ap=[[SL + 1, 32], [128, SL // 128]])
                        pb = bass.AP(tensor=Pa[:].tensor, offset=0, ap=[[SL + 1, 32], [128, SL // 128]])
                        inst = vector.tensor_tensor(dec_sm[:, 0:SL // 128], pe, pb, ALU.subtract)
                        sV.inc(inst)
                        vector.drain()
                    tag(f"{P_}dec0", sV)

                oc_list = [OC_DT] + list(range(OC_X0, OC_DT)) + ([] if is_pref else list(range(OC_X0)))
                for j_loc, oc in enumerate(oc_list):
                    par = ipg % 2
                    wb = win_buf[:, par * 1024:(par + 1) * 1024]
                    psd = ps_en2[:, 0:512] if par == 0 else ps_en2[:, 512:1024]
                    jg = ipg

                    @block.gpsimd
                    def _(gpsimd, oc=oc, wb=wb, jg=jg):
                        if jg >= 2:
                            wait(gpsimd, f"ip{jg - 2}_mm")
                        ncols = 128 if oc != OC_DT else 32
                        inst = gpsimd.dma_start(
                            out=wb[:].rearrange("p (ic o) -> p ic o", ic=8)[:, :, 0:ncols],
                            in_=d_winT.rearrange("(ic p) o -> p ic o", p=128)[:, :, oc * 128: oc * 128 + ncols])
                        dI[jg % 2].inc(inst, 16)
                    tag(f"ip{jg}_w", dI[jg % 2])

                    @block.tensor
                    def _(tensor, oc=oc, wb=wb, psd=psd, jg=jg, j_loc=j_loc, P_=P_,
                          prev_last=prev_last, sl_i=sl_i):
                        wait(tensor, f"ip{jg}_w")
                        wait(tensor, f"{P_}xn_done")
                        if jg >= 2:
                            wait(tensor, f"ip{jg - 2}_ev")
                        if j_loc < 2 and prev_last >= 0:
                            wait(tensor, f"c{prev_last}_sup")
                        ncols = 128 if oc != OC_DT else 32
                        for ic in range(8):
                            inst = tensor.matmul(psd[0:ncols, :],
                                                 wb[:].rearrange("p (ic o) -> p ic o", ic=8)[:, ic, 0:ncols],
                                                 xn_cols(ic, 0, SL), start=(ic == 0), stop=(ic == 7))
                        sT.inc(inst)
                    tag(f"ip{jg}_mm", sT)

                    if oc < OC_X0:
                        @block.scalar
                        def _(scalar, oc=oc, psd=psd, jg=jg):
                            wait(scalar, f"ip{jg}_mm")
                            inst = scalar.activation(zs[:, oc * SL:(oc + 1) * SL], psd[:], AF.Silu)
                            sS.inc(inst)
                            scalar.drain()
                        tag(f"ip{jg}_ev", sS)
                    elif oc == OC_DT:
                        @block.vector
                        def _(vector, psd=psd, jg=jg):
                            wait(vector, f"ip{jg}_mm")
                            inst = vector.tensor_copy(dts[:, 0:SL], psd[0:32, :])
                            sV.inc(inst)
                            vector.drain()
                        tag(f"ip{jg}_ev", sV)
                    else:
                        cc = oc - OC_X0

                        @block.scalar
                        def _(scalar, cc=cc, psd=psd, jg=jg, P_=P_):
                            wait(scalar, f"ip{jg}_mm")
                            wait(scalar, f"{P_}tails")
                            inst = scalar.copy(xbc_t(cc)[:, 3:3 + SL], psd[:])
                            sS.inc(inst)
                            scalar.drain()
                        tag(f"ip{jg}_ev", sS)
                    H[f"{P_}ip_{oc}"] = f"ip{jg}_ev"
                    ipg += 1
                    if oc == OC_DT:
                        emit_dt_chain()
                    elif OC_X0 <= oc < OC_DT and (oc - OC_X0) in cc_list:
                        emit_conv(oc - OC_X0)
                tags[f"{P_}convdone"] = tags[f"{P_}cvs{cc_list[-1]}"]

                @block.scalar
                def _(scalar, P_=P_):
                    wait(scalar, f"{P_}dec0")
                    inst = scalar.activation(dec_sm[:, 4:4 + SL // 128], dec_sm[:, 0:SL // 128], AF.Exp)
                    sS.inc(inst)
                    scalar.drain()
                tag(f"{P_}dec1", sS)

                @block.tensor
                def _(tensor, P_=P_):
                    wait(tensor, f"{P_}dec1")
                    if goff > 0:
                        wait(tensor, f"c{goff // Q - 1}_sup")
                    inst = tensor.transpose(ps_en2[0:SL // 128, 448:480], dec_sm[:, 4:4 + SL // 128],
                                            ident_f[0:32, 0:32])
                    sT.inc(inst)
                tag(f"{P_}dectr", sT)

                @block.vector
                def _(vector, P_=P_):
                    wait(vector, f"{P_}dectr")
                    inst = vector.tensor_copy(decrow[0:SL // 128, :], ps_en2[0:SL // 128, 448:480])
                    sV.inc(inst)
                    vector.drain()
                tag(f"{P_}decrows", sV)

                @block.scalar
                def _(scalar, P_=P_, prev_last=prev_last):
                    wait(scalar, f"{P_}pa")
                    if prev_last >= 0:
                        wait(scalar, f"c{prev_last}_rrtm")
                    for c4 in range(SL // Q):
                        inst = scalar.activation(rr_all[:, c4 * 128:(c4 + 1) * 128],
                                                 Pa[:, c4 * 128 + 1:(c4 + 1) * 128 + 1], AF.Exp,
                                                 scale=-1.0,
                                                 bias=Pa[:, (c4 + 1) * 128:(c4 + 1) * 128 + 1])
                        scalar.drain()
                    sS.inc(inst)
                tag(f"{P_}rrall", sS)

                for cl in range(SL // Q):
                    gch = goff // Q + cl
                    col = cl * Q
                    CP = f"c{gch}_"
                    pcol = cl * Q
                    psbig_bf = ps_big[:].bitcast(BF16)
                    en2_bf = ps_en2[:].bitcast(BF16)

                    @block.tensor
                    def _(tensor, col=col, CP=CP, P_=P_, gch=gch, is_pref=is_pref):
                        wait(tensor, f"{P_}convdone")
                        wait(tensor, f"{P_}rrall")
                        if not is_pref:
                            wait(tensor, H[f"{P_}ip_15"])
                        if gch > 0:
                            wait(tensor, f"c{gch - 1}_psbig_free")
                            wait(tensor, f"c{gch - 1}_sup")
                        for cc in range(16):
                            tensor.transpose(psbig_bf[:, cc * 128:(cc + 1) * 128],
                                             xsilu[:, cc * SL + col: cc * SL + col + 128], ident_b[:])
                        if not is_pref:
                            for cc in range(16):
                                tensor.transpose(psbig_bf[:, 2048 + cc * 128: 2048 + (cc + 1) * 128],
                                                 zs[:, cc * SL + col: cc * SL + col + 128], ident_b[:])
                        tensor.transpose(ps_en2[:, 0:32], dts[:, col:col + 128], ident_f[0:32, 0:32])
                        tensor.transpose(ps_en2[:, 32:64], rr_all[:, col:col + 128], ident_f[0:32, 0:32])
                        inst = tensor.transpose(en2_bf[:, 128:256], B_sl[:, col:col + 128], ident_b[:])
                        if not is_pref:
                            inst = tensor.matmul(ps_en2[:, 256:384], B_sl[:, col:col + 128],
                                                 C_sl[:, col:col + 128], start=True, stop=True)
                        sT.inc(inst)
                    tag(f"{CP}xtr", sT)

                    @block.vector
                    def _(vector, CP=CP, gch=gch, is_pref=is_pref):
                        wait(vector, f"{CP}xtr")
                        vector.tensor_copy(small_tm[:, 0:64], ps_en2[:, 0:64])
                        vector.drain()
                        vector.tensor_copy(B_tm[:], en2_bf[:, 128:256])
                        if not is_pref:
                            vector.tensor_tensor(Gt_m[:], ps_en2[:, 256:384], mask01[:], ALU.mult)
                            vector.drain()
                            vector.tensor_copy(zs_tm[:], psbig_bf[:, 2048:4096])
                        vector.drain()
                        dt_rep = small_tm[:, 0:32].unsqueeze(2).broadcast_to([128, 32, 64])
                        rr_rep = small_tm[:, 32:64].unsqueeze(2).broadcast_to([128, 32, 64])
                        xv = lambda t: t[:].rearrange("p (h e) -> p h e", h=32)
                        vector.tensor_tensor(xv(Xd), psbig_bf[:, 0:2048].rearrange("p (h e) -> p h e", h=32),
                                             dt_rep, ALU.mult)
                        vector.drain()
                        vector.tensor_tensor(xv(Xw), xv(Xd), rr_rep, ALU.mult)
                        vector.drain()
                        vector.reciprocal(small_tm[:, 96:128], small_tm[:, 0:32])
                        vector.drain()
                        vector.tensor_tensor(small_tm[:, 64:96], dpar_sb[:], small_tm[:, 96:128], ALU.mult)
                        vector.drain()
                        ddt_rep = small_tm[:, 64:96].unsqueeze(2).broadcast_to([128, 32, 64])
                        inst = vector.tensor_tensor(xv(XDt), xv(Xd), ddt_rep, ALU.mult)
                        sV.inc(inst)
                        vector.drain()
                    tag(f"{CP}xd", sV)
                    tag(f"{CP}rrtm", sV)

                    if not is_pref:
                        def emm_round(tensor, r, CP=CP, pcol=pcol):
                            ps = ps_e if r % 2 == 0 else ps_en2
                            if r >= 2:
                                wait(tensor, f"{CP}eex{r - 2}")
                            if r < 4:
                                tensor.matmul(ps[:, 0:512], ident_b[:], maskneg8[:, 0:512],
                                              start=True, stop=False, skip_group_check=True)
                                tensor.matmul(ps[:, 512:1024], ident_b[:], maskneg8[:, 512:1024],
                                              start=True, stop=False, skip_group_check=True)
                            for hh in range(8):
                                h = (r % 4) * 8 + hh
                                sl8 = ps[:, hh * 128:(hh + 1) * 128]
                                selh = sel32[:, h * 128:(h + 1) * 128]
                                if r < 4:
                                    tensor.matmul(sl8, selh, Pa[0:32, pcol + 1:pcol + 129],
                                                  start=False, stop=False, skip_group_check=True)
                                    inst = tensor.matmul(sl8, nPa[0:32, pcol + 1:pcol + 129], selh,
                                                         start=False, stop=(hh == 7), skip_group_check=True)
                                else:
                                    tensor.matmul(sl8, selh, Pa[0:32, pcol + 1:pcol + 129],
                                                  start=True, stop=False, skip_group_check=True)
                                    inst = tensor.matmul(sl8, npp_all[:, pcol:pcol + 128], selh,
                                                         start=False, stop=True, skip_group_check=True)
                            return inst

                        def y_round(tensor, r, CP=CP):
                            wait(tensor, f"{CP}mt{r}")
                            if r == 0:
                                wait(tensor, f"{CP}xd")
                                for b4 in range(4):
                                    tensor.matmul(ps_big[:, b4 * 512:(b4 + 1) * 512], ones_row_b[:],
                                                  zero_row[:], start=True, stop=False,
                                                  skip_group_check=True)
                            for hh in range(8):
                                h = (r % 4) * 8 + hh
                                mtb = Mt[:, (r % 2) * 1024 + hh * 128:(r % 2) * 1024 + (hh + 1) * 128]
                                if r < 4:
                                    inst = tensor.matmul(ps_big[:, h * 64:(h + 1) * 64], mtb,
                                                         Xd[:, h * 64:(h + 1) * 64],
                                                         start=False, stop=False, skip_group_check=True)
                                else:
                                    inst = tensor.matmul(ps_big[:, h * 64:(h + 1) * 64], mtb,
                                                         S_sb[:, h * 64:(h + 1) * 64],
                                                         start=False, stop=(r == 7 and hh == 7),
                                                         skip_group_check=True)
                            return inst

                        @block.tensor
                        def _(tensor, CP=CP, gch=gch):
                            wait(tensor, f"{CP}rrtm")
                            if gch == nch_pref:
                                wait(tensor, "sel_done")
                            inst = emm_round(tensor, 0)
                            sT.inc(inst)
                        tag(f"{CP}emm0", sT)

                        @block.scalar
                        def _(scalar, CP=CP):
                            wait(scalar, f"{CP}emm0")
                            inst = scalar.activation(Eex[:, 0:1024], ps_e[:], AF.Exp)
                            sS.inc(inst)
                            scalar.drain()
                        tag(f"{CP}eex0", sS)

                        @block.vector
                        def _(vector, CP=CP, col=col):
                            wait(vector, f"{CP}eex0")
                            gt_rep = Gt_m[:].unsqueeze(1).broadcast_to([128, 8, 128])
                            inst = vector.tensor_tensor(
                                Mt[:, 0:1024].rearrange("p (h i) -> p h i", h=8),
                                Eex[:, 0:1024].rearrange("p (h i) -> p h i", h=8), gt_rep, ALU.mult)
                            sV.inc(inst)
                            vector.drain()
                        tag(f"{CP}mt0", sV)

                        for r in range(1, 8):
                            @block.tensor
                            def _(tensor, r=r, CP=CP):
                                inst = emm_round(tensor, r)
                                sT.inc(inst)
                            tag(f"{CP}emm{r}", sT)

                            @block.scalar
                            def _(scalar, r=r, CP=CP):
                                wait(scalar, f"{CP}emm{r}")
                                if r >= 2:
                                    wait(scalar, f"{CP}mt{r - 2}")
                                ps = ps_e if r % 2 == 0 else ps_en2
                                inst = scalar.activation(Eex[:, (r % 2) * 1024:(r % 2) * 1024 + 1024],
                                                         ps[:], AF.Exp)
                                sS.inc(inst)
                                scalar.drain()
                            tag(f"{CP}eex{r}", sS)

                            @block.vector
                            def _(vector, r=r, CP=CP, col=col):
                                wait(vector, f"{CP}eex{r}")
                                if r >= 2:
                                    wait(vector, f"{CP}y{r - 2}")
                                rep = (Gt_m[:] if r < 4 else C_sl[:, col:col + 128]).unsqueeze(1).broadcast_to([128, 8, 128])
                                inst = vector.tensor_tensor(
                                    Mt[:, (r % 2) * 1024:(r % 2) * 1024 + 1024].rearrange("p (h i) -> p h i", h=8),
                                    Eex[:, (r % 2) * 1024:(r % 2) * 1024 + 1024].rearrange("p (h i) -> p h i", h=8),
                                    rep, ALU.mult)
                                sV.inc(inst)
                                vector.drain()
                            tag(f"{CP}mt{r}", sV)

                            @block.tensor
                            def _(tensor, r=r, CP=CP):
                                inst = y_round(tensor, r - 1)
                                sT.inc(inst)
                            tag(f"{CP}y{r - 1}", sT)

                        @block.tensor
                        def _(tensor, CP=CP):
                            inst = y_round(tensor, 7)
                            sT.inc(inst)
                        tag(f"{CP}y7", sT)

                        @block.vector
                        def _(vector, CP=CP):
                            wait(vector, f"{CP}y7")
                            inst = vector.tensor_tensor(y2[:], ps_big[:], XDt[:], ALU.add)
                            sV.inc(inst)
                            vector.drain()
                        tag(f"{CP}y2", sV)

                        @block.vector
                        def _(vector, CP=CP):
                            inst = vector.tensor_tensor(y3[:], y2[:], zs_tm[:], ALU.mult)
                            sV.inc(inst)
                            vector.drain()
                        tag(f"{CP}y3", sV)

                        @block.scalar
                        def _(scalar, CP=CP):
                            wait(scalar, f"{CP}y3")
                            inst = scalar.activation(y2[:], y3[:], AF.Square, accum_out=nrm_sm[:, 0:1])
                            sS.inc(inst)
                            scalar.drain()
                        tag(f"{CP}sqy", sS)

                        @block.vector
                        def _(vector, CP=CP):
                            wait(vector, f"{CP}sqy")
                            inst = vector.tensor_scalar(nrm_sm[:, 1:2], nrm_sm[:, 0:1], 1.0 / D_INNER,
                                                        EPS, ALU.mult, ALU.add)
                            sV.inc(inst)
                            vector.drain()
                        tag(f"{CP}mse", sV)

                        @block.scalar
                        def _(scalar, CP=CP):
                            wait(scalar, f"{CP}mse")
                            scalar.activation(nrm_sm[:, 2:3], nrm_sm[:, 1:2], AF.Ln)
                            scalar.drain()
                            inst = scalar.activation(nrm_sm[:, 3:4], nrm_sm[:, 2:3], AF.Exp, scale=-0.5)
                            sS.inc(inst)
                            scalar.drain()
                        tag(f"{CP}rstd", sS)

                        @block.vector
                        def _(vector, CP=CP, gch=gch):
                            wait(vector, f"{CP}rstd")
                            inst = vector.tensor_scalar(y3[:], y3[:], nrm_sm[:, 3:4], None, ALU.mult)
                            sV.inc(inst)
                            vector.drain()
                        tag(f"{CP}y4", sV)

                        tag_schunk_ready = f"{CP}y2"
                    else:
                        tag_schunk_ready = f"{CP}xd"

                    @block.tensor
                    def _(tensor, CP=CP, tag_schunk_ready=tag_schunk_ready):
                        wait(tensor, tag_schunk_ready)
                        for j4 in range(4):
                            inst = tensor.matmul(ps_big[:, j4 * 512:(j4 + 1) * 512], B_tm[:],
                                                 Xw[:, j4 * 512:(j4 + 1) * 512], start=True, stop=True)
                        sT.inc(inst)
                    tag(f"{CP}sch", sT)

                    @block.tensor
                    def _(tensor, CP=CP, cl=cl, P_=P_):
                        wait(tensor, f"{P_}decrows")
                        wait(tensor, f"{CP}sch")
                        inst = tensor.matmul(ps_en2[:, 480:512], sel32[0:SL // 128, cl * 128:(cl + 1) * 128],
                                             decrow[0:SL // 128, :], start=True, stop=True)
                        sT.inc(inst)
                    tag(f"{CP}decb", sT)

                    @block.vector
                    def _(vector, CP=CP, gch=gch):
                        wait(vector, f"{CP}decb")
                        dec_rep = ps_en2[:, 480:512].unsqueeze(2).broadcast_to([128, 32, 64])
                        vector.tensor_tensor(S_t1[:].rearrange("p (h e) -> p h e", h=32),
                                             S_sb[:].rearrange("p (h e) -> p h e", h=32), dec_rep, ALU.mult)
                        vector.drain()
                        inst = vector.tensor_tensor(S_sb[:], S_t1[:], ps_big[:], ALU.add)
                        sV.inc(inst)
                        vector.drain()
                        if gch == nch_pref - 1:
                            inst = vector.tensor_scalar_mul(S_sb[:], S_sb[:], flag_sb[:])
                            sV.inc(inst)
                            vector.drain()
                    tag(f"{CP}sup", sV)

                    if not is_pref:
                        @block.tensor
                        def _(tensor, CP=CP, gch=gch):
                            wait(tensor, f"{CP}y4")
                            wait(tensor, f"{CP}sup")
                            for ec in range(16):
                                inst = tensor.transpose(psbig_bf[:, ec * 128:(ec + 1) * 128],
                                                        y3[:, ec * 128:(ec + 1) * 128], ident_b[:])
                            sT.inc(inst)
                        tag(f"{CP}y4tr", sT)

                        @block.scalar
                        def _(scalar, CP=CP, gch=gch):
                            wait(scalar, f"{CP}y4tr")
                            if gch - 1 >= nch_pref:
                                wait(scalar, f"c{gch - 1}_y4out")
                            inst = scalar.copy(y4asm[:], psbig_bf[:, 0:2048])
                            sS.inc(inst)
                            scalar.drain()
                        tag(f"{CP}y4ev", sS)
                        tag(f"{CP}psbig_free2", sS)

                        @block.sync
                        def _(sync, CP=CP, gch=gch):
                            wait(sync, f"{CP}y4ev")
                            mch = gch - nch_pref
                            inst = sync.dma_start(out=d_y4.rearrange("(c p) f -> c p f", p=128)[mch],
                                                  in_=y4asm[:])
                            dY.inc(inst, 16)
                        tag(f"{CP}y4out", dY)

                    if is_pref:
                        tags[f"{CP}psbig_free"] = tags[f"{CP}sup"]
                    else:
                        tags[f"{CP}psbig_free"] = tags[f"{CP}psbig_free2"]

            last_sup = f"c{NTOT // Q - 1}_sup"

        ESm = contextlib.ExitStack()
        ffn_acc = ESm.enter_context(nc.sbuf_tensor("ffn_acc", [128, 16 * D_MODEL], BF16))
        ES2 = contextlib.ExitStack()
        hbuf = ES2.enter_context(nc.sbuf_tensor("hbuf", [128, 16 * NT_MAIN], BF16))
        wd_sb = ES2.enter_context(nc.sbuf_tensor("wd_sb", [128, 16 * D_MODEL], BF16))
        wgu_buf = ES2.enter_context(nc.sbuf_tensor("wgu_buf", [128, 2 * 2048], BF16))
        sg_sb = ES2.enter_context(nc.sbuf_tensor("sg_sb", [128, 2 * 512], BF16))
        lgs = ES2.enter_context(nc.sbuf_tensor("lgs", [128, 40], F32))
        with ES2:
         if True:
            for tc in range(n_tc):
                @block.tensor
                def _(tensor, tc=tc):
                    wait(tensor, last_sup)
                    wait(tensor, "p0_dG")
                    if tc > 0:
                        wait(tensor, f"rt{tc - 1}_lg")
                    for dc in range(8):
                        inst = tensor.matmul(ps_en2[:, 0:8],
                                             xn_fm[:, dc * NT_MAIN + tc * 128: dc * NT_MAIN + (tc + 1) * 128],
                                             gwT_sb[:, dc * 8:(dc + 1) * 8],
                                             start=(dc == 0), stop=(dc == 7))
                    sT.inc(inst)
                tag(f"rt{tc}_mm", sT)

                @block.vector
                def _(vector, tc=tc):
                    wait(vector, f"rt{tc}_mm")
                    if tc > 0:
                        wait(vector, f"rt{tc - 1}_sg")
                    inst = vector.tensor_copy(lgs[:, 0:8], ps_en2[:, 0:8])
                    sV.inc(inst)
                    vector.drain()
                tag(f"rt{tc}_lg", sV)

                @block.scalar
                def _(scalar, tc=tc):
                    wait(scalar, f"rt{tc}_lg")
                    inst = scalar.activation(lgs[:, 8:16], lgs[:, 0:8], AF.Sigmoid)
                    sS.inc(inst)
                    scalar.drain()
                tag(f"rt{tc}_sg", sS)

                @block.vector
                def _(vector, tc=tc):
                    wait(vector, f"rt{tc}_sg")
                    cslot = c_tm[:, tc * 8:(tc + 1) * 8]
                    vector.tensor_tensor(cslot, lgs[:, 0:8], beta_sb[:], ALU.add)
                    vector.drain()
                    vector.reduce_max(lgs[:, 32:33], cslot, axis=AX)
                    vector.drain()
                    vector.tensor_scalar(lgs[:, 16:24], cslot, lgs[:, 32:33], None, ALU.is_ge)
                    vector.drain()
                    vector.scalar_tensor_tensor(lgs[:, 24:32], lgs[:, 16:24], -1e30, cslot, ALU.mult, ALU.add)
                    vector.drain()
                    vector.reduce_max(lgs[:, 33:34], lgs[:, 24:32], axis=AX)
                    vector.drain()
                    vector.tensor_scalar(lgs[:, 24:32], lgs[:, 24:32], lgs[:, 33:34], None, ALU.is_ge)
                    vector.drain()
                    vector.tensor_add(lgs[:, 16:24], lgs[:, 16:24], lgs[:, 24:32])
                    vector.drain()
                    inst = vector.tensor_tensor(cslot, lgs[:, 8:16], lgs[:, 16:24], ALU.mult)
                    sV.inc(inst)
                    vector.drain()
                tag(f"rt{tc}_c", sV)
            tag("router_done", sV)

            for e in range(NEXP + 1):
                is_shared = e == NEXP
                EP = f"e{e}_"
                for mc in range(16):
                    par = mc % 2

                    @block.gpsimd
                    def _(gpsimd, e=e, mc=mc, par=par, EP=EP):
                        if mc >= 2:
                            wait(gpsimd, f"{EP}gu_{mc - 2}_{n_tt - 1}")
                        elif e > 0:
                            wait(gpsimd, f"e{e - 1}_gu_{mc + 14}_{n_tt - 1}")
                        else:
                            wait(gpsimd, last_sup)
                        inst = gpsimd.dma_start(
                            out=wgu_buf[:, par * 2048: par * 2048 + 1024].rearrange("p (hc m) -> p hc m", hc=8),
                            in_=d_wg.rearrange("e (hc p) m -> e p hc m", p=128)[e][:, :, mc * 128:(mc + 1) * 128])
                        dM[mc % 2].inc(inst, 16)
                        inst = gpsimd.dma_start(
                            out=wgu_buf[:, par * 2048 + 1024: par * 2048 + 2048].rearrange("p (hc m) -> p hc m", hc=8),
                            in_=d_wu.rearrange("e (hc p) m -> e p hc m", p=128)[e][:, :, mc * 128:(mc + 1) * 128])
                        dM[mc % 2].inc(inst, 16)
                        if e > 0:
                            wait(gpsimd, f"e{e - 1}_dn{2 * n_tc - 1}")
                        inst = gpsimd.dma_start(
                            out=wd_sb[:, mc * 1024:(mc + 1) * 1024],
                            in_=d_wd.rearrange("e (mc p) hh -> e mc p hh", p=128)[e, mc])
                        dD.inc(inst, 16)
                    tag(f"{EP}w{mc}", dM[mc % 2])
                    tag(f"{EP}wd{mc}", dD)

                    for tt in range(n_tt):
                        j = mc * n_tt + tt
                        qpar = j % 2
                        gps = ps_big[:, qpar * 1024: qpar * 1024 + 512]
                        ups = ps_big[:, qpar * 1024 + 512: qpar * 1024 + 1024]

                        @block.tensor
                        def _(tensor, e=e, mc=mc, tt=tt, j=j, gps=gps, ups=ups, par=par, EP=EP):
                            wait(tensor, f"{EP}w{mc}")
                            if j >= 2:
                                wait(tensor, f"{EP}h_{(j - 2) // n_tt}_{(j - 2) % n_tt}")
                            elif e > 0:
                                wait(tensor, f"e{e - 1}_h_15_{n_tt - 1}")
                            for hc in range(8):
                                tensor.matmul(gps, wgu_buf[:, par * 2048 + hc * 128: par * 2048 + (hc + 1) * 128],
                                              xn_fm[:, hc * NT_MAIN + tt * 512: hc * NT_MAIN + tt * 512 + 512],
                                              start=(hc == 0), stop=(hc == 7))
                            for hc in range(8):
                                inst = tensor.matmul(ups,
                                                     wgu_buf[:, par * 2048 + 1024 + hc * 128: par * 2048 + 1024 + (hc + 1) * 128],
                                                     xn_fm[:, hc * NT_MAIN + tt * 512: hc * NT_MAIN + tt * 512 + 512],
                                                     start=(hc == 0), stop=(hc == 7))
                            sT.inc(inst)
                        tag(f"{EP}gu_{mc}_{tt}", sT)

                        @block.scalar
                        def _(scalar, j=j, mc=mc, tt=tt, qpar=qpar, gps=gps, e=e, EP=EP):
                            wait(scalar, f"{EP}gu_{mc}_{tt}")
                            if j >= 2:
                                wait(scalar, f"{EP}h_{(j - 2) // n_tt}_{(j - 2) % n_tt}")
                            elif e > 0:
                                wait(scalar, f"e{e - 1}_h_15_{n_tt - 1}")
                            inst = scalar.activation(sg_sb[:, qpar * 512:(qpar + 1) * 512], gps, AF.Silu)
                            sS.inc(inst)
                            scalar.drain()
                        tag(f"{EP}sg_{mc}_{tt}", sS)

                        @block.vector
                        def _(vector, j=j, mc=mc, tt=tt, qpar=qpar, ups=ups, e=e, EP=EP):
                            wait(vector, f"{EP}sg_{mc}_{tt}")
                            inst = vector.tensor_tensor(
                                hbuf[:, mc * NT_MAIN + tt * 512: mc * NT_MAIN + tt * 512 + 512],
                                sg_sb[:, qpar * 512:(qpar + 1) * 512], ups, ALU.mult)
                            sV.inc(inst)
                            vector.drain()
                        tag(f"{EP}h_{mc}_{tt}", sV)

                for tc in range(n_tc):
                    for nt in range(2):
                        j = tc * 2 + nt
                        dpar_ = j % 2
                        dps = ps_e[:, dpar_ * 512:(dpar_ + 1) * 512]

                        @block.tensor
                        def _(tensor, e=e, tc=tc, nt=nt, j=j, dps=dps, EP=EP):
                            if j == 0:
                                for mc in range(16):
                                    wait(tensor, f"{EP}h_{mc}_{n_tt - 1}")
                                wait(tensor, f"{EP}wd15")
                                wait(tensor, "router_done")
                            if j >= 2:
                                wait(tensor, f"{EP}dnv{j - 2}")
                            elif e > 0:
                                wait(tensor, f"e{e - 1}_dnv{2 * n_tc - 2 + j}")
                            for mc in range(16):
                                inst = tensor.matmul(dps,
                                                     hbuf[:, mc * NT_MAIN + tc * 128: mc * NT_MAIN + (tc + 1) * 128],
                                                     wd_sb[:, mc * 1024 + nt * 512: mc * 1024 + nt * 512 + 512],
                                                     start=(mc == 0), stop=(mc == 15))
                            sT.inc(inst)
                        tag(f"{EP}dn{j}", sT)

                        @block.vector
                        def _(vector, e=e, tc=tc, nt=nt, j=j, dps=dps, EP=EP, is_shared=is_shared):
                            wait(vector, f"{EP}dn{j}")
                            acc = ffn_acc[:, tc * D_MODEL + nt * 512: tc * D_MODEL + nt * 512 + 512]
                            if e == 0:
                                inst = vector.tensor_scalar_mul(acc, dps, c_tm[:, tc * 8 + e: tc * 8 + e + 1])
                            elif is_shared:
                                inst = vector.tensor_tensor(acc, dps, acc, ALU.add)
                            else:
                                inst = vector.scalar_tensor_tensor(acc, dps, c_tm[:, tc * 8 + e: tc * 8 + e + 1],
                                                                   acc, ALU.mult, ALU.add)
                            sV.inc(inst)
                            vector.drain()
                        tag(f"{EP}dnv{j}", sV)
            tag("moe_done", sV)

        ES3 = contextlib.ExitStack()
        woutT_sb = ES3.enter_context(nc.sbuf_tensor("woutT_sb", [128, 16 * D_MODEL], BF16))
        y4t = ES3.enter_context(nc.sbuf_tensor("y4t", [128, 2 * D_INNER], BF16))
        xres = ES3.enter_context(nc.sbuf_tensor("xres", [128, 2 * D_MODEL], F32))
        gsil = ES3.enter_context(nc.sbuf_tensor("gsil", [128, 2 * D_MODEL], BF16))
        sg_f = ES3.enter_context(nc.sbuf_tensor("sg_f", [128, D_MODEL], BF16))
        otile = ES3.enter_context(nc.sbuf_tensor("otile", [128, 2 * D_MODEL], F32))
        with ES3:
         if True:
            @block.gpsimd
            def _(gpsimd):
                wait(gpsimd, f"e{NEXP}_dn{2 * n_tc - 1}")
                for ec in range(16):
                    inst = gpsimd.dma_start(out=woutT_sb[:, ec * 1024:(ec + 1) * 1024],
                                            in_=d_woutT.rearrange("(ec p) d -> ec p d", p=128)[ec])
                    dWo.inc(inst, 16)
            tag("wout_raw", dWo)

            @block.vector
            def _(vector):
                wait(vector, "wout_raw")
                inst = vector.tensor_tensor(
                    woutT_sb[:].rearrange("p (ec d) -> p ec d", ec=16),
                    woutT_sb[:].rearrange("p (ec d) -> p ec d", ec=16),
                    normw_sb[:].unsqueeze(2).broadcast_to([128, 16, D_MODEL]), ALU.mult)
                sV.inc(inst)
                vector.drain()
            tag("wout_ld", sV)

            for tc in range(n_tc):
                par = tc % 2

                @block.sync
                def _(sync, tc=tc, par=par):
                    wait(sync, f"e{NEXP}_dn{2 * n_tc - 1}")
                    wait(sync, f"c{nch_pref + nch_main - 1}_y4out")
                    if tc >= 2:
                        wait(sync, f"fin{tc - 2}_out")
                    inst = sync.dma_start(out=y4t[:, par * 2048:(par + 1) * 2048],
                                          in_=d_y4.rearrange("(c p) f -> c p f", p=128)[tc])
                    dF[par].inc(inst, 16)
                    inst = sync.dma_start(out=xres[:, par * 1024:(par + 1) * 1024],
                                          in_=d_xm[tc * 128:(tc + 1) * 128, :])
                    dF[par].inc(inst, 16)
                tag(f"fin{tc}_ld", dF[par])

                psF = ps_en2 if par == 0 else ps_e

                @block.tensor
                def _(tensor, tc=tc, par=par, psF=psF):
                    wait(tensor, f"fin{tc}_ld")
                    wait(tensor, "wout_ld")
                    if tc >= 2:
                        wait(tensor, f"fin{tc - 2}_sev")
                    for half in range(2):
                        dpsF = psF[:, 0:512] if half == 0 else psF[:, 512:1024]
                        for ec in range(16):
                            inst = tensor.matmul(dpsF[:],
                                                 y4t[:, par * 2048 + ec * 128: par * 2048 + (ec + 1) * 128],
                                                 woutT_sb[:, ec * 1024 + half * 512: ec * 1024 + half * 512 + 512],
                                                 start=(ec == 0), stop=(ec == 15))
                    sT.inc(inst)
                tag(f"fin{tc}_s", sT)

                @block.scalar
                def _(scalar, tc=tc, par=par):
                    wait(scalar, "moe_done")
                    if tc >= 2:
                        wait(scalar, f"fin{tc - 2}_sev")
                    inst = scalar.activation(gsil[:, par * 1024:(par + 1) * 1024],
                                             ffn_acc[:, tc * D_MODEL:(tc + 1) * D_MODEL], AF.Silu)
                    sS.inc(inst)
                    scalar.drain()
                tag(f"fin{tc}_g", sS)

                @block.vector
                def _(vector, tc=tc, par=par, psF=psF):
                    wait(vector, f"fin{tc}_s")
                    wait(vector, f"fin{tc}_g")
                    if tc >= 2:
                        wait(vector, f"fin{tc - 2}_out")
                    vector.tensor_tensor(sg_f[:, 0:512], psF[:, 0:512], gsil[:, par * 1024: par * 1024 + 512], ALU.mult)
                    vector.drain()
                    vector.tensor_tensor(sg_f[:, 512:1024], psF[:, 512:1024], gsil[:, par * 1024 + 512:(par + 1) * 1024], ALU.mult)
                    vector.drain()
                    inst = vector.tensor_tensor(otile[:, par * 1024:(par + 1) * 1024],
                                                xres[:, par * 1024:(par + 1) * 1024], sg_f[:], ALU.add)
                    sV.inc(inst)
                    vector.drain()
                tag(f"fin{tc}_sev", sV)

                @block.sync
                def _(sync, tc=tc, par=par):
                    wait(sync, f"fin{tc}_sev")
                    inst = sync.dma_start(out=d_out[tc * 128:(tc + 1) * 128, :],
                                          in_=otile[:, par * 1024:(par + 1) * 1024])
                    dO[par].inc(inst, 16)
                tag(f"fin{tc}_out", dO[par])

            @block.sync
            def _(sync):
                sync.wait_ge(_dO0, C["dO0"])
                sync.wait_ge(_dO1, C["dO1"])
        ESm.close()

    f = nc.m.functions[0]
    blocks = list(f.blocks)
    head, tail = blocks[0], blocks[-1]
    mid = blocks[1:-1]
    order = {"EngineType.SP": 1, "EngineType.Pool": 2, "EngineType.PE": 3,
             "EngineType.DVE": 4, "EngineType.Activation": 5}

    def eng_of(b):
        for i in b.instructions:
            if str(i.engine) != "EngineType.Unassigned":
                return order.get(str(i.engine), 0)
        return 0

    f.blocks = [head] + sorted(mid, key=eng_of) + [tail]
    return nc


def _prep_inputs(kw):
    x = np.ascontiguousarray(np.asarray(kw["L0_attn_x"], np.float32))
    W_in = np.asarray(kw["L0_attn_in_proj_w"], np.float32)
    conv_w = np.asarray(kw["L0_attn_conv_w"], np.float32)
    conv_b = np.asarray(kw["L0_attn_conv_b"], np.float32)
    norm_w = np.asarray(kw["L0_attn_norm_w"], np.float32)
    W_out = np.asarray(kw["L0_attn_out_proj_w"], np.float32)
    dt_bias = np.asarray(kw["L0_attn_dt_bias"], np.float32)
    A_log = np.asarray(kw["L0_attn_A_log"], np.float32)
    D_param = np.asarray(kw["L0_attn_D_param"], np.float32)
    gate_w = np.asarray(kw["L0_ffn_gate_w"], np.float32)
    beta = np.asarray(kw["L0_ffn_beta"], np.float32)
    com = dict(
        w_inT=np.ascontiguousarray(W_in.T),
        w_outT=np.ascontiguousarray(W_out.T),
        gwT=np.ascontiguousarray(gate_w.T),
        convw=np.ascontiguousarray(conv_w[:, 0, :].reshape(18, 128, 4).transpose(1, 0, 2).reshape(128, 72)),
        convb=np.ascontiguousarray(conv_b.reshape(18, 128).T),
        normw=np.ascontiguousarray(norm_w.reshape(16, 128).T),
        dpar=np.ascontiguousarray(np.tile(D_param[None, :], (128, 1))),
        dtb=np.ascontiguousarray(dt_bias[:, None]),
        alog=np.ascontiguousarray(A_log[:, None]),
        beta_t=np.ascontiguousarray(np.tile(beta[None, :], (128, 1))),
        w_gate=np.ascontiguousarray(np.concatenate(
            [np.asarray(kw["L0_ffn_gate_proj_w"], np.float32),
             np.asarray(kw["L0_ffn_shared_gate_w"], np.float32)], axis=0)),
        w_up=np.ascontiguousarray(np.concatenate(
            [np.asarray(kw["L0_ffn_up_proj_w"], np.float32),
             np.asarray(kw["L0_ffn_shared_up_w"], np.float32)], axis=0)),
        w_down=np.ascontiguousarray(np.concatenate(
            [np.asarray(kw["L0_ffn_down_proj_w"], np.float32),
             np.asarray(kw["L0_ffn_shared_down_w"], np.float32)], axis=0)),
    )
    return x, com


def kernel(**kw):
    x, com = _prep_inputs(kw)
    B, L, D = x.shape
    NT = L // 2
    nc = build_nc(NT_MAIN=NT, NT_PREF=NT)
    in_maps = []
    for core in range(8):
        b, h = core // 2, core % 2
        m = dict(com)
        m["x_main"] = np.ascontiguousarray(x[b, h * NT:(h + 1) * NT])
        m["x_pref"] = np.ascontiguousarray(x[b, 0:NT]) if h == 1 else np.zeros((NT, D), np.float32)
        m["pref_flag"] = np.full((128, 1), float(h), np.float32)
        in_maps.append(m)
    res = run_bass_kernel_spmd(nc, in_maps, core_ids=list(range(8)))
    out = np.empty((B, L, D), np.float32)
    for core in range(8):
        b, h = core // 2, core % 2
        out[b, h * NT:(h + 1) * NT] = res.results[core]["out"]
    return out
